# version 10
# baseline (speedup 1.0000x reference)
import numpy as np
from contextlib import ExitStack

import concourse.bass as bass
import concourse.mybir as mybir
from concourse.bass_utils import run_bass_kernel_spmd
from concourse.alu_op_type import AluOpType as OP
from concourse.bass import IndirectOffsetOnAxis as IOA

F32 = mybir.dt.float32
BF16 = mybir.dt.bfloat16
I32 = mybir.dt.int32
U32 = mybir.dt.uint32
AF = mybir.ActivationFunctionType
AX = mybir.AxisListType.X

D = 1024
NCT = 8
NEG = -1.0e9
L3CAP = 1536

ENGS = ("sync", "tensor", "vector", "scalar", "gpsimd")


class R:
    __slots__ = ("writer", "readers", "name", "strict")

    def __init__(self, name="", strict=False):
        self.writer = None
        self.readers = []
        self.name = name
        self.strict = strict


class Prog:
    def __init__(self, nc, block):
        self.nc = nc
        self.block = block
        self.pend = {e: [] for e in ENGS}
        self.cnt = {}
        self.sems = {}
        self.waited = {e: {} for e in ENGS}
        self.know = {}
        self.n_ins = 0

    def _merge_know(self, w_eng, s, c):
        lst = self.know.get(s)
        if not lst:
            return
        lo, hi = 0, len(lst)
        while lo < hi:
            mid = (lo + hi) // 2
            if lst[mid][0] <= c:
                lo = mid + 1
            else:
                hi = mid
        if lo == 0:
            return
        for k, v in lst[lo - 1][1].items():
            if w_eng.get(k, 0) < v:
                w_eng[k] = v

    def ins(self, eng, fn, reads=(), writes=(), dq=False, slot=None, strict=False):
        need = {}
        strict_sems = set()
        for res in reads:
            if res.writer is not None:
                s, c = res.writer
                if need.get(s, 0) < c:
                    need[s] = c
                if strict or res.strict:
                    strict_sems.add(s)
        for res in writes:
            if res.writer is not None:
                s, c = res.writer
                if need.get(s, 0) < c:
                    need[s] = c
                if strict or res.strict:
                    strict_sems.add(s)
            for (s, c) in res.readers:
                if need.get(s, 0) < c:
                    need[s] = c
        if dq:
            sem = "dq_" + eng + ("" if slot is None else str(slot))
            amt = 16
            prev = self.cnt.get(sem, 0)
            if prev and need.get(sem, 0) < prev:
                need[sem] = prev
        else:
            sem = "e_" + eng
            amt = 1
        own = "e_" + eng
        waits = []
        w_eng = self.waited[eng]
        for s, c in sorted(need.items()):
            if s == own and eng == "tensor" and s not in strict_sems:
                continue
            if w_eng.get(s, 0) < c:
                waits.append((s, c))
                w_eng[s] = c
                self._merge_know(w_eng, s, c)
        self.cnt[sem] = self.cnt.get(sem, 0) + amt
        my_dep = (sem, self.cnt[sem])
        snap = dict(w_eng)
        snap[sem] = self.cnt[sem]
        self.know.setdefault(sem, []).append((self.cnt[sem], snap))
        sems = self.sems

        def thunk(e, waits=waits, fn=fn, dep=my_dep, amt=amt):
            for s, c in waits:
                e.wait_ge(sems[s], c)
            instr = fn(e)
            instr.then_inc(sems[dep[0]], amt)

        self.pend[eng].append(thunk)
        self.n_ins += 1
        for res in writes:
            res.writer = my_dep
            res.readers = []
        for res in reads:
            res.readers.append(my_dep)
        return my_dep

    def dma(self, eng, out, in_, reads=(), writes=(), slot=None, **kw):
        return self.ins(eng, lambda e, out=out, in_=in_, kw=kw: e.dma_start(out=out, in_=in_, **kw),
                        reads=reads, writes=writes, dq=True, slot=slot)

    def raw_wait(self, eng, waits):
        w_eng = self.waited[eng]
        todo = []
        for (s, c) in waits:
            if w_eng.get(s, 0) < c:
                todo.append((s, c))
                w_eng[s] = c
                self._merge_know(w_eng, s, c)
        if not todo:
            return
        sems = self.sems

        def thunk(e, todo=todo):
            for s, c in todo:
                e.wait_ge(sems[s], c)

        self.pend[eng].append(thunk)

    def barrier(self):
        targets = sorted((s, c) for s, c in self.cnt.items() if c > 0)
        for eng in ENGS:
            self.raw_wait(eng, targets)

    def flush(self):
        for eng in ENGS:
            if self.pend[eng]:
                thunks = self.pend[eng]
                self.pend[eng] = []
                getattr(self.block, eng)(lambda e, ts=tuple(thunks): [t(e) for t in ts])

    def sb(self, es, name, shape, dtype, strict=False):
        t = es.enter_context(self.nc.sbuf_tensor(name, list(shape), dtype))
        return t, R(name, strict=strict)

    def ps(self, es, name, shape, dtype=F32):
        t = es.enter_context(self.nc.psum_tensor(name, list(shape), dtype))
        return t, R(name)



def emit_ln_tile(P, x_ap, x_res, xn_ap, xn_res, bn12, bn12_r, mvi, mvi_r):
    eps_ap, eps_r = P.cpool_eps5
    P.ins("vector", lambda e: e.bn_stats(out=bn12[:, 0:6], in_=x_ap[:, 0:512]),
          reads=[x_res], writes=[bn12_r])
    P.ins("vector", lambda e: e.bn_stats(out=bn12[:, 6:12], in_=x_ap[:, 512:1024]),
          reads=[x_res], writes=[bn12_r])
    P.ins("vector", lambda e: e.bn_aggr(out=mvi[:, 0:2], in_=bn12[:, 0:12]),
          reads=[bn12_r], writes=[mvi_r])
    P.ins("scalar", lambda e: e.activation(out=mvi[:, 1:2], in_=mvi[:, 1:2], func=AF.Sqrt,
                                           bias=eps_ap, scale=1.0),
          reads=[mvi_r, eps_r], writes=[mvi_r])
    P.ins("vector", lambda e: e.reciprocal(out=mvi[:, 1:2], in_=mvi[:, 1:2]),
          reads=[mvi_r], writes=[mvi_r])
    P.ins("vector", lambda e: e.tensor_scalar(out=xn_ap, in0=x_ap, scalar1=mvi[:, 0:1],
                                              scalar2=mvi[:, 1:2], op0=OP.subtract, op1=OP.mult),
          reads=[x_res, mvi_r], writes=[xn_res])


def emit_transpose_128(P, dst_ap, dst_res, src_ap, src_res, ptr, ptr_res, ident_ap, ident_res):
    P.ins("tensor", lambda e: e.transpose(ptr, in_=src_ap, identity=ident_ap),
          reads=[src_res, ident_res], writes=[ptr_res])
    P.ins("scalar", lambda e: e.copy(out=dst_ap, in_=ptr),
          reads=[ptr_res], writes=[dst_res])


def emit_transpose_batch(P, srcs, src_res, dst_ap, dst_res, ps_ap, ps_res, ident_ap, ident_res,
                         eng_evac="scalar"):
    k = len(srcs)
    for t in range(k):
        src_ap = srcs[t]
        P.ins("tensor", lambda e, t=t, src_ap=src_ap: e.transpose(
            ps_ap[:, t * 128:(t + 1) * 128], in_=src_ap, identity=ident_ap),
            reads=[src_res, ident_res], writes=[ps_res])
    in3 = ps_ap[:, 0:k * 128].rearrange("p (k m) -> p k m", m=128)
    P.ins(eng_evac, lambda e: e.copy(out=dst_ap, in_=in3) if eng_evac == "scalar"
          else e.tensor_copy(out=dst_ap, in_=in3),
          reads=[ps_res], writes=[dst_res])



def emit_layer0(P, es, dram, NT):
    nc = P.nc
    x_in = dram["l0_in"]
    x_out = dram["l0_out"]
    NST = NT // 4

    wqT, wqT_r = P.sb(es, "l0_wqT", [128, NCT, D], BF16)
    wkT, wkT_r = P.sb(es, "l0_wkT", [128, NCT, D], BF16)
    wvT, wvT_r = P.sb(es, "l0_wvT", [128, NCT, D], BF16)
    woT, woT_r = P.sb(es, "l0_woT", [128, NCT, D], BF16)
    wf1, wf1_r = P.sb(es, "l0_wf1", [128, NCT, D], BF16)
    wf2, wf2_r = P.sb(es, "l0_wf2", [128, NCT, D], BF16)
    wf3, wf3_r = P.sb(es, "l0_wf3", [128, NCT, D], BF16)
    bias_b, bias_b_r = P.sb(es, "l0_biasb", [128, D], F32)
    for (buf, res, key) in [(wqT, wqT_r, "w0_qT"), (wkT, wkT_r, "w0_kT"), (wvT, wvT_r, "w0_vT"),
                            (woT, woT_r, "w0_oT"), (wf1, wf1_r, "w0_f1"), (wf2, wf2_r, "w0_f2"),
                            (wf3, wf3_r, "w0_f3")]:
        P.dma("gpsimd", buf[:], dram[key].rearrange("(b p) d -> p b d", p=128), writes=[res])
    P.dma("sync", bias_b[:], dram["w0_bias"].unsqueeze(0).broadcast_to([128, D]), writes=[bias_b_r])

    ident = dram["ident_sb"]
    ident_ap, ident_r = ident
    m_old_ap, m_old_r = dram["mask_old_sb"]
    m_diag_ap, m_diag_r = dram["mask_diag_sb"]

    x_st, x_st_r = P.sb(es, "l0_xst", [128, 4, D], F32)
    xn_st, xn_st_r = P.sb(es, "l0_xnst", [128, 4, D], BF16)
    xnT, xnT_r = P.sb(es, "l0_xnT", [128, NCT, 512], BF16, strict=True)
    qT, qT_r = P.sb(es, "l0_qT", [128, NCT, 512], BF16)
    kT, kT_r = P.sb(es, "l0_kT", [128, NCT, 768], BF16)
    vbuf, vbuf_r = P.sb(es, "l0_v", [128, 6, D], BF16)
    h1T, h1T_r = P.sb(es, "l0_h1T", [128, NCT, 512], BF16)
    h2T, h2T_r = P.sb(es, "l0_h2T", [128, NCT, 512], BF16)
    w_sb, w_sb_r = P.sb(es, "l0_wsb", [128, 384], BF16)
    wT_sb, wT_sb_r = P.sb(es, "l0_wTsb", [128, 384], BF16)
    o_attn, o_attn_r = P.sb(es, "l0_oattn", [128, D], BF16)
    oT, oT_r = P.sb(es, "l0_oT", [128, NCT, 128], BF16)
    tanh_sb, tanh_sb_r = P.sb(es, "l0_tanh", [128, 512], F32)
    bn12, bn12_r = P.sb(es, "l0_bn12", [128, 12], F32, strict=True)
    mvi, mvi_r = P.sb(es, "l0_mvi", [128, 2], F32, strict=True)
    negmax, negmax_r = P.sb(es, "l0_negmax", [128, 1], F32, strict=True)
    sumr, sumr_r = P.sb(es, "l0_sumr", [128, 1], F32, strict=True)

    pm = [P.ps(es, f"l0_pm{i}", [128, 512]) for i in range(2)]
    ps_s, ps_s_r = P.ps(es, "l0_ps_s", [128, 384])
    ps_tr, ps_tr_r = P.ps(es, "l0_ps_tr", [128, 1024], BF16)
    po = [P.ps(es, f"l0_po{i}", [128, 512]) for i in range(2)]
    pf = [P.ps(es, f"l0_pf{i}", [128, 512]) for i in range(2)]

    x_in_t = x_in["ap"].rearrange("(j p) c -> j p c", p=128)
    x_out_t = x_out["ap"].rearrange("(j p) c -> j p c", p=128)

    def proj_to(dst_ap_fn, dst_res, w_ap, w_res, n0, n1):
        for a in range(NCT):
            p_ap, p_res = pm[a % 2]
            for b in range(NCT):
                P.ins("tensor", lambda e, p_ap=p_ap, a=a, b=b: e.matmul(
                    p_ap[:, 0:(n1 - n0)], lhsT=w_ap[:, b, a * 128:(a + 1) * 128],
                    rhs=xnT[:, b, n0:n1], start=(b == 0), stop=(b == NCT - 1)),
                    reads=[w_res, xnT_r], writes=[p_res])
            P.ins("scalar", lambda e, p_ap=p_ap, a=a: e.copy(out=dst_ap_fn(a), in_=p_ap[:, 0:(n1 - n0)]),
                  reads=[p_res], writes=[dst_res])

    for u in range(NST):
        if u > 0:
            for b in range(NCT):
                P.ins("gpsimd", lambda e, b=b: e.tensor_copy(out=kT[:, b, 0:256], in_=kT[:, b, 512:768]),
                      reads=[kT_r], writes=[kT_r])
            for s in range(2):
                P.ins("gpsimd", lambda e, s=s: e.tensor_copy(out=vbuf[:, s, :], in_=vbuf[:, 4 + s, :]),
                      reads=[vbuf_r], writes=[vbuf_r])

        for j in range(4):
            i = u * 4 + j
            P.dma("sync", x_st[:, j, :], x_in_t[i], reads=[x_in["r"]], writes=[x_st_r])
        for j in range(4):
            emit_ln_tile(P, x_st[:, j, :], x_st_r, xn_st[:, j, :], xn_st_r, bn12, bn12_r, mvi, mvi_r)
            emit_transpose_batch(P, [xn_st[:, j, b * 128:(b + 1) * 128] for b in range(NCT)],
                                 xn_st_r, xnT[:, :, j * 128:(j + 1) * 128], xnT_r,
                                 ps_tr, ps_tr_r, ident_ap, ident_r)

        proj_to(lambda a: qT[:, a, :], qT_r, wqT, wqT_r, 0, 512)
        proj_to(lambda a: kT[:, a, 256:768], kT_r, wkT, wkT_r, 0, 512)
        for j in range(4):
            for half in range(2):
                p_ap, p_res = pm[half]
                for b in range(NCT):
                    P.ins("tensor", lambda e, p_ap=p_ap, j=j, b=b, half=half: e.matmul(
                        p_ap[:, 0:512], lhsT=xnT[:, b, j * 128:(j + 1) * 128],
                        rhs=wvT[:, b, half * 512:(half + 1) * 512],
                        start=(b == 0), stop=(b == NCT - 1)),
                        reads=[wvT_r, xnT_r], writes=[p_res])
                P.ins("scalar", lambda e, p_ap=p_ap, j=j, half=half: e.copy(
                    out=vbuf[:, 2 + j, half * 512:(half + 1) * 512], in_=p_ap[:, 0:512]),
                    reads=[p_res], writes=[vbuf_r])

        for j in range(4):
            i = u * 4 + j
            z0 = 2 - min(i, 2)
            ncols = (3 - z0) * 128
            c0 = z0 * 128
            for h in range(16):
                pb = (h % 2) * 64
                a = h // 2
                for z in range(z0, 3):
                    dk = 2 - z
                    pos = 256 + (j - dk) * 128
                    P.ins("tensor", lambda e, z=z, pb=pb, a=a, j=j, pos=pos: e.matmul(
                        ps_s[:, z * 128:(z + 1) * 128],
                        lhsT=qT[pb:pb + 64, a, j * 128:(j + 1) * 128],
                        rhs=kT[pb:pb + 64, a, pos:pos + 128], start=True, stop=True),
                        reads=[qT_r, kT_r], writes=[ps_s_r])
                if z0 == 0:
                    P.ins("vector", lambda e: e.tensor_tensor(out=ps_s[:, 0:128], in0=ps_s[:, 0:128],
                                                              in1=m_old_ap, op=OP.add),
                          reads=[ps_s_r, m_old_r], writes=[ps_s_r])
                P.ins("vector", lambda e: e.tensor_tensor(out=ps_s[:, 256:384], in0=ps_s[:, 256:384],
                                                          in1=m_diag_ap, op=OP.add),
                      reads=[ps_s_r, m_diag_r], writes=[ps_s_r])
                P.ins("vector", lambda e, c0=c0, ncols=ncols: e.reduce_max(
                    out=negmax[:, 0:1], in_=ps_s[:, c0:c0 + ncols], axis=AX, negate=True),
                    reads=[ps_s_r], writes=[negmax_r])
                P.ins("vector", lambda e: e.tensor_scalar(out=negmax[:, 0:1], in0=negmax[:, 0:1],
                                                          scalar1=0.125, scalar2=None, op0=OP.mult),
                      reads=[negmax_r], writes=[negmax_r])
                P.ins("scalar", lambda e, c0=c0, ncols=ncols: e.activation(
                    out=w_sb[:, c0:c0 + ncols], in_=ps_s[:, c0:c0 + ncols], func=AF.Exp,
                    bias=negmax[:, 0:1], scale=0.125, accum_out=sumr[:, 0:1]),
                    reads=[ps_s_r, negmax_r], writes=[w_sb_r, sumr_r])
                P.ins("vector", lambda e: e.reciprocal(out=sumr[:, 0:1], in_=sumr[:, 0:1]),
                      reads=[sumr_r], writes=[sumr_r])
                P.ins("vector", lambda e, c0=c0, ncols=ncols: e.tensor_scalar(
                    out=w_sb[:, c0:c0 + ncols], in0=w_sb[:, c0:c0 + ncols],
                    scalar1=sumr[:, 0:1], scalar2=None, op0=OP.mult),
                    reads=[w_sb_r, sumr_r], writes=[w_sb_r])
                emit_transpose_batch(P, [w_sb[:, z * 128:(z + 1) * 128] for z in range(z0, 3)],
                                     w_sb_r,
                                     wT_sb[:, z0 * 128:384].rearrange("p (k m) -> p k m", m=128),
                                     wT_sb_r, ps_tr, ps_tr_r, ident_ap, ident_r)
                bank, bank_r = po[h // 8]
                col = (h % 8) * 64
                for z in range(z0, 3):
                    dk = 2 - z
                    slot = 2 + j - dk
                    P.ins("tensor", lambda e, bank=bank, col=col, z=z, slot=slot, h=h, z0=z0: e.matmul(
                        bank[:, col:col + 64], lhsT=wT_sb[:, z * 128:(z + 1) * 128],
                        rhs=vbuf[:, slot, h * 64:h * 64 + 64],
                        start=(z == z0 and h % 8 == 0), stop=(z == 2 and h % 8 == 7)),
                        reads=[wT_sb_r, vbuf_r], writes=[bank_r])
            for half in range(2):
                bank, bank_r = po[half]
                P.ins("scalar", lambda e, bank=bank, half=half: e.copy(
                    out=o_attn[:, half * 512:(half + 1) * 512], in_=bank[:, 0:512]),
                    reads=[bank_r], writes=[o_attn_r])
            emit_transpose_batch(P, [o_attn[:, b * 128:(b + 1) * 128] for b in range(NCT)],
                                 o_attn_r, oT[:, :, :], oT_r, ps_tr, ps_tr_r, ident_ap, ident_r)
            for half in range(2):
                p_ap, p_res = pf[half]
                for b in range(NCT):
                    P.ins("tensor", lambda e, p_ap=p_ap, b=b, half=half: e.matmul(
                        p_ap[:, 0:512], lhsT=oT[:, b, :], rhs=woT[:, b, half * 512:(half + 1) * 512],
                        start=(b == 0), stop=(b == NCT - 1)),
                        reads=[oT_r, woT_r], writes=[p_res])
                P.ins("vector", lambda e, p_ap=p_ap, j=j, half=half: e.tensor_tensor(
                    out=x_st[:, j, half * 512:(half + 1) * 512],
                    in0=x_st[:, j, half * 512:(half + 1) * 512], in1=p_ap[:, 0:512], op=OP.add),
                    reads=[p_res, x_st_r], writes=[x_st_r])

        proj_to(lambda a: h1T[:, a, :], h1T_r, wf1, wf1_r, 0, 512)
        for a in range(NCT):
            p_ap, p_res = pm[a % 2]
            for b in range(NCT):
                P.ins("tensor", lambda e, p_ap=p_ap, a=a, b=b: e.matmul(
                    p_ap[:, 0:512], lhsT=wf2[:, b, a * 128:(a + 1) * 128], rhs=h1T[:, b, :],
                    start=(b == 0), stop=(b == NCT - 1)),
                    reads=[wf2_r, h1T_r], writes=[p_res])
            P.ins("scalar", lambda e, p_ap=p_ap, a=a: e.copy(out=h2T[:, a, :], in_=p_ap[:, 0:512]),
                  reads=[p_res], writes=[h2T_r])
        for j in range(4):
            for half in range(2):
                p_ap, p_res = pf[half]
                for b in range(NCT):
                    P.ins("tensor", lambda e, p_ap=p_ap, j=j, b=b, half=half: e.matmul(
                        p_ap[:, 0:512], lhsT=h2T[:, b, j * 128:(j + 1) * 128],
                        rhs=wf3[:, b, half * 512:(half + 1) * 512],
                        start=(b == 0), stop=(b == NCT - 1)),
                        reads=[h2T_r, wf3_r], writes=[p_res])
                P.ins("vector", lambda e, p_ap=p_ap, half=half: e.tensor_tensor(
                    out=p_ap[:, 0:512], in0=p_ap[:, 0:512],
                    in1=bias_b[:, half * 512:(half + 1) * 512], op=OP.add),
                    reads=[p_res, bias_b_r], writes=[p_res])
                P.ins("scalar", lambda e, p_ap=p_ap: e.activation(out=tanh_sb[:, 0:512],
                                                                  in_=p_ap[:, 0:512], func=AF.Tanh),
                      reads=[p_res], writes=[tanh_sb_r])
                P.ins("vector", lambda e, j=j, half=half: e.tensor_tensor(
                    out=x_st[:, j, half * 512:(half + 1) * 512],
                    in0=x_st[:, j, half * 512:(half + 1) * 512], in1=tanh_sb[:, 0:512], op=OP.add),
                    reads=[tanh_sb_r, x_st_r], writes=[x_st_r])

        for j in range(4):
            i = u * 4 + j
            P.dma("scalar", x_out_t[i], x_st[:, j, :], reads=[x_st_r], writes=[x_out["r"]])



def emit_layer3(P, es, dram, NT):
    nc = P.nc
    x_in = dram["l3_in"]
    x_out = dram["l3_out"]
    ident_ap, ident_r = dram["ident_sb"]
    ident32_ap, ident32_r = dram["ident32_sb"]
    ones_ap, ones_r = dram["ones_sb"]
    GT = 8
    NG = NT // GT
    NHT = 32

    kv_acc, kv_acc_r = P.sb(es, "l3_kvacc", [128, 512], F32, strict=True)
    kv_bf, kv_bf_r = P.sb(es, "l3_kvbf", [128, 512], BF16, strict=True)
    mvi_all, mvi_all_r = P.sb(es, "l3_mviall", [128, 2 * NT], F32, strict=True)
    w_all, w_all_r = P.sb(es, "l3_wall", [128, 8 * NT], F32, strict=True)
    wT8_all, wT8_all_r = P.sb(es, "l3_wT8", [8, 128 * NT], BF16)
    bn12, bn12_r = P.sb(es, "l3_bn12", [128, 12], F32, strict=True)
    P.ins("vector", lambda e: e.memset(kv_acc[:], 0.0), writes=[kv_acc_r])

    pool = [P.ps(es, f"l3_pp{i}", [128, 512]) for i in range(8)]

    x_in_t = x_in["ap"].rearrange("(j p) c -> j p c", p=128)
    x_out_t = x_out["ap"].rearrange("(j p) c -> j p c", p=128)

    def _phase1():
      with ExitStack() as es1:
        wk, wk_r = P.sb(es1, "l3_wk", [128, NCT, D], BF16)
        wv, wv_r = P.sb(es1, "l3_wv", [128, NCT, D], BF16)
        P.dma("gpsimd", wk[:], dram["w3_Wk"].rearrange("(b p) d -> p b d", p=128), writes=[wk_r])
        P.dma("gpsimd", wv[:], dram["w3_Wv"].rearrange("(b p) d -> p b d", p=128), writes=[wv_r])
        bk_row, bk_row_r = P.sb(es1, "l3_bk", [1, D], BF16)
        bv_row, bv_row_r = P.sb(es1, "l3_bv", [1, D], BF16)
        P.dma("gpsimd", bk_row[:], dram["w3_bk"].unsqueeze(0), writes=[bk_row_r])
        P.dma("gpsimd", bv_row[:], dram["w3_bv"].unsqueeze(0), writes=[bv_row_r])
        x_sb, x_sb_r = P.sb(es1, "l3_x", [128, D], F32)
        xn_sb, xn_sb_r = P.sb(es1, "l3_xn", [128, D], BF16)
        xnT, xnT_r = P.sb(es1, "l3_xnT", [128, NCT, 128], BF16, strict=True)
        k_sb, k_sb_r = P.sb(es1, "l3_k", [128, D], BF16)
        v_sb, v_sb_r = P.sb(es1, "l3_v", [128, D], BF16)
        t1, t1_r = P.sb(es1, "l3_t1", [128, 512], F32)
        t2, t2_r = P.sb(es1, "l3_t2", [128, 512], F32)
        mvi, mvi_r = P.sb(es1, "l3_mvi", [128, 2], F32, strict=True)

        for i in range(NT):
            P.dma("sync", x_sb[:], x_in_t[i], reads=[x_in["r"]], writes=[x_sb_r])
            emit_ln_tile(P, x_sb[:], x_sb_r, xn_sb[:], xn_sb_r, bn12, bn12_r, mvi, mvi_r)
            P.ins("vector", lambda e, i=i: e.tensor_copy(out=mvi_all[:, 2 * i:2 * i + 2],
                                                         in_=mvi[:, 0:2]),
                  reads=[mvi_r], writes=[mvi_all_r])
            emit_transpose_batch(P, [xn_sb[:, b * 128:(b + 1) * 128] for b in range(NCT)],
                                 xn_sb_r, xnT[:, :, :], xnT_r,
                                 pool[7][0][:, :].bitcast(BF16), pool[7][1], ident_ap, ident_r)
            for (wmat, wmat_r, brow, brow_r, dst, dst_r, use_phi) in [
                    (wk, wk_r, bk_row, bk_row_r, k_sb, k_sb_r, True),
                    (wv, wv_r, bv_row, bv_row_r, v_sb, v_sb_r, False)]:
                for half in range(2):
                    p_ap, p_res = pool[half]
                    for b in range(NCT):
                        P.ins("tensor", lambda e, p_ap=p_ap, b=b, half=half, wmat=wmat: e.matmul(
                            p_ap[:, 0:512], lhsT=xnT[:, b, :],
                            rhs=wmat[:, b, half * 512:(half + 1) * 512],
                            start=(b == 0), stop=False),
                            reads=[xnT_r, wmat_r], writes=[p_res])
                    P.ins("tensor", lambda e, p_ap=p_ap, half=half, brow=brow: e.matmul(
                        p_ap[:, 0:512], lhsT=ones_ap[0:1, 0:128],
                        rhs=brow[0:1, half * 512:(half + 1) * 512], start=False, stop=True),
                        reads=[ones_r, brow_r], writes=[p_res])
                    if use_phi:
                        emit_phi_evac(P, dst[:, half * 512:(half + 1) * 512], dst_r,
                                      p_ap[:, 0:512], p_res, t1, t1_r, t2, t2_r, 512)
                    else:
                        P.ins("scalar", lambda e, p_ap=p_ap, half=half, dst=dst: e.copy(
                            out=dst[:, half * 512:(half + 1) * 512], in_=p_ap[:, 0:512]),
                            reads=[p_res], writes=[dst_r])
            for h in range(16):
                pb = (h % 2) * 64
                p2 = h // 2
                p_ap, p_res = pool[2]
                P.ins("tensor", lambda e, p_ap=p_ap, h=h: e.matmul(
                    p_ap[0:64, 0:64], lhsT=k_sb[:, h * 64:(h + 1) * 64],
                    rhs=v_sb[:, h * 64:(h + 1) * 64], start=True, stop=True),
                    reads=[k_sb_r, v_sb_r], writes=[p_res])
                P.ins("vector", lambda e, p_ap=p_ap, pb=pb, p2=p2: e.tensor_tensor(
                    out=kv_acc[pb:pb + 64, p2 * 64:(p2 + 1) * 64],
                    in0=kv_acc[pb:pb + 64, p2 * 64:(p2 + 1) * 64],
                    in1=p_ap[0:64, 0:64], op=OP.add),
                    reads=[p_res, kv_acc_r], writes=[kv_acc_r])
        P.ins("scalar", lambda e: e.copy(out=kv_bf[:], in_=kv_acc[:]),
              reads=[kv_acc_r], writes=[kv_bf_r])
        P.barrier()

    _phase1()

    def _phase2():
      xo_rt = [R(f"l3xo{i}") for i in range(NT)]
      b2_all, b2_all_r = P.sb(es, "l3_b2", [8, D], BF16)
      P.dma("gpsimd", b2_all[:], dram["w3_b2"][:, :], writes=[b2_all_r])
      b1c, b1c_r = P.sb(es, "l3_b1c", [128, NHT * 8], F32, strict=True)
      for e_ in range(8):
          emit_load_cols(P, b1c[:, e_ * NHT:(e_ + 1) * NHT], b1c_r, dram["w3_b1"][e_], nb=NHT)

      def _phaseA():
        with ExitStack() as es2:
            wq, wq_r = P.sb(es2, "l3_wq", [128, NCT, D], BF16)
            wp, wp_r = P.sb(es2, "l3_wp", [128, NCT, D], BF16)
            P.dma("gpsimd", wq[:], dram["w3_Wq"].rearrange("(b p) d -> p b d", p=128), writes=[wq_r])
            P.dma("gpsimd", wp[:], dram["w3_Wp"].rearrange("(b p) d -> p b d", p=128), writes=[wp_r])
            bp_row, bp_row_r = P.sb(es2, "l3_bp", [1, D], BF16)
            P.dma("gpsimd", bp_row[:], dram["w3_bp"].unsqueeze(0), writes=[bp_row_r])
            bq_cols, bq_cols_r = P.sb(es2, "l3_bqc", [128, NCT], F32, strict=True)
            emit_load_cols(P, bq_cols, bq_cols_r, dram["w3_bq"])
            centT, centT_r = P.sb(es2, "l3_centT", [128, NCT, 8], BF16)
            P.dma("gpsimd", centT[:], dram["w3_centT"].rearrange("(b p) e -> p b e", p=128),
                  writes=[centT_r])
            xsb2 = [P.sb(es2, f"l3_x2{i}", [128, D], F32) for i in range(2)]
            xn_sb, xn_sb_r = P.sb(es2, "l3_xn2", [128, D], BF16)
            xnT_g, xnT_g_r = P.sb(es2, "l3_xnTg", [128, NCT, GT * 128], BF16, strict=True)
            qb_sb, qb_sb_r = P.sb(es2, "l3_qb", [128, 512], F32)
            qT_g, qT_g_r = P.sb(es2, "l3_qTg", [128, NCT, GT * 128], BF16)
            oT_t, oT_t_r = P.sb(es2, "l3_oTt", [128, NCT, 128], BF16)
            t1, t1_r = P.sb(es2, "l3_t1b", [128, 512], F32)
            g_sb, g_sb_r = P.sb(es2, "l3_g", [128, 8], F32)
            rot_sb, rot_sb_r = P.sb(es2, "l3_rot", [128, 8], F32)
            cmp_sb, cmp_sb_r = P.sb(es2, "l3_cmp", [128, 8], F32)
            ngt_sb, ngt_sb_r = P.sb(es2, "l3_ngt", [128, 8], F32, strict=True)
            wnorm, wnorm_r = P.sb(es2, "l3_wn", [128, 1], F32, strict=True)

            for g in range(NG):
                for jj in range(GT):
                    i = g * GT + jj
                    x_sb, x_sb_r = xsb2[jj % 2]
                    P.dma("sync", x_sb[:], x_in_t[i], reads=[x_in["r"]], writes=[x_sb_r],
                          slot=jj % 2)
                    P.ins("vector", lambda e, i=i, x_sb=x_sb: e.tensor_scalar(
                        out=xn_sb[:], in0=x_sb[:], scalar1=mvi_all[:, 2 * i:2 * i + 1],
                        scalar2=mvi_all[:, 2 * i + 1:2 * i + 2], op0=OP.subtract, op1=OP.mult),
                        reads=[x_sb_r, mvi_all_r], writes=[xn_sb_r])
                    emit_transpose_batch(P, [xn_sb[:, b * 128:(b + 1) * 128] for b in range(NCT)],
                                         xn_sb_r, xnT_g[:, :, jj * 128:(jj + 1) * 128], xnT_g_r,
                                         pool[7][0][:, :].bitcast(BF16), pool[7][1],
                                         ident_ap, ident_r)
                    p_ap, p_res = pool[6]
                    for b in range(NCT):
                        P.ins("tensor", lambda e, p_ap=p_ap, b=b, jj=jj: e.matmul(
                            p_ap[:, 0:8], lhsT=xnT_g[:, b, jj * 128:(jj + 1) * 128],
                            rhs=centT[:, b, :], start=(b == 0), stop=(b == NCT - 1)),
                            reads=[xnT_g_r, centT_r], writes=[p_res])
                    P.ins("scalar", lambda e, p_ap=p_ap: e.activation(out=g_sb[:], in_=p_ap[:, 0:8],
                                                                     func=AF.Sigmoid),
                          reads=[p_res], writes=[g_sb_r])
                    P.ins("vector", lambda e: e.memset(ngt_sb[:], 0.0), writes=[ngt_sb_r])
                    for s in range(1, 8):
                        P.ins("vector", lambda e, s=s: e.tensor_copy(out=rot_sb[:, 0:8 - s],
                                                                     in_=g_sb[:, s:8]),
                              reads=[g_sb_r], writes=[rot_sb_r])
                        P.ins("vector", lambda e, s=s: e.tensor_copy(out=rot_sb[:, 8 - s:8],
                                                                     in_=g_sb[:, 0:s]),
                              reads=[g_sb_r], writes=[rot_sb_r])
                        P.ins("vector", lambda e: e.tensor_tensor(out=cmp_sb[:], in0=rot_sb[:],
                                                                  in1=g_sb[:], op=OP.is_gt),
                              reads=[rot_sb_r, g_sb_r], writes=[cmp_sb_r])
                        P.ins("vector", lambda e: e.tensor_tensor(out=ngt_sb[:], in0=ngt_sb[:],
                                                                  in1=cmp_sb[:], op=OP.add),
                              reads=[cmp_sb_r, ngt_sb_r], writes=[ngt_sb_r])
                    P.ins("vector", lambda e: e.tensor_scalar(out=ngt_sb[:], in0=ngt_sb[:],
                                                              scalar1=1.5, scalar2=1.0,
                                                              op0=OP.is_gt, op1=OP.mult),
                          reads=[ngt_sb_r], writes=[ngt_sb_r])
                    P.ins("vector", lambda e: e.tensor_tensor(out=cmp_sb[:], in0=g_sb[:],
                                                              in1=ngt_sb[:], op=OP.mult),
                          reads=[g_sb_r, ngt_sb_r], writes=[cmp_sb_r])
                    P.ins("vector", lambda e: e.tensor_tensor(out=g_sb[:], in0=g_sb[:],
                                                              in1=cmp_sb[:], op=OP.subtract),
                          reads=[cmp_sb_r, g_sb_r], writes=[g_sb_r])
                    P.ins("vector", lambda e: e.reduce_sum(out=wnorm[:, 0:1], in_=g_sb[:], axis=AX),
                          reads=[g_sb_r], writes=[wnorm_r])
                    P.ins("vector", lambda e: e.reciprocal(out=wnorm[:, 0:1], in_=wnorm[:, 0:1]),
                          reads=[wnorm_r], writes=[wnorm_r])
                    P.ins("vector", lambda e, i=i: e.tensor_scalar(
                        out=w_all[:, i * 8:(i + 1) * 8], in0=g_sb[:], scalar1=wnorm[:, 0:1],
                        scalar2=None, op0=OP.mult),
                        reads=[g_sb_r, wnorm_r], writes=[w_all_r])
                    P.ins("tensor", lambda e, i=i: e.transpose(
                        pool[7][0][0:8, 0:128], in_=w_all[:, i * 8:(i + 1) * 8],
                        identity=ident32_ap),
                        reads=[w_all_r, ident32_r], writes=[pool[7][1]])
                    P.ins("scalar", lambda e, i=i: e.copy(out=wT8_all[0:8, i * 128:(i + 1) * 128],
                                                          in_=pool[7][0][0:8, 0:128]),
                          reads=[pool[7][1]], writes=[wT8_all_r])
                for a in range(NCT):
                    for chunk in range(2):
                        p_ap, p_res = pool[(a * 2 + chunk) % 2]
                        for b in range(NCT):
                            P.ins("tensor", lambda e, p_ap=p_ap, a=a, b=b, chunk=chunk: e.matmul(
                                p_ap[:, 0:512], lhsT=wq[:, b, a * 128:(a + 1) * 128],
                                rhs=xnT_g[:, b, chunk * 512:(chunk + 1) * 512],
                                start=(b == 0), stop=(b == NCT - 1)),
                                reads=[wq_r, xnT_g_r], writes=[p_res])
                        P.ins("scalar", lambda e, p_ap=p_ap, a=a: e.activation(
                            out=qb_sb[:, 0:512], in_=p_ap[:, 0:512], func=AF.Identity,
                            bias=bq_cols[:, a:a + 1], scale=1.0),
                            reads=[p_res, bq_cols_r], writes=[qb_sb_r])
                        emit_phi_evac(P, qT_g[:, a, chunk * 512:(chunk + 1) * 512], qT_g_r,
                                      qb_sb[:, 0:512], qb_sb_r, t1, t1_r, qb_sb, qb_sb_r, 512)
                for jj in range(GT):
                    i = g * GT + jj
                    x_sb, x_sb_r = xsb2[jj % 2]
                    P.dma("sync", x_sb[:], x_in_t[i], reads=[x_in["r"]], writes=[x_sb_r],
                          slot=jj % 2)
                    for a in range(NCT):
                        p_ap, p_res = pool[2 + (a % 2)]
                        for par in range(2):
                            pb = par * 64
                            h = a * 2 + par
                            P.ins("tensor", lambda e, p_ap=p_ap, pb=pb, h=h, a=a, jj=jj: e.matmul(
                                p_ap[pb:pb + 64, 0:128],
                                lhsT=kv_bf[pb:pb + 64, (h // 2) * 64:(h // 2 + 1) * 64],
                                rhs=qT_g[pb:pb + 64, a, jj * 128:(jj + 1) * 128],
                                start=True, stop=True),
                                reads=[kv_bf_r, qT_g_r], writes=[p_res])
                        P.ins("scalar", lambda e, p_ap=p_ap, a=a: e.copy(out=oT_t[:, a, :],
                                                                        in_=p_ap[:, 0:128]),
                              reads=[p_res], writes=[oT_t_r])
                    for half in range(2):
                        p_ap, p_res = pool[4 + half]
                        for b in range(NCT):
                            P.ins("tensor", lambda e, p_ap=p_ap, b=b, half=half: e.matmul(
                                p_ap[:, 0:512], lhsT=oT_t[:, b, :],
                                rhs=wp[:, b, half * 512:(half + 1) * 512],
                                start=(b == 0), stop=False),
                                reads=[oT_t_r, wp_r], writes=[p_res])
                        P.ins("tensor", lambda e, p_ap=p_ap, half=half: e.matmul(
                            p_ap[:, 0:512], lhsT=ones_ap[0:1, 0:128],
                            rhs=bp_row[0:1, half * 512:(half + 1) * 512], start=False, stop=False),
                            reads=[ones_r, bp_row_r], writes=[p_res])
                        P.ins("tensor", lambda e, p_ap=p_ap, i=i, half=half: e.matmul(
                            p_ap[:, 0:512], lhsT=wT8_all[0:8, i * 128:(i + 1) * 128],
                            rhs=b2_all[0:8, half * 512:(half + 1) * 512], start=False, stop=True),
                            reads=[wT8_all_r, b2_all_r], writes=[p_res])
                        P.ins("vector", lambda e, p_ap=p_ap, half=half, x_sb=x_sb: e.tensor_tensor(
                            out=x_sb[:, half * 512:(half + 1) * 512],
                            in0=x_sb[:, half * 512:(half + 1) * 512], in1=p_ap[:, 0:512],
                            op=OP.add),
                            reads=[p_res, x_sb_r], writes=[x_sb_r])
                    P.dma("scalar", x_out_t[i], x_sb[:], reads=[x_sb_r],
                          writes=[x_out["r"], xo_rt[i]], slot=jj % 2)
                xnt_d = dram["l3_xnt"]["ap"].rearrange("(b p) t -> p b t", p=128)
                P.dma("gpsimd", xnt_d[:, :, g * GT * 128:(g + 1) * GT * 128], xnT_g[:],
                      reads=[xnT_g_r], writes=[dram["l3_xnt"]["r"]], slot=g % 2)
        P.barrier()

      _phaseA()

      def _phaseB():
        QD = 2
        with ExitStack() as esB:
            W1sb, W1sb_r = P.sb(esB, "l3_W1sb", [128, NCT, 4096], BF16)
            W2sb, W2sb_r = P.sb(esB, "l3_W2sb", [128, NHT, D], BF16)
            hTq2 = [P.sb(esB, f"l3_hTq{i}", [128, NHT, QD * 128], BF16) for i in range(2)]
            xnTq2 = [P.sb(esB, f"l3_xnTq{i}", [128, NCT, QD * 128], BF16, strict=True) for i in range(2)]
            xo2 = [P.sb(esB, f"l3_xo{i}", [128, D], F32) for i in range(2)]
            w1_t = [dram["w3_W1"][e_].rearrange("(b p) m -> p b m", p=128) for e_ in range(8)]
            w2_t = [dram["w3_W2"][e_].rearrange("(hb p) m -> p hb m", p=128) for e_ in range(8)]
            xnt_d = dram["l3_xnt"]["ap"].rearrange("(b p) t -> p b t", p=128)
            for e_ in range(8):
                for b in range(NCT):
                    P.dma("gpsimd", W1sb[:, b, :], w1_t[e_][:, b, :],
                          writes=[W1sb_r], slot=b % 2)
                for c4 in range(8):
                    P.dma("gpsimd", W2sb[:, c4 * 4:(c4 + 1) * 4, :],
                          w2_t[e_][:, c4 * 4:(c4 + 1) * 4, :], writes=[W2sb_r], slot=c4 % 2)
                for quad in range(NT // QD):
                    xnT_q, xnT_q_r = xnTq2[quad % 2]
                    hT_q, hT_q_r = hTq2[quad % 2]
                    P.dma("sync", xnT_q[:], xnt_d[:, :, quad * QD * 128:(quad + 1) * QD * 128],
                          reads=[dram["l3_xnt"]["r"]], writes=[xnT_q_r], slot=quad % 2)
                    for ht in range(NHT):
                        p_ap, p_res = pool[ht % 4]
                        for b in range(NCT):
                            P.ins("tensor", lambda e, p_ap=p_ap, b=b, ht=ht, xnT_q=xnT_q: e.matmul(
                                p_ap[:, 0:QD * 128], lhsT=W1sb[:, b, ht * 128:(ht + 1) * 128],
                                rhs=xnT_q[:, b, 0:QD * 128],
                                start=(b == 0), stop=(b == NCT - 1)),
                                reads=[W1sb_r, xnT_q_r], writes=[p_res])
                        P.ins("scalar", lambda e, p_ap=p_ap, ht=ht, e_=e_, hT_q=hT_q: e.activation(
                            out=hT_q[:, ht, :], in_=p_ap[:, 0:QD * 128], func=AF.Gelu,
                            bias=b1c[:, e_ * NHT + ht:e_ * NHT + ht + 1], scale=1.0),
                            reads=[p_res, b1c_r], writes=[hT_q_r])
                    for tloc in range(QD):
                        i = quad * QD + tloc
                        xo_sb, xo_r = xo2[tloc % 2]
                        P.dma("sync", xo_sb[:], x_out_t[i], reads=[xo_rt[i]], writes=[xo_r],
                              slot=tloc % 2)
                        for half in range(2):
                            p_ap, p_res = pool[4 + tloc * 2 + half]
                            for ht in range(NHT):
                                P.ins("tensor", lambda e, p_ap=p_ap, ht=ht, tloc=tloc, half=half,
                                      hT_q=hT_q: e.matmul(p_ap[:, 0:512],
                                               lhsT=hT_q[:, ht, tloc * 128:(tloc + 1) * 128],
                                               rhs=W2sb[:, ht, half * 512:(half + 1) * 512],
                                               start=(ht == 0), stop=(ht == NHT - 1)),
                                      reads=[hT_q_r, W2sb_r], writes=[p_res])
                            P.ins("vector", lambda e, p_ap=p_ap, i=i, e_=e_, half=half,
                                  xo_sb=xo_sb: e.scalar_tensor_tensor(
                                      out=xo_sb[:, half * 512:(half + 1) * 512],
                                      in0=p_ap[:, 0:512],
                                      scalar=w_all[:, i * 8 + e_:i * 8 + e_ + 1],
                                      in1=xo_sb[:, half * 512:(half + 1) * 512],
                                      op0=OP.mult, op1=OP.add),
                                  reads=[p_res, w_all_r, xo_r], writes=[xo_r])
                        P.dma("scalar", x_out_t[i], xo_sb[:], reads=[xo_r],
                              writes=[x_out["r"], xo_rt[i]], slot=tloc % 2)
        P.barrier()

      _phaseB()

    _phase2()



def emit_load_cols(P, dst, dst_r, dram_ap, nb=NCT, eng="sync"):
    src = dram_ap.rearrange("(b p) -> b p", p=128)
    for b in range(nb):
        P.dma(eng, dst[:, b:b + 1], src[b].unsqueeze(1), writes=[dst_r])


def emit_fp4_quant(P, out_ap, out_res, in_ap, in_res, scr, K, eng_v="vector"):
    nb = K // 16
    z, z_r = scr["z"]
    y, y_r = scr["y"]
    aq, aq_r = scr["aq"]
    bq, bq_r = scr["bq"]
    sg, sg_r = scr["sg"]
    bm, bm_r = scr["bm"]
    in3 = in_ap.rearrange("p (g k) -> p g k", k=16)
    P.ins("vector", lambda e: e.tensor_reduce(out=bm[:, 0:nb], in_=in3, axis=AX, op=OP.max,
                                           apply_absolute_value=True),
          reads=[in_res], writes=[bm_r])
    P.ins("vector", lambda e: e.tensor_scalar(out=bm[:, nb:2 * nb], in0=bm[:, 0:nb],
                                           scalar1=1.0 / 6.0, scalar2=1e-12, op0=OP.mult, op1=OP.max),
          reads=[bm_r], writes=[bm_r])
    P.ins("vector", lambda e: e.reciprocal(out=bm[:, 2 * nb:3 * nb], in_=bm[:, nb:2 * nb]),
          reads=[bm_r], writes=[bm_r])
    sc3 = bm[:, nb:2 * nb].unsqueeze(2).broadcast_to([128, nb, 16])
    rs3 = bm[:, 2 * nb:3 * nb].unsqueeze(2).broadcast_to([128, nb, 16])
    z3 = z[:, 0:K].rearrange("p (g k) -> p g k", k=16)
    P.ins(eng_v, lambda e: e.tensor_tensor(out=z3, in0=in3, in1=rs3, op=OP.mult),
          reads=[in_res, bm_r], writes=[z_r])
    P.ins("scalar", lambda e: e.sign(out=sg[:, 0:K], in_=z[:, 0:K]),
          reads=[z_r], writes=[sg_r])
    P.ins("scalar", lambda e: e.activation(out=y[:, 0:K], in_=z[:, 0:K], func=AF.Abs),
          reads=[z_r], writes=[y_r])
    MAGIC = 12582912.0
    i1, i1_r = scr["i1"]
    i2, i2_r = scr["i2"]
    P.ins(eng_v, lambda e: e.tensor_scalar(out=i1[:, 0:K], in0=y[:, 0:K], scalar1=2.0, scalar2=2.0,
                                           op0=OP.min, op1=OP.mult),
          reads=[y_r], writes=[i1_r])
    P.ins(eng_v, lambda e: e.tensor_scalar(out=i1[:, 0:K], in0=i1[:, 0:K], scalar1=MAGIC,
                                           scalar2=-MAGIC, op0=OP.add, op1=OP.add),
          reads=[i1_r], writes=[i1_r])
    P.ins(eng_v, lambda e: e.tensor_scalar(out=aq[:, 0:K], in0=i1[:, 0:K], scalar1=0.5, scalar2=-2.0,
                                           op0=OP.mult, op1=OP.add),
          reads=[i1_r], writes=[aq_r])
    P.ins(eng_v, lambda e: e.tensor_scalar(out=i2[:, 0:K], in0=y[:, 0:K], scalar1=2.0, scalar2=4.0,
                                           op0=OP.max, op1=OP.min),
          reads=[y_r], writes=[i2_r])
    P.ins(eng_v, lambda e: e.tensor_scalar(out=i2[:, 0:K], in0=i2[:, 0:K], scalar1=MAGIC,
                                           scalar2=-MAGIC, op0=OP.add, op1=OP.add),
          reads=[i2_r], writes=[i2_r])
    P.ins(eng_v, lambda e: e.tensor_tensor(out=aq[:, 0:K], in0=i2[:, 0:K], in1=aq[:, 0:K],
                                           op=OP.add),
          reads=[i2_r, aq_r], writes=[aq_r])
    P.ins(eng_v, lambda e: e.tensor_scalar(out=bq[:, 0:K], in0=y[:, 0:K], scalar1=5.0, scalar2=2.0,
                                           op0=OP.is_gt, op1=OP.mult),
          reads=[y_r], writes=[bq_r])
    P.ins(eng_v, lambda e: e.tensor_tensor(out=aq[:, 0:K], in0=aq[:, 0:K], in1=bq[:, 0:K], op=OP.add),
          reads=[aq_r, bq_r], writes=[aq_r])
    P.ins(eng_v, lambda e: e.tensor_tensor(out=aq[:, 0:K], in0=aq[:, 0:K], in1=sg[:, 0:K], op=OP.mult),
          reads=[aq_r, sg_r], writes=[aq_r])
    aq3 = aq[:, 0:K].rearrange("p (g k) -> p g k", k=16)
    out3 = out_ap.rearrange("p (g k) -> p g k", k=16)
    P.ins(eng_v, lambda e: e.tensor_tensor(out=out3, in0=aq3, in1=sc3, op=OP.mult),
          reads=[aq_r, bm_r], writes=[out_res])


def emit_layer1_attn(P, es, dram, NT):
    nc = P.nc
    x_in = dram["l1_in"]
    delta = dram["delta"]
    NSEG = NT // 4
    ident_ap, ident_r = dram["ident_sb"]
    ident32_ap, ident32_r = dram["ident32_sb"]
    ones_ap, ones_r = dram["ones_sb"]
    eps5_ap, eps5_r = P.cpool_eps5

    wkT, wkT_r = P.sb(es, "l1_wkT", [128, NCT, D], BF16)
    wvT, wvT_r = P.sb(es, "l1_wvT", [128, NCT, D], BF16)
    wrT, wrT_r = P.sb(es, "l1_wrT", [128, NCT, D], BF16)
    woT, woT_r = P.sb(es, "l1_woT", [128, NCT, D], BF16)
    for (buf, res, key) in [(wkT, wkT_r, "w1_kT"), (wvT, wvT_r, "w1_vT"),
                            (wrT, wrT_r, "w1_rT"), (woT, woT_r, "w1_oT")]:
        P.dma("gpsimd", buf[:], dram[key].rearrange("(b p) d -> p b d", p=128), writes=[res])
    tmk, tmk_r = P.sb(es, "l1_tmk", [128, NCT], F32, strict=True)
    tmv, tmv_r = P.sb(es, "l1_tmv", [128, NCT], F32, strict=True)
    tmr, tmr_r = P.sb(es, "l1_tmr", [128, NCT], F32, strict=True)
    dec, dec_r = P.sb(es, "l1_dec", [128, NCT], F32, strict=True)
    tf, tf_r = P.sb(es, "l1_tf", [128, NCT], F32, strict=True)
    lng, lng_r = P.sb(es, "l1_lng", [128, NCT], F32, strict=True)
    lnb, lnb_r = P.sb(es, "l1_lnb", [128, NCT], BF16, strict=True)
    emit_load_cols(P, tmk, tmk_r, dram["w1_tmk"])
    emit_load_cols(P, tmv, tmv_r, dram["w1_tmv"])
    emit_load_cols(P, tmr, tmr_r, dram["w1_tmr"])
    emit_load_cols(P, dec, dec_r, dram["w1_decay"])
    emit_load_cols(P, tf, tf_r, dram["w1_tfirst"])
    emit_load_cols(P, lng, lng_r, dram["w1_lng"])
    emit_load_cols(P, lnb, lnb_r, dram["w1_lnb"], eng="gpsimd")
    P.ins("scalar", lambda e: e.activation(out=dec[:], in_=dec[:], func=AF.Exp),
          reads=[dec_r], writes=[dec_r])

    Gb, Gb_r = P.sb(es, "l1_Gb", [128, D], F32)
    Bb, Bb_r = P.sb(es, "l1_Bb", [128, D], F32)
    row_sb, row_sb_r = P.sb(es, "l1_row", [1, D], BF16)

    x_sb, x_sb_r = P.sb(es, "l1_x", [128, D], F32)
    xn_sb, xn_sb_r = P.sb(es, "l1_xn", [128, D], BF16)
    xnT, xnT_r = P.sb(es, "l1_xnT", [128, NCT, 513], BF16, strict=True)
    xkT, xkT_r = P.sb(es, "l1_xkT", [128, NCT, 512], BF16)
    xvT, xvT_r = P.sb(es, "l1_xvT", [128, NCT, 512], BF16)
    xrT, xrT_r = P.sb(es, "l1_xrT", [128, NCT, 512], BF16)
    kTs, kTs_r = P.sb(es, "l1_kTs", [128, NCT, 512], BF16)
    vTs, vTs_r = P.sb(es, "l1_vTs", [128, NCT, 512], BF16)
    rTs, rTs_r = P.sb(es, "l1_rTs", [128, NCT, 512], BF16)
    kvs, kvs_r = P.sb(es, "l1_kvs", [128, NCT, 512], F32)
    otm, otm_r = P.sb(es, "l1_otm", [128, NCT, 512], F32)
    sseg, sseg_r = P.sb(es, "l1_sseg", [128, NCT, 512], F32)
    oTs, oTs_r = P.sb(es, "l1_oTs", [128, NCT, 512], BF16)
    sq_sb, sq_sb_r = P.sb(es, "l1_sq", [128, 512], BF16)
    states, states_r = P.sb(es, "l1_states", [128, NCT], F32, strict=True)
    srow, srow_r = P.sb(es, "l1_srow", [1, 3 * 512], F32, strict=True)
    mvc, mvc_r = P.sb(es, "l1_mvc", [128, 2], F32, strict=True)
    att_sb, att_sb_r = P.sb(es, "l1_att", [128, 512], F32)
    c1n, c1n_r = P.sb(es, "l1_c1n", [128, 1], F32, strict=True)
    bn12, bn12_r = P.sb(es, "l1_bn12", [128, 12], F32, strict=True)
    mvi, mvi_r = P.sb(es, "l1_mvi", [128, 2], F32, strict=True)

    pm = [P.ps(es, f"l1_pm{i}", [128, 512]) for i in range(2)]
    pstat, pstat_r = P.ps(es, "l1_pstat", [128, 1024])
    ps_tr, ps_tr_r = P.ps(es, "l1_ps_tr", [128, 1024], BF16)
    pf = [P.ps(es, f"l1_pf{i}", [128, 512]) for i in range(2)]
    ps32, ps32_r = P.ps(es, "l1_ps32", [128, 2])

    x_in_t = x_in["ap"].rearrange("(j p) c -> j p c", p=128)
    delta_t = delta["ap"].rearrange("(j p) c -> j p c", p=128)

    for half in range(2):
        for b in range(NCT):
            P.ins("tensor", lambda e, half=half, b=b: e.matmul(
                pstat[0:1, half * 512:half * 512 + 512], lhsT=lnb[:, b:b + 1],
                rhs=woT[:, b, half * 512:(half + 1) * 512], start=(b == 0), stop=(b == NCT - 1)),
                reads=[lnb_r, woT_r], writes=[pstat_r])
    P.ins("scalar", lambda e: e.copy(out=row_sb[0:1, :], in_=pstat[0:1, 0:1024]),
          reads=[pstat_r], writes=[row_sb_r])
    for half in range(2):
        P.ins("tensor", lambda e, half=half: e.matmul(
            pf[half][0][:, 0:512], lhsT=ones_ap[0:1, 0:128],
            rhs=row_sb[0:1, half * 512:(half + 1) * 512], start=True, stop=True),
            reads=[row_sb_r, ones_r], writes=[pf[half][1]])
        P.ins("scalar", lambda e, half=half: e.copy(out=Bb[:, half * 512:(half + 1) * 512],
                                                    in_=pf[half][0][:, 0:512]),
              reads=[pf[half][1]], writes=[Bb_r])
    for b in range(NCT):
        P.ins("vector", lambda e, b=b: e.tensor_scalar(out=woT[:, b, :], in0=woT[:, b, :],
                                                       scalar1=lng[:, b:b + 1], scalar2=None,
                                                       op0=OP.mult),
              reads=[woT_r, lng_r], writes=[woT_r])
    for half in range(2):
        for b in range(NCT):
            P.ins("tensor", lambda e, half=half, b=b: e.matmul(
                pstat[0:1, half * 512:half * 512 + 512], lhsT=ones_ap[:, 0:1],
                rhs=woT[:, b, half * 512:(half + 1) * 512], start=(b == 0), stop=(b == NCT - 1)),
                reads=[ones_r, woT_r], writes=[pstat_r])
    P.ins("scalar", lambda e: e.copy(out=row_sb[0:1, :], in_=pstat[0:1, 0:1024]),
          reads=[pstat_r], writes=[row_sb_r])
    for half in range(2):
        P.ins("tensor", lambda e, half=half: e.matmul(
            pf[half][0][:, 0:512], lhsT=ones_ap[0:1, 0:128],
            rhs=row_sb[0:1, half * 512:(half + 1) * 512], start=True, stop=True),
            reads=[row_sb_r, ones_r], writes=[pf[half][1]])
        P.ins("scalar", lambda e, half=half: e.copy(out=Gb[:, half * 512:(half + 1) * 512],
                                                    in_=pf[half][0][:, 0:512]),
              reads=[pf[half][1]], writes=[Gb_r])

    P.ins("vector", lambda e: e.memset(states[:], 0.0), writes=[states_r])
    for b in range(NCT):
        P.ins("vector", lambda e, b=b: e.memset(xnT[:, b, 0:1], 0.0), writes=[xnT_r])

    for seg in range(NSEG):
        if seg > 0:
            for b in range(NCT):
                P.ins("gpsimd", lambda e, b=b: e.tensor_copy(out=xnT[:, b, 0:1], in_=xnT[:, b, 512:513]),
                      reads=[xnT_r], writes=[xnT_r])
        for j4 in range(4):
            i = seg * 4 + j4
            P.dma("sync", x_sb[:], x_in_t[i], reads=[x_in["r"]], writes=[x_sb_r])
            emit_ln_tile(P, x_sb[:], x_sb_r, xn_sb[:], xn_sb_r, bn12, bn12_r, mvi, mvi_r)
            emit_transpose_batch(P, [xn_sb[:, b * 128:(b + 1) * 128] for b in range(NCT)],
                                 xn_sb_r, xnT[:, :, 1 + j4 * 128:1 + (j4 + 1) * 128], xnT_r,
                                 ps_tr, ps_tr_r, ident_ap, ident_r)
        for b in range(NCT):
            P.ins("gpsimd", lambda e, b=b: e.tensor_tensor(out=sq_sb[:, 0:512], in0=xnT[:, b, 1:513],
                                                           in1=xnT[:, b, 0:512], op=OP.subtract),
                  reads=[xnT_r], writes=[sq_sb_r])
            for mi, (dst, dst_r, tm, tm_r) in enumerate([(xkT, xkT_r, tmk, tmk_r),
                                                         (xvT, xvT_r, tmv, tmv_r),
                                                         (xrT, xrT_r, tmr, tmr_r)]):
                P.ins("vector", lambda e, b=b, dst=dst, tm=tm:
                      e.scalar_tensor_tensor(
                    out=dst[:, b, :], in0=sq_sb[:, 0:512], scalar=tm[:, b:b + 1],
                    in1=xnT[:, b, 0:512], op0=OP.mult, op1=OP.add),
                    reads=[sq_sb_r, tm_r, xnT_r], writes=[dst_r])
        for (src, src_r, wT, wT_r, dst, dst_r) in [
                (xkT, xkT_r, wkT, wkT_r, kTs, kTs_r),
                (xvT, xvT_r, wvT, wvT_r, vTs, vTs_r),
                (xrT, xrT_r, wrT, wrT_r, rTs, rTs_r)]:
            for a in range(NCT):
                p_ap, p_res = pm[a % 2]
                for b in range(NCT):
                    P.ins("tensor", lambda e, p_ap=p_ap, a=a, b=b, wT=wT, src=src: e.matmul(
                        p_ap[:, 0:512], lhsT=wT[:, b, a * 128:(a + 1) * 128], rhs=src[:, b, :],
                        start=(b == 0), stop=(b == NCT - 1)),
                        reads=[wT_r, src_r], writes=[p_res])
                P.ins("scalar", lambda e, p_ap=p_ap, a=a, dst=dst: e.copy(out=dst[:, a, :],
                                                                          in_=p_ap[:, 0:512]),
                      reads=[p_res], writes=[dst_r])
        for a in range(NCT):
            eng_kv = "gpsimd" if a % 2 else "vector"
            P.ins(eng_kv, lambda e, a=a: e.tensor_tensor(out=kvs[:, a, :], in0=kTs[:, a, :],
                                                           in1=vTs[:, a, :], op=OP.mult),
                  reads=[kTs_r, vTs_r], writes=[kvs_r])
            P.ins("vector", lambda e, a=a: e.tensor_scalar(out=otm[:, a, :], in0=kvs[:, a, :],
                                                           scalar1=tf[:, a:a + 1], scalar2=None,
                                                           op0=OP.mult),
                  reads=[kvs_r, tf_r], writes=[otm_r])
            P.ins("vector", lambda e, a=a: e.tensor_tensor_scan(
                out=sseg[:, a, :], data0=dec[:, a:a + 1].broadcast_to([128, 512]),
                data1=kvs[:, a, :], initial=states[:, a:a + 1], op0=OP.mult, op1=OP.add),
                reads=[dec_r, kvs_r, states_r], writes=[sseg_r])
            P.ins("vector", lambda e, a=a: e.tensor_copy(out=states[:, a:a + 1],
                                                         in_=sseg[:, a, 511:512]),
                  reads=[sseg_r], writes=[states_r])
            eng_o = "gpsimd" if a % 2 == 0 else "vector"
            P.ins(eng_o, lambda e, a=a: e.tensor_tensor(out=otm[:, a, :], in0=otm[:, a, :],
                                                           in1=sseg[:, a, :], op=OP.add),
                  reads=[otm_r, sseg_r], writes=[otm_r])
            P.ins(eng_o, lambda e, a=a: e.tensor_tensor(out=oTs[:, a, :], in0=otm[:, a, :],
                                                           in1=rTs[:, a, :], op=OP.mult),
                  reads=[otm_r, rTs_r], writes=[oTs_r])
        for a in range(NCT):
            P.ins("scalar", lambda e, a=a: e.square(out=sq_sb[:, 0:512], in_=oTs[:, a, :]),
                  reads=[oTs_r], writes=[sq_sb_r])
            P.ins("tensor", lambda e, a=a: e.matmul(
                pstat[0:1, 0:512], lhsT=ones_ap[:, 0:1], rhs=oTs[:, a, :],
                start=(a == 0), stop=(a == NCT - 1)),
                reads=[ones_r, oTs_r], writes=[pstat_r])
            P.ins("tensor", lambda e, a=a: e.matmul(
                pstat[0:1, 512:1024], lhsT=ones_ap[:, 0:1], rhs=sq_sb[:, 0:512],
                start=(a == 0), stop=(a == NCT - 1)),
                reads=[ones_r, sq_sb_r], writes=[pstat_r])
        P.ins("vector", lambda e: e.tensor_scalar(out=srow[0:1, 0:512], in0=pstat[0:1, 0:512],
                                                  scalar1=1.0 / D, scalar2=None, op0=OP.mult),
              reads=[pstat_r], writes=[srow_r])
        P.ins("vector", lambda e: e.tensor_scalar(out=srow[0:1, 512:1024], in0=pstat[0:1, 512:1024],
                                                  scalar1=1.0 / D, scalar2=None, op0=OP.mult),
              reads=[pstat_r], writes=[srow_r])
        P.ins("vector", lambda e: e.tensor_tensor(out=srow[0:1, 1024:1536], in0=srow[0:1, 0:512],
                                                  in1=srow[0:1, 0:512], op=OP.mult),
              reads=[srow_r], writes=[srow_r])
        P.ins("vector", lambda e: e.tensor_tensor(out=srow[0:1, 512:1024], in0=srow[0:1, 512:1024],
                                                  in1=srow[0:1, 1024:1536], op=OP.subtract),
              reads=[srow_r], writes=[srow_r])
        P.ins("scalar", lambda e: e.activation(out=srow[0:1, 512:1024], in_=srow[0:1, 512:1024],
                                               func=AF.Sqrt, bias=eps5_ap[0:1, 0:1], scale=1.0),
              reads=[srow_r, eps5_r], writes=[srow_r])
        P.ins("vector", lambda e: e.reciprocal(out=srow[0:1, 512:1024], in_=srow[0:1, 512:1024]),
              reads=[srow_r], writes=[srow_r])
        for j4 in range(4):
            i = seg * 4 + j4
            P.ins("tensor", lambda e, j4=j4: e.transpose(
                ps32[:, 0:1], in_=srow[0:1, j4 * 128:(j4 + 1) * 128], identity=ident32_ap[0:1, 0:1]),
                reads=[srow_r, ident32_r], writes=[ps32_r])
            P.ins("tensor", lambda e, j4=j4: e.transpose(
                ps32[:, 1:2], in_=srow[0:1, 512 + j4 * 128:512 + (j4 + 1) * 128],
                identity=ident32_ap[0:1, 0:1]),
                reads=[srow_r, ident32_r], writes=[ps32_r])
            P.ins("scalar", lambda e: e.copy(out=mvc[:, 0:2], in_=ps32[:, 0:2]),
                  reads=[ps32_r], writes=[mvc_r])
            P.ins("vector", lambda e: e.tensor_tensor(out=c1n[:, 0:1], in0=mvc[:, 0:1],
                                                      in1=mvc[:, 1:2], op=OP.mult),
                  reads=[mvc_r], writes=[c1n_r])
            P.ins("vector", lambda e: e.tensor_scalar(out=c1n[:, 0:1], in0=c1n[:, 0:1],
                                                      scalar1=-1.0, scalar2=None, op0=OP.mult),
                  reads=[c1n_r], writes=[c1n_r])
            for half in range(2):
                p_ap, p_res = pf[half]
                for b in range(NCT):
                    P.ins("tensor", lambda e, p_ap=p_ap, b=b, j4=j4, half=half: e.matmul(
                        p_ap[:, 0:512], lhsT=oTs[:, b, j4 * 128:(j4 + 1) * 128],
                        rhs=woT[:, b, half * 512:(half + 1) * 512],
                        start=(b == 0), stop=(b == NCT - 1)),
                        reads=[oTs_r, woT_r], writes=[p_res])
                P.ins("vector", lambda e, p_ap=p_ap: e.tensor_scalar(
                    out=att_sb[:, 0:512], in0=p_ap[:, 0:512], scalar1=mvc[:, 1:2], scalar2=None,
                    op0=OP.mult),
                    reads=[p_res, mvc_r], writes=[att_sb_r])
                P.ins("vector", lambda e, half=half: e.scalar_tensor_tensor(
                    out=att_sb[:, 0:512], in0=Gb[:, half * 512:(half + 1) * 512],
                    scalar=c1n[:, 0:1], in1=att_sb[:, 0:512], op0=OP.mult, op1=OP.add),
                    reads=[Gb_r, c1n_r, att_sb_r], writes=[att_sb_r])
                P.ins("vector", lambda e, half=half: e.tensor_tensor(
                    out=att_sb[:, 0:512], in0=att_sb[:, 0:512],
                    in1=Bb[:, half * 512:(half + 1) * 512], op=OP.add),
                    reads=[att_sb_r, Bb_r], writes=[att_sb_r])
                P.dma("scalar", delta_t[i][:, half * 512:(half + 1) * 512], att_sb[:, 0:512],
                      reads=[att_sb_r], writes=[delta["r"]])


def emit_layer1_ffn(P, es, dram, NT):
    nc = P.nc
    x_in = dram["l1_in"]
    x_out = dram["l1_out"]
    delta = dram["delta"]
    ident_ap, ident_r = dram["ident_sb"]

    NI = 2816
    NKT = NI // 128
    wgT, wgT_r = P.sb(es, "l1_wgT", [128, NCT, NI], BF16)
    wuT, wuT_r = P.sb(es, "l1_wuT", [128, NCT, NI], BF16)
    wdT, wdT_r = P.sb(es, "l1_wdT", [128, NKT, D], BF16)
    ps_tr, ps_tr_r = P.ps(es, "l1f_ps_tr", [128, 1024], BF16)

    with ExitStack() as es_q:
        stage = [P.sb(es_q, f"l1_stage{i}", [128, NI], F32) for i in range(2)]
        z = P.sb(es_q, "l1_qz", [128, 1024], F32)
        y = P.sb(es_q, "l1_qy", [128, 1024], F32)
        aq = P.sb(es_q, "l1_qa", [128, 1024], F32)
        bq = P.sb(es_q, "l1_qb", [128, 1024], F32)
        sg = P.sb(es_q, "l1_qs", [128, 1024], BF16)
        bm = P.sb(es_q, "l1_qm", [128, 3 * 64], F32, strict=True)
        qi1 = P.sb(es_q, "l1_qi1", [128, 1024], F32)
        qi2 = P.sb(es_q, "l1_qi2", [128, 1024], F32)
        wq_nat = [P.sb(es_q, f"l1_wqn{i}", [128, NI], BF16) for i in range(2)]
        scr = {"z": z, "y": y, "aq": aq, "bq": bq, "sg": sg, "bm": bm, "i1": qi1, "i2": qi2}

        def quant_transpose(dram_key, nrows_t, kdim, dst, dst_r):
            src_t = dram[dram_key].rearrange("(r p) k -> r p k", p=128)
            for r in range(nrows_t):
                st, st_r = stage[r % 2]
                wn, wn_r = wq_nat[r % 2]
                P.dma("sync", st[:, 0:kdim], src_t[r], writes=[st_r])
                for k0 in range(0, kdim, 1024):
                    kw = min(1024, kdim - k0)
                    emit_fp4_quant(P, wn[:, k0:k0 + kw], wn_r, st[:, k0:k0 + kw], st_r, scr, kw,
                                   eng_v=("gpsimd" if r % 3 == 2 else "vector"))
                nkb = kdim // 128
                for kb0 in range(0, nkb, 8):
                    kk = min(8, nkb - kb0)
                    emit_transpose_batch(
                        P, [wn[:, (kb0 + t) * 128:(kb0 + t + 1) * 128] for t in range(kk)],
                        wn_r, dst[:, kb0:kb0 + kk, r * 128:(r + 1) * 128], dst_r,
                        ps_tr, ps_tr_r, ident_ap, ident_r)

        quant_transpose("w1_gate", NKT, D, wgT, wgT_r)
        quant_transpose("w1_up", NKT, D, wuT, wuT_r)
        quant_transpose("w1_down", NCT, NI, wdT, wdT_r)
        P.barrier()

    x_sb, x_sb_r = P.sb(es, "l1f_x", [128, D], F32)
    xn_sb, xn_sb_r = P.sb(es, "l1f_xn", [128, D], BF16)
    xq_sb, xq_sb_r = P.sb(es, "l1f_xq", [128, D], BF16)
    xqT, xqT_r = P.sb(es, "l1f_xqT", [128, NCT, 128], BF16)
    hq, hq_r = P.sb(es, "l1f_hq", [128, NI], BF16)
    hqT, hqT_r = P.sb(es, "l1f_hqT", [128, NKT, 128], BF16)
    h_sb, h_sb_r = P.sb(es, "l1f_h", [128, 512], BF16)
    silu_sb, silu_sb_r = P.sb(es, "l1f_silu", [128, 512], F32)
    d_sb, d_sb_r = P.sb(es, "l1f_d", [128, D], F32)
    bn12, bn12_r = P.sb(es, "l1f_bn12", [128, 12], F32, strict=True)
    mvi, mvi_r = P.sb(es, "l1f_mvi", [128, 2], F32, strict=True)
    zb = P.sb(es, "l1f_qz", [128, 1024], BF16)
    yb = P.sb(es, "l1f_qy", [128, 1024], BF16)
    ab = P.sb(es, "l1f_qa", [128, 1024], BF16)
    bb = P.sb(es, "l1f_qb", [128, 1024], BF16)
    sgb = P.sb(es, "l1f_qs", [128, 1024], BF16)
    bmb = P.sb(es, "l1f_qm", [128, 3 * 64], F32, strict=True)
    qi1b = P.sb(es, "l1f_qi1", [128, 1024], BF16)
    qi2b = P.sb(es, "l1f_qi2", [128, 1024], BF16)
    scrb = {"z": zb, "y": yb, "aq": ab, "bq": bb, "sg": sgb, "bm": bmb, "i1": qi1b, "i2": qi2b}

    pg, pg_r = P.ps(es, "l1f_pg", [128, 512])
    pu, pu_r = P.ps(es, "l1f_pu", [128, 512])
    pd = [P.ps(es, f"l1f_pd{i}", [128, 512]) for i in range(2)]

    x_in_t = x_in["ap"].rearrange("(j p) c -> j p c", p=128)
    x_out_t = x_out["ap"].rearrange("(j p) c -> j p c", p=128)
    delta_t = delta["ap"].rearrange("(j p) c -> j p c", p=128)

    chunks = [(c0, min(512, NI - c0)) for c0 in range(0, NI, 512)]
    for i in range(NT):
        P.dma("sync", x_sb[:], x_in_t[i], reads=[x_in["r"]], writes=[x_sb_r])
        P.dma("sync", d_sb[:], delta_t[i], reads=[delta["r"]], writes=[d_sb_r])
        emit_ln_tile(P, x_sb[:], x_sb_r, xn_sb[:], xn_sb_r, bn12, bn12_r, mvi, mvi_r)
        emit_fp4_quant(P, xq_sb[:], xq_sb_r, xn_sb[:], xn_sb_r, scrb, D,
                       eng_v=("vector" if i % 3 else "gpsimd"))
        emit_transpose_batch(P, [xq_sb[:, b * 128:(b + 1) * 128] for b in range(NCT)],
                             xq_sb_r, xqT[:, :, :], xqT_r, ps_tr, ps_tr_r, ident_ap, ident_r)
        for (c0, cw) in chunks:
            for b in range(NCT):
                P.ins("tensor", lambda e, b=b, c0=c0, cw=cw: e.matmul(
                    pg[:, 0:cw], lhsT=xqT[:, b, :], rhs=wgT[:, b, c0:c0 + cw],
                    start=(b == 0), stop=(b == NCT - 1)),
                    reads=[xqT_r, wgT_r], writes=[pg_r])
            for b in range(NCT):
                P.ins("tensor", lambda e, b=b, c0=c0, cw=cw: e.matmul(
                    pu[:, 0:cw], lhsT=xqT[:, b, :], rhs=wuT[:, b, c0:c0 + cw],
                    start=(b == 0), stop=(b == NCT - 1)),
                    reads=[xqT_r, wuT_r], writes=[pu_r])
            P.ins("scalar", lambda e, cw=cw: e.activation(out=silu_sb[:, 0:cw], in_=pg[:, 0:cw],
                                                          func=AF.Silu),
                  reads=[pg_r], writes=[silu_sb_r])
            P.ins("vector", lambda e, cw=cw: e.tensor_tensor(out=h_sb[:, 0:cw], in0=silu_sb[:, 0:cw],
                                                             in1=pu[:, 0:cw], op=OP.mult),
                  reads=[silu_sb_r, pu_r], writes=[h_sb_r])
            emit_fp4_quant(P, hq[:, c0:c0 + cw], hq_r, h_sb[:, 0:cw], h_sb_r, scrb, cw,
                           eng_v=("gpsimd" if (i + c0 // 512) % 3 == 2 else "vector"))
        for kb0 in range(0, NKT, 8):
            kk = min(8, NKT - kb0)
            emit_transpose_batch(P, [hq[:, (kb0 + t) * 128:(kb0 + t + 1) * 128] for t in range(kk)],
                                 hq_r, hqT[:, kb0:kb0 + kk, :], hqT_r,
                                 ps_tr, ps_tr_r, ident_ap, ident_r)
        for half in range(2):
            p_ap, p_res = pd[half]
            for kb in range(NKT):
                P.ins("tensor", lambda e, p_ap=p_ap, kb=kb, half=half: e.matmul(
                    p_ap[:, 0:512], lhsT=hqT[:, kb, :], rhs=wdT[:, kb, half * 512:(half + 1) * 512],
                    start=(kb == 0), stop=(kb == NKT - 1)),
                    reads=[hqT_r, wdT_r], writes=[p_res])
            P.ins("vector", lambda e, p_ap=p_ap, half=half: e.tensor_tensor(
                out=x_sb[:, half * 512:(half + 1) * 512], in0=x_sb[:, half * 512:(half + 1) * 512],
                in1=p_ap[:, 0:512], op=OP.add),
                reads=[p_res, x_sb_r], writes=[x_sb_r])
            P.ins("vector", lambda e, half=half: e.tensor_tensor(
                out=x_sb[:, half * 512:(half + 1) * 512], in0=x_sb[:, half * 512:(half + 1) * 512],
                in1=d_sb[:, half * 512:(half + 1) * 512], op=OP.add),
                reads=[d_sb_r, x_sb_r], writes=[x_sb_r])
        P.dma("scalar", x_out_t[i], x_sb[:], reads=[x_sb_r], writes=[x_out["r"]])



def emit_phi_evac(P, dst_ap, dst_res, psum_ap, psum_res, t1, t1_r, t2, t2_r, n):
    P.ins("scalar", lambda e: e.activation(out=t1[:, 0:n], in_=psum_ap, func=AF.Relu),
          reads=[psum_res], writes=[t1_r])
    P.ins("vector", lambda e: e.tensor_tensor(out=t2[:, 0:n], in0=psum_ap, in1=t1[:, 0:n],
                                              op=OP.subtract),
          reads=[psum_res, t1_r], writes=[t2_r])
    P.ins("scalar", lambda e: e.activation(out=t2[:, 0:n], in_=t2[:, 0:n], func=AF.Exp),
          reads=[t2_r], writes=[t2_r])
    P.ins("vector", lambda e: e.tensor_tensor(out=dst_ap, in0=t2[:, 0:n], in1=t1[:, 0:n], op=OP.add),
          reads=[t2_r, t1_r], writes=[dst_res])


def emit_layer2_attn(P, es, dram, NT):
    nc = P.nc
    x_in = dram["l2_in"]
    delta = dram["delta"]
    NST = NT // 4
    ident_ap, ident_r = dram["ident_sb"]
    binut_ap, binut_r = dram["binut_sb"]

    wqkvT, wqkvT_r = P.sb(es, "l2_wqkvT", [128, NCT, 3 * D], BF16)
    woT, woT_r = P.sb(es, "l2_woT", [128, NCT, D], BF16)
    P.dma("gpsimd", wqkvT[:], dram["w2_qkvT"].rearrange("(b p) d -> p b d", p=128), writes=[wqkvT_r])
    P.dma("gpsimd", woT[:], dram["w2_outT"].rearrange("(b p) d -> p b d", p=128), writes=[woT_r])

    kv_acc, kv_acc_r = P.sb(es, "l2_kvacc", [128, 8 * 65], F32, strict=True)
    kv_bf, kv_bf_r = P.sb(es, "l2_kvbf", [128, 8 * 65], BF16, strict=True)
    P.ins("vector", lambda e: e.memset(kv_acc[:], 0.0), writes=[kv_acc_r])
    P.ins("vector", lambda e: e.memset(kv_bf[:], 0.0), writes=[kv_bf_r])

    x_st, x_st_r = P.sb(es, "l2_xst", [128, 4, D], F32)
    xn_sb, xn_sb_r = P.sb(es, "l2_xn", [128, D], BF16)
    xnT, xnT_r = P.sb(es, "l2_xnT", [128, NCT, 512], BF16, strict=True)
    qT, qT_r = P.sb(es, "l2_qT", [128, NCT, 512], BF16)
    kT, kT_r = P.sb(es, "l2_kT", [128, NCT, 512], BF16)
    k_st, k_st_r = P.sb(es, "l2_kst", [128, 4, D], BF16)
    v_st, v_st_r = P.sb(es, "l2_vst", [128, 4, 16 * 65], BF16)
    sT_sb, sT_sb_r = P.sb(es, "l2_sT", [128, 128], BF16)
    o_attn, o_attn_r = P.sb(es, "l2_oattn", [128, D], BF16)
    oT, oT_r = P.sb(es, "l2_oT", [128, NCT, 128], BF16)
    t1, t1_r = P.sb(es, "l2_t1", [128, 512], F32)
    t2, t2_r = P.sb(es, "l2_t2", [128, 512], F32)
    rden, rden_r = P.sb(es, "l2_rden", [128, 1], F32, strict=True)
    att_sb, att_sb_r = P.sb(es, "l2_att", [128, 512], F32)
    bn12, bn12_r = P.sb(es, "l2_bn12", [128, 12], F32, strict=True)
    mvi, mvi_r = P.sb(es, "l2_mvi", [128, 2], F32, strict=True)

    pm = [P.ps(es, f"l2_pm{i}", [128, 512]) for i in range(2)]
    ps_sT, ps_sT_r = P.ps(es, "l2_ps_sT", [128, 128])
    patt, patt_r = P.ps(es, "l2_patt", [128, 1536])
    ps_kv, ps_kv_r = P.ps(es, "l2_pskv", [128, 65])
    ps_tr, ps_tr_r = P.ps(es, "l2_ps_tr", [128, 1024], BF16)

    x_in_t = x_in["ap"].rearrange("(j p) c -> j p c", p=128)
    delta_t = delta["ap"].rearrange("(j p) c -> j p c", p=128)

    def slot_col(h):
        return (h % 7) * 65 + (h // 7) * 512

    v3 = v_st[:, :, :].rearrange("p j (h x) -> p j h x", x=65)

    for u in range(NST):
        for j4 in range(4):
            i = u * 4 + j4
            P.dma("sync", x_st[:, j4, :], x_in_t[i], reads=[x_in["r"]], writes=[x_st_r])
            emit_ln_tile(P, x_st[:, j4, :], x_st_r, xn_sb[:], xn_sb_r, bn12, bn12_r, mvi, mvi_r)
            emit_transpose_batch(P, [xn_sb[:, b * 128:(b + 1) * 128] for b in range(NCT)],
                                 xn_sb_r, xnT[:, :, j4 * 128:(j4 + 1) * 128], xnT_r,
                                 ps_tr, ps_tr_r, ident_ap, ident_r)
        for (dst, dst_r, off) in [(qT, qT_r, 0), (kT, kT_r, D)]:
            for a in range(NCT):
                p_ap, p_res = pm[a % 2]
                for b in range(NCT):
                    P.ins("tensor", lambda e, p_ap=p_ap, a=a, b=b, off=off: e.matmul(
                        p_ap[:, 0:512], lhsT=wqkvT[:, b, off + a * 128:off + (a + 1) * 128],
                        rhs=xnT[:, b, :], start=(b == 0), stop=(b == NCT - 1)),
                        reads=[wqkvT_r, xnT_r], writes=[p_res])
                emit_phi_evac(P, dst[:, a, :], dst_r, p_ap[:, 0:512], p_res, t1, t1_r, t2, t2_r, 512)
        for j4 in range(4):
            emit_transpose_batch(P, [kT[:, b, j4 * 128:(j4 + 1) * 128] for b in range(NCT)],
                                 kT_r, k_st[:, j4, :].rearrange("p (k m) -> p k m", m=128), k_st_r,
                                 ps_tr, ps_tr_r, ident_ap, ident_r)
        for j4 in range(4):
            for half in range(2):
                p_ap, p_res = pm[half]
                for b in range(NCT):
                    P.ins("tensor", lambda e, p_ap=p_ap, j4=j4, b=b, half=half: e.matmul(
                        p_ap[:, 0:512], lhsT=xnT[:, b, j4 * 128:(j4 + 1) * 128],
                        rhs=wqkvT[:, b, 2 * D + half * 512:2 * D + (half + 1) * 512],
                        start=(b == 0), stop=(b == NCT - 1)),
                        reads=[wqkvT_r, xnT_r], writes=[p_res])
                P.ins("scalar", lambda e, p_ap=p_ap, j4=j4, half=half: e.copy(
                    out=v3[:, j4, half * 8:(half + 1) * 8, 0:64],
                    in_=p_ap[:, 0:512].rearrange("p (h x) -> p h x", x=64)),
                    reads=[p_res], writes=[v_st_r])
            P.ins("vector", lambda e, j4=j4: e.memset(v3[:, j4, :, 64:65], 1.0), writes=[v_st_r])
        for j4 in range(4):
            for h in range(16):
                pb = (h % 2) * 64
                a = h // 2
                p2 = h // 2
                sc = slot_col(h)
                first_in_bank = h in (0, 7, 14)
                last_in_bank = h in (6, 13, 15)
                P.ins("tensor", lambda e, pb=pb, a=a, j4=j4: e.matmul(
                    ps_sT[:, 0:128], lhsT=kT[pb:pb + 64, a, j4 * 128:(j4 + 1) * 128],
                    rhs=qT[pb:pb + 64, a, j4 * 128:(j4 + 1) * 128], start=True, stop=True),
                    reads=[kT_r, qT_r], writes=[ps_sT_r])
                P.ins("vector", lambda e: e.tensor_tensor(out=sT_sb[:], in0=ps_sT[:, 0:128],
                                                          in1=binut_ap, op=OP.mult),
                      reads=[ps_sT_r, binut_r], writes=[sT_sb_r])
                P.ins("tensor", lambda e, sc=sc, j4=j4, h=h, first=first_in_bank: e.matmul(
                    patt[:, sc:sc + 65], lhsT=sT_sb[:], rhs=v3[:, j4, h, 0:65],
                    start=first, stop=False),
                    reads=[sT_sb_r, v_st_r], writes=[patt_r])
                P.ins("tensor", lambda e, sc=sc, pb=pb, a=a, p2=p2, j4=j4, last=last_in_bank: e.matmul(
                    patt[:, sc:sc + 65], lhsT=qT[pb:pb + 64, a, j4 * 128:(j4 + 1) * 128],
                    rhs=kv_bf[pb:pb + 64, p2 * 65:(p2 + 1) * 65], start=False, stop=last),
                    reads=[qT_r, kv_bf_r], writes=[patt_r])
            for h in range(16):
                sc = slot_col(h)
                P.ins("vector", lambda e, sc=sc: e.tensor_scalar(
                    out=rden[:, 0:1], in0=patt[:, sc + 64:sc + 65], scalar1=1e-6, scalar2=None,
                    op0=OP.add),
                    reads=[patt_r], writes=[rden_r])
                P.ins("vector", lambda e: e.reciprocal(out=rden[:, 0:1], in_=rden[:, 0:1]),
                      reads=[rden_r], writes=[rden_r])
                P.ins("vector", lambda e, sc=sc, h=h: e.tensor_scalar(
                    out=o_attn[:, h * 64:(h + 1) * 64], in0=patt[:, sc:sc + 64],
                    scalar1=rden[:, 0:1], scalar2=None, op0=OP.mult),
                    reads=[patt_r, rden_r], writes=[o_attn_r])
            for h in range(16):
                pb = (h % 2) * 64
                p2 = h // 2
                P.ins("tensor", lambda e, h=h, j4=j4: e.matmul(
                    ps_kv[0:64, 0:65], lhsT=k_st[:, j4, h * 64:(h + 1) * 64],
                    rhs=v3[:, j4, h, 0:65], start=True, stop=True),
                    reads=[k_st_r, v_st_r], writes=[ps_kv_r])
                P.ins("vector", lambda e, pb=pb, p2=p2: e.tensor_tensor(
                    out=kv_acc[pb:pb + 64, p2 * 65:(p2 + 1) * 65],
                    in0=kv_acc[pb:pb + 64, p2 * 65:(p2 + 1) * 65], in1=ps_kv[0:64, 0:65],
                    op=OP.add),
                    reads=[ps_kv_r, kv_acc_r], writes=[kv_acc_r])
            P.ins("scalar", lambda e: e.copy(out=kv_bf[:], in_=kv_acc[:]),
                  reads=[kv_acc_r], writes=[kv_bf_r])
            emit_transpose_batch(P, [o_attn[:, b * 128:(b + 1) * 128] for b in range(NCT)],
                                 o_attn_r, oT[:, :, :], oT_r, ps_tr, ps_tr_r, ident_ap, ident_r)
            i = u * 4 + j4
            for half in range(2):
                p_ap, p_res = pm[half]
                for b in range(NCT):
                    P.ins("tensor", lambda e, p_ap=p_ap, b=b, half=half: e.matmul(
                        p_ap[:, 0:512], lhsT=oT[:, b, :], rhs=woT[:, b, half * 512:(half + 1) * 512],
                        start=(b == 0), stop=(b == NCT - 1)),
                        reads=[oT_r, woT_r], writes=[p_res])
                P.ins("scalar", lambda e, p_ap=p_ap: e.copy(out=att_sb[:, 0:512], in_=p_ap[:, 0:512]),
                      reads=[p_res], writes=[att_sb_r])
                P.dma("scalar", delta_t[i][:, half * 512:(half + 1) * 512], att_sb[:, 0:512],
                      reads=[att_sb_r], writes=[delta["r"]])


def emit_absmean_quant(P, es_q, dram, dram_key, nrt, kdim, dst, dst_r, stage, stage_r, cpool_one):
    nc = P.nc
    one_ap, one_r = cpool_one
    asum, asum_r = P.sb(es_q, "q_asum_" + dram_key, [128, nrt], F32)
    g_ps, g_ps_r = P.ps(es_q, "q_gps_" + dram_key, [128, 512])
    grow, grow_r = P.sb(es_q, "q_grow_" + dram_key, [1, max(nrt, 128)], F32)
    thr, thr_r = P.sb(es_q, "q_thr_" + dram_key, [128, 1], F32, strict=True)
    sgn, sgn_r = P.sb(es_q, "q_sgn_" + dram_key, [128, kdim], BF16)
    ones_row32, ones_row32_r = P.sb(es_q, "q_or32_" + dram_key, [1, 128], F32)
    P.ins("vector", lambda e: e.memset(ones_row32[:], 1.0), writes=[ones_row32_r])
    src_t = dram[dram_key].rearrange("(r p) k -> r p k", p=128)
    for r in range(nrt):
        P.dma("sync", stage[:, 0:kdim], src_t[r], writes=[stage_r])
        P.ins("scalar", lambda e, r=r: e.activation(out=stage[:, 0:kdim], in_=stage[:, 0:kdim],
                                                    func=AF.Abs, accum_out=asum[:, r:r + 1]),
              reads=[stage_r], writes=[stage_r, asum_r])
    P.ins("tensor", lambda e: e.matmul(g_ps[0:1, 0:nrt], lhsT=one_ap, rhs=asum[:, 0:nrt],
                                       start=True, stop=True),
          reads=[one_r, asum_r], writes=[g_ps_r])
    P.ins("vector", lambda e: e.reduce_sum(out=grow[0:1, 0:1], in_=g_ps[0:1, 0:nrt], axis=AX),
          reads=[g_ps_r], writes=[grow_r])
    N = nrt * 128 * kdim
    P.ins("vector", lambda e: e.tensor_scalar(out=grow[0:1, 0:1], in0=grow[0:1, 0:1],
                                              scalar1=0.5 / N, scalar2=0.5e-6, op0=OP.mult,
                                              op1=OP.add),
          reads=[grow_r], writes=[grow_r])
    P.ins("tensor", lambda e: e.matmul(g_ps[:, 0:1], lhsT=ones_row32[0:1, 0:128],
                                       rhs=grow[0:1, 0:1], start=True, stop=True),
          reads=[ones_row32_r, grow_r], writes=[g_ps_r])
    P.ins("scalar", lambda e: e.copy(out=thr[:, 0:1], in_=g_ps[:, 0:1]),
          reads=[g_ps_r], writes=[thr_r])
    for r in range(nrt):
        P.dma("sync", stage[:, 0:kdim], src_t[r], writes=[stage_r])
        P.ins("scalar", lambda e: e.sign(out=sgn[:, 0:kdim], in_=stage[:, 0:kdim]),
              reads=[stage_r], writes=[sgn_r])
        P.ins("scalar", lambda e: e.activation(out=stage[:, 0:kdim], in_=stage[:, 0:kdim], func=AF.Abs),
              reads=[stage_r], writes=[stage_r])
        P.ins("vector", lambda e: e.tensor_scalar(out=stage[:, 0:kdim], in0=stage[:, 0:kdim],
                                                  scalar1=thr[:, 0:1], scalar2=1.0,
                                                  op0=OP.is_gt, op1=OP.mult),
              reads=[stage_r, thr_r], writes=[stage_r])
        P.ins("vector", lambda e, r=r: e.tensor_tensor(out=dst[:, r, :], in0=stage[:, 0:kdim],
                                                       in1=sgn[:, 0:kdim], op=OP.mult),
              reads=[stage_r, sgn_r], writes=[dst_r])


def emit_layer2_ffn(P, es, dram, NT):
    nc = P.nc
    x_in = dram["l2_in"]
    x_out = dram["l2_out"]
    delta = dram["delta"]
    ident_ap, ident_r = dram["ident_sb"]
    NH = 4096
    NHT = NH // 128

    wfc, wfc_r = P.sb(es, "l2_wfc", [128, NCT, NH], BF16)
    wpj, wpj_r = P.sb(es, "l2_wpj", [128, NHT, D], BF16)
    lnw_b, lnw_b_r = P.sb(es, "l2_lnwb", [128, NH], BF16)
    lnb_b, lnb_b_r = P.sb(es, "l2_lnbb", [128, NH], BF16, strict=True)
    P.dma("gpsimd", lnw_b[:], dram["w2_lnw"].unsqueeze(0).broadcast_to([128, NH]), writes=[lnw_b_r])
    P.dma("gpsimd", lnb_b[:], dram["w2_lnb"].unsqueeze(0).broadcast_to([128, NH]), writes=[lnb_b_r])

    with ExitStack() as es_q:
        stage, stage_r = P.sb(es_q, "l2_stage", [128, NH], F32)
        emit_absmean_quant(P, es_q, dram, "w2_fcT", NCT, NH, wfc, wfc_r, stage, stage_r, P.cpool_one)
        emit_absmean_quant(P, es_q, dram, "w2_projT", NHT, D, wpj, wpj_r, stage, stage_r, P.cpool_one)
        P.barrier()

    x_sb, x_sb_r = P.sb(es, "l2f_x", [128, D], F32)
    d_sb, d_sb_r = P.sb(es, "l2f_d", [128, D], F32)
    xn_sb, xn_sb_r = P.sb(es, "l2f_xn", [128, D], BF16)
    xnT, xnT_r = P.sb(es, "l2f_xnT", [128, NCT, 128], BF16, strict=True)
    h_sb, h_sb_r = P.sb(es, "l2f_h", [128, NH], F32)
    hn_sb, hn_sb_r = P.sb(es, "l2f_hn", [128, NH], BF16)
    hnT, hnT_r = P.sb(es, "l2f_hnT", [128, NHT, 128], BF16)
    bn12, bn12_r = P.sb(es, "l2f_bn12", [128, 12], F32, strict=True)
    bnh, bnh_r = P.sb(es, "l2f_bnh", [128, 6 * 8], F32, strict=True)
    mvi, mvi_r = P.sb(es, "l2f_mvi", [128, 2], F32, strict=True)
    mvh, mvh_r = P.sb(es, "l2f_mvh", [128, 2], F32, strict=True)
    eps5_ap, eps5_r = P.cpool_eps5

    pm = [P.ps(es, f"l2f_pm{i}", [128, 512]) for i in range(2)]
    pf = [P.ps(es, f"l2f_pf{i}", [128, 512]) for i in range(2)]
    ps_tr, ps_tr_r = P.ps(es, "l2f_ps_tr", [128, 1024], BF16)

    x_in_t = x_in["ap"].rearrange("(j p) c -> j p c", p=128)
    x_out_t = x_out["ap"].rearrange("(j p) c -> j p c", p=128)
    delta_t = delta["ap"].rearrange("(j p) c -> j p c", p=128)

    for i in range(NT):
        P.dma("sync", x_sb[:], x_in_t[i], reads=[x_in["r"]], writes=[x_sb_r])
        P.dma("sync", d_sb[:], delta_t[i], reads=[delta["r"]], writes=[d_sb_r])
        emit_ln_tile(P, x_sb[:], x_sb_r, xn_sb[:], xn_sb_r, bn12, bn12_r, mvi, mvi_r)
        emit_transpose_batch(P, [xn_sb[:, b * 128:(b + 1) * 128] for b in range(NCT)],
                             xn_sb_r, xnT[:, :, :], xnT_r, ps_tr, ps_tr_r, ident_ap, ident_r)
        for hc in range(NCT):
            p_ap, p_res = pm[hc % 2]
            for b in range(NCT):
                P.ins("tensor", lambda e, p_ap=p_ap, hc=hc, b=b: e.matmul(
                    p_ap[:, 0:512], lhsT=xnT[:, b, :], rhs=wfc[:, b, hc * 512:(hc + 1) * 512],
                    start=(b == 0), stop=(b == NCT - 1)),
                    reads=[xnT_r, wfc_r], writes=[p_res])
            P.ins("scalar", lambda e, p_ap=p_ap, hc=hc: e.activation(
                out=h_sb[:, hc * 512:(hc + 1) * 512], in_=p_ap[:, 0:512], func=AF.Gelu),
                reads=[p_res], writes=[h_sb_r])
        for hc in range(NCT):
            P.ins("vector", lambda e, hc=hc: e.bn_stats(out=bnh[:, hc * 6:(hc + 1) * 6],
                                                        in_=h_sb[:, hc * 512:(hc + 1) * 512]),
                  reads=[h_sb_r], writes=[bnh_r])
        P.ins("vector", lambda e: e.bn_aggr(out=mvh[:, 0:2], in_=bnh[:, 0:48]),
              reads=[bnh_r], writes=[mvh_r])
        P.ins("scalar", lambda e: e.activation(out=mvh[:, 1:2], in_=mvh[:, 1:2], func=AF.Sqrt,
                                               bias=eps5_ap, scale=1.0),
              reads=[mvh_r, eps5_r], writes=[mvh_r])
        P.ins("vector", lambda e: e.reciprocal(out=mvh[:, 1:2], in_=mvh[:, 1:2]),
              reads=[mvh_r], writes=[mvh_r])
        P.ins("vector", lambda e: e.tensor_scalar(out=h_sb[:], in0=h_sb[:], scalar1=mvh[:, 0:1],
                                                  scalar2=mvh[:, 1:2], op0=OP.subtract, op1=OP.mult),
              reads=[h_sb_r, mvh_r], writes=[h_sb_r])
        P.ins("vector", lambda e: e.tensor_tensor(out=h_sb[:], in0=h_sb[:], in1=lnw_b[:], op=OP.mult),
              reads=[h_sb_r, lnw_b_r], writes=[h_sb_r])
        P.ins("vector", lambda e: e.tensor_tensor(out=hn_sb[:], in0=h_sb[:], in1=lnb_b[:], op=OP.add),
              reads=[h_sb_r, lnb_b_r], writes=[hn_sb_r])
        for hb0 in range(0, NHT, 8):
            emit_transpose_batch(P, [hn_sb[:, (hb0 + t) * 128:(hb0 + t + 1) * 128] for t in range(8)],
                                 hn_sb_r, hnT[:, hb0:hb0 + 8, :], hnT_r,
                                 ps_tr, ps_tr_r, ident_ap, ident_r)
        for half in range(2):
            p_ap, p_res = pf[half]
            for hb in range(NHT):
                P.ins("tensor", lambda e, p_ap=p_ap, hb=hb, half=half: e.matmul(
                    p_ap[:, 0:512], lhsT=hnT[:, hb, :], rhs=wpj[:, hb, half * 512:(half + 1) * 512],
                    start=(hb == 0), stop=(hb == NHT - 1)),
                    reads=[hnT_r, wpj_r], writes=[p_res])
            P.ins("vector", lambda e, p_ap=p_ap, half=half: e.tensor_tensor(
                out=x_sb[:, half * 512:(half + 1) * 512], in0=x_sb[:, half * 512:(half + 1) * 512],
                in1=p_ap[:, 0:512], op=OP.add),
                reads=[p_res, x_sb_r], writes=[x_sb_r])
            P.ins("vector", lambda e, half=half: e.tensor_tensor(
                out=x_sb[:, half * 512:(half + 1) * 512], in0=x_sb[:, half * 512:(half + 1) * 512],
                in1=d_sb[:, half * 512:(half + 1) * 512], op=OP.add),
                reads=[d_sb_r, x_sb_r], writes=[x_sb_r])
        P.dma("scalar", x_out_t[i], x_sb[:], reads=[x_sb_r], writes=[x_out["r"]])


L2_PARTS = ("attn", "ffn")


def emit_layer2(P, es, dram, NT):
    if "attn" in L2_PARTS:
        with ExitStack() as es_a:
            emit_layer2_attn(P, es_a, dram, NT)
            P.barrier()
    if "ffn" in L2_PARTS:
        with ExitStack() as es_b:
            emit_layer2_ffn(P, es_b, dram, NT)
            P.barrier()


def emit_layer1(P, es, dram, NT):
    with ExitStack() as es_a:
        emit_layer1_attn(P, es_a, dram, NT)
        P.barrier()
    with ExitStack() as es_b:
        emit_layer1_ffn(P, es_b, dram, NT)
        P.barrier()



def build_program(T=4096, layers=(0, 1, 2, 3)):
    nc = bass.Bass("TRN2", target_bir_lowering=False, detect_race_conditions=False)
    NT = T // 128

    dram = {}
    dram["x_in"] = {"ap": nc.dram_tensor("x_in", [T, D], F32, kind="ExternalInput").ap(), "r": R("x_in")}
    dram["x_out"] = {"ap": nc.dram_tensor("x_out", [T, D], F32, kind="ExternalOutput").ap(), "r": R("x_out")}
    dram["x_mid_a"] = {"ap": nc.dram_tensor("x_mid_a", [T, D], F32, kind="Internal").ap(), "r": R("x_mid_a")}
    dram["x_mid_b"] = {"ap": nc.dram_tensor("x_mid_b", [T, D], F32, kind="Internal").ap(), "r": R("x_mid_b")}
    dram["delta"] = {"ap": nc.dram_tensor("delta", [T, D], F32, kind="Internal").ap(), "r": R("delta")}
    if 3 in layers:
        dram["l3_xnt"] = {"ap": nc.dram_tensor("l3_xnt", [D, T], BF16, kind="Internal").ap(),
                          "r": R("l3_xnt")}


    wspecs = []
    if 0 in layers:
        wspecs += [("w0_qT", [D, D]), ("w0_kT", [D, D]), ("w0_vT", [D, D]), ("w0_oT", [D, D]),
                   ("w0_f1", [D, D]), ("w0_f2", [D, D]), ("w0_f3", [D, D]), ("w0_bias", [D])]
    if 1 in layers:
        wspecs += [("w1_kT", [D, D]), ("w1_vT", [D, D]), ("w1_rT", [D, D]), ("w1_oT", [D, D]),
                   ("w1_tmk", [D]), ("w1_tmv", [D]), ("w1_tmr", [D]),
                   ("w1_decay", [D]), ("w1_tfirst", [D]), ("w1_lng", [D]), ("w1_lnb", [D]),
                   ("w1_gate", [2816, D]), ("w1_up", [2816, D]), ("w1_down", [D, 2816])]
    if 2 in layers:
        wspecs += [("w2_qkvT", [D, 3 * D]), ("w2_outT", [D, D]), ("w2_fcT", [D, 4096]),
                   ("w2_projT", [4096, D]), ("w2_lnw", [4096]), ("w2_lnb", [4096])]
    if 3 in layers:
        wspecs += [("w3_Wq", [D, D]), ("w3_Wk", [D, D]), ("w3_Wv", [D, D]), ("w3_Wp", [D, D]),
                   ("w3_bq", [D]), ("w3_bk", [D]), ("w3_bv", [D]), ("w3_bp", [D]),
                   ("w3_centT", [D, 8]), ("w3_W1", [8, D, 4096]), ("w3_b1", [8, 4096]),
                   ("w3_W2", [8, 4096, D]), ("w3_b2", [8, D])]
    for key, shape in wspecs:
        dram[key] = nc.dram_tensor(key, shape, F32, kind="ExternalInput").ap()
    dram["c_ident"] = nc.dram_tensor("c_ident", [128, 128], F32, kind="ExternalInput").ap()
    dram["c_mask_old"] = nc.dram_tensor("c_mask_old", [128, 128], F32, kind="ExternalInput").ap()
    dram["c_mask_diag"] = nc.dram_tensor("c_mask_diag", [128, 128], F32, kind="ExternalInput").ap()
    dram["c_bin_ut"] = nc.dram_tensor("c_bin_ut", [128, 128], F32, kind="ExternalInput").ap()

    es_global = ExitStack()
    with es_global:
        sems = {}
        sem_names = ["e_" + e for e in ENGS] + ["dq_" + e for e in ("sync", "scalar", "gpsimd")]
        sem_names += ["dq_" + e + str(k) for e in ("sync", "scalar", "gpsimd") for k in (0, 1)]
        for s in sem_names:
            sems[s] = es_global.enter_context(nc.semaphore(s))
        with nc.Block() as block:
            P = Prog(nc, block)
            P.sems = sems

            cpool, cpool_r = P.sb(es_global, "g_cpool", [128, 4], F32, strict=True)
            for col, val in [(0, 1e-5), (1, 1e-6), (2, 1.0), (3, 0.0)]:
                P.ins("vector", lambda e, col=col, val=val: e.memset(cpool[:, col:col + 1], val),
                      writes=[cpool_r])
            P.cpool_eps5 = (cpool[:, 0:1], cpool_r)
            P.cpool_eps6 = (cpool[:, 1:2], cpool_r)
            P.cpool_one = (cpool[:, 2:3], cpool_r)
            P.cpool_zero = (cpool[:, 3:4], cpool_r)

            ident_sb, ident_r = P.sb(es_global, "g_ident", [128, 128], BF16)
            ident32_sb, ident32_r = P.sb(es_global, "g_ident32", [128, 128], F32)
            ones_sb, ones_r = P.sb(es_global, "g_ones", [128, 512], BF16)
            mold_sb, mold_r = P.sb(es_global, "g_mold", [128, 128], F32)
            mdiag_sb, mdiag_r = P.sb(es_global, "g_mdiag", [128, 128], F32)
            binut_sb, binut_r = P.sb(es_global, "g_binut", [128, 128], BF16)
            P.dma("gpsimd", ident_sb[:], dram["c_ident"][:, :], writes=[ident_r])
            P.dma("gpsimd", binut_sb[:], dram["c_bin_ut"][:, :], writes=[binut_r])
            P.dma("sync", ident32_sb[:], dram["c_ident"][:, :], writes=[ident32_r])
            P.ins("vector", lambda e: e.memset(ones_sb[:], 1.0), writes=[ones_r])
            P.dma("sync", mold_sb[:], dram["c_mask_old"][:, :], writes=[mold_r])
            P.dma("sync", mdiag_sb[:], dram["c_mask_diag"][:, :], writes=[mdiag_r])
            dram["ident_sb"] = (ident_sb[:], ident_r)
            dram["ident32_sb"] = (ident32_sb[:], ident32_r)
            dram["ones_sb"] = (ones_sb[:], ones_r)
            dram["mask_old_sb"] = (mold_sb[:], mold_r)
            dram["mask_diag_sb"] = (mdiag_sb[:], mdiag_r)
            dram["binut_sb"] = (binut_sb[:], binut_r)

            bufs = ["x_in"] + (["x_mid_a", "x_mid_b", "x_mid_a"][:len(layers) - 1]) + ["x_out"]
            for li, layer in enumerate(layers):
                dram["l%d_in" % layer] = dram[bufs[li]]
                dram["l%d_out" % layer] = dram[bufs[li + 1]]
                with ExitStack() as es_layer:
                    if layer == 0:
                        emit_layer0(P, es_layer, dram, NT)
                    elif layer == 1:
                        emit_layer1(P, es_layer, dram, NT)
                    elif layer == 2:
                        emit_layer2(P, es_layer, dram, NT)
                    elif layer == 3:
                        emit_layer3(P, es_layer, dram, NT)
                    else:
                        raise NotImplementedError(layer)
                    P.barrier()
                    P.flush()
            P.barrier()
            P.flush()
    return nc


def make_weight_arrays(inputs):
    c = np.ascontiguousarray
    w = {}
    w["w0_qT"] = c(inputs["L0_attn_Wq"].T)
    w["w0_kT"] = c(inputs["L0_attn_Wk"].T)
    w["w0_vT"] = c(inputs["L0_attn_Wv"].T)
    w["w0_oT"] = c(inputs["L0_attn_Wo"].T)
    w["w0_f1"] = c(inputs["L0_ffn_weight"])
    w["w0_f2"] = c(inputs["L0_ffn_aux_weight1"])
    w["w0_f3"] = c(inputs["L0_ffn_aux_weight2"])
    w["w0_bias"] = c(inputs["L0_ffn_bias"])
    w["w1_kT"] = c(inputs["L1_attn_Wk"].T)
    w["w1_vT"] = c(inputs["L1_attn_Wv"].T)
    w["w1_rT"] = c(inputs["L1_attn_Wr"].T)
    w["w1_oT"] = c(inputs["L1_attn_Wo"].T)
    w["w1_tmk"] = c(inputs["L1_attn_time_mix_k"].reshape(-1))
    w["w1_tmv"] = c(inputs["L1_attn_time_mix_v"].reshape(-1))
    w["w1_tmr"] = c(inputs["L1_attn_time_mix_r"].reshape(-1))
    w["w1_decay"] = c(inputs["L1_attn_time_decay"].reshape(-1))
    w["w1_tfirst"] = c(inputs["L1_attn_time_first"].reshape(-1))
    w["w1_lng"] = c(inputs["L1_attn_ln_g"])
    w["w1_lnb"] = c(inputs["L1_attn_ln_b"])
    w["w1_gate"] = c(inputs["L1_ffn_w_gate"])
    w["w1_up"] = c(inputs["L1_ffn_w_up"])
    w["w1_down"] = c(inputs["L1_ffn_w_down"])
    w["w2_qkvT"] = c(inputs["L2_attn_w_qkv"].T)
    w["w2_outT"] = c(inputs["L2_attn_w_out"].T)
    w["w2_fcT"] = c(inputs["L2_ffn_W_fc"].T)
    w["w2_projT"] = c(inputs["L2_ffn_W_proj"].T)
    w["w2_lnw"] = c(inputs["L2_ffn_ln_w"])
    w["w2_lnb"] = c(inputs["L2_ffn_ln_b"])
    w["w3_Wq"] = c(inputs["L3_attn_Wq"])
    w["w3_Wk"] = c(inputs["L3_attn_Wk"])
    w["w3_Wv"] = c(inputs["L3_attn_Wv"])
    w["w3_Wp"] = c(inputs["L3_attn_Wp"])
    w["w3_bq"] = c(inputs["L3_attn_bq"])
    w["w3_bk"] = c(inputs["L3_attn_bk"])
    w["w3_bv"] = c(inputs["L3_attn_bv"])
    w["w3_bp"] = c(inputs["L3_attn_bp"])
    w["w3_centT"] = c(inputs["L3_ffn_centroids"].T)
    w["w3_W1"] = c(inputs["L3_ffn_W1"])
    w["w3_b1"] = c(inputs["L3_ffn_b1"])
    w["w3_W2"] = c(inputs["L3_ffn_W2"])
    w["w3_b2"] = c(inputs["L3_ffn_b2"])
    w["c_ident"] = np.eye(128, dtype=np.float32)
    r = np.arange(128)
    w["c_mask_old"] = np.where(r[None, :] >= r[:, None], 0.0, NEG).astype(np.float32)
    w["c_mask_diag"] = np.where(r[None, :] <= r[:, None], 0.0, NEG).astype(np.float32)
    w["c_bin_ut"] = (r[None, :] >= r[:, None]).astype(np.float32)
    return w


def kernel(**inputs):
    x = np.ascontiguousarray(inputs["L0_attn_x"], dtype=np.float32)
    w = make_weight_arrays(inputs)
    nc = build_program(T=4096, layers=(0, 1, 2, 3))
    in_maps = []
    for i in range(8):
        m = dict(w)
        m["x_in"] = np.ascontiguousarray(x[i])
        in_maps.append(m)
    res = run_bass_kernel_spmd(nc, in_maps, core_ids=list(range(8)))
    out = np.stack([np.asarray(r["x_out"]) for r in res.results], axis=0)
    return out.astype(np.float32)


# version 11
# speedup vs baseline: 1.3887x; 1.3887x over previous
import numpy as np
from contextlib import ExitStack

import concourse.bass as bass
import concourse.mybir as mybir
from concourse.bass_utils import run_bass_kernel_spmd
from concourse.alu_op_type import AluOpType as OP
from concourse.bass import IndirectOffsetOnAxis as IOA

F32 = mybir.dt.float32
BF16 = mybir.dt.bfloat16
I32 = mybir.dt.int32
U32 = mybir.dt.uint32
AF = mybir.ActivationFunctionType
AX = mybir.AxisListType.X

D = 1024
NCT = 8
NEG = -1.0e9
L3CAP = 1536

ENGS = ("sync", "tensor", "vector", "scalar", "gpsimd")


class R:
    __slots__ = ("writer", "readers", "name", "strict")

    def __init__(self, name="", strict=False):
        self.writer = None
        self.readers = []
        self.name = name
        self.strict = strict


class Prog:
    def __init__(self, nc, block):
        self.nc = nc
        self.block = block
        self.pend = {e: [] for e in ENGS}
        self.cnt = {}
        self.sems = {}
        self.waited = {e: {} for e in ENGS}
        self.know = {}
        self.n_ins = 0

    def _merge_know(self, w_eng, s, c):
        lst = self.know.get(s)
        if not lst:
            return
        lo, hi = 0, len(lst)
        while lo < hi:
            mid = (lo + hi) // 2
            if lst[mid][0] <= c:
                lo = mid + 1
            else:
                hi = mid
        if lo == 0:
            return
        for k, v in lst[lo - 1][1].items():
            if w_eng.get(k, 0) < v:
                w_eng[k] = v

    def ins(self, eng, fn, reads=(), writes=(), dq=False, slot=None, strict=False):
        need = {}
        strict_sems = set()
        for res in reads:
            if res.writer is not None:
                s, c = res.writer
                if need.get(s, 0) < c:
                    need[s] = c
                if strict or res.strict:
                    strict_sems.add(s)
        for res in writes:
            if res.writer is not None:
                s, c = res.writer
                if need.get(s, 0) < c:
                    need[s] = c
                if strict or res.strict:
                    strict_sems.add(s)
            for (s, c) in res.readers:
                if need.get(s, 0) < c:
                    need[s] = c
        if dq:
            sem = "dq_" + eng + ("" if slot is None else str(slot))
            amt = 16
            prev = self.cnt.get(sem, 0)
            if prev and need.get(sem, 0) < prev:
                need[sem] = prev
        else:
            sem = "e_" + eng
            amt = 1
        own = "e_" + eng
        waits = []
        w_eng = self.waited[eng]
        for s, c in sorted(need.items()):
            if s == own and eng == "tensor" and s not in strict_sems:
                continue
            if w_eng.get(s, 0) < c:
                waits.append((s, c))
                w_eng[s] = c
                self._merge_know(w_eng, s, c)
        self.cnt[sem] = self.cnt.get(sem, 0) + amt
        my_dep = (sem, self.cnt[sem])
        snap = dict(w_eng)
        snap[sem] = self.cnt[sem]
        self.know.setdefault(sem, []).append((self.cnt[sem], snap))
        sems = self.sems

        def thunk(e, waits=waits, fn=fn, dep=my_dep, amt=amt):
            for s, c in waits:
                e.wait_ge(sems[s], c)
            instr = fn(e)
            instr.then_inc(sems[dep[0]], amt)

        self.pend[eng].append(thunk)
        self.n_ins += 1
        for res in writes:
            res.writer = my_dep
            res.readers = []
        for res in reads:
            res.readers.append(my_dep)
        return my_dep

    def dma(self, eng, out, in_, reads=(), writes=(), slot=None, **kw):
        return self.ins(eng, lambda e, out=out, in_=in_, kw=kw: e.dma_start(out=out, in_=in_, **kw),
                        reads=reads, writes=writes, dq=True, slot=slot)

    def raw_wait(self, eng, waits):
        w_eng = self.waited[eng]
        todo = []
        for (s, c) in waits:
            if w_eng.get(s, 0) < c:
                todo.append((s, c))
                w_eng[s] = c
                self._merge_know(w_eng, s, c)
        if not todo:
            return
        sems = self.sems

        def thunk(e, todo=todo):
            for s, c in todo:
                e.wait_ge(sems[s], c)

        self.pend[eng].append(thunk)

    def barrier(self):
        targets = sorted((s, c) for s, c in self.cnt.items() if c > 0)
        for eng in ENGS:
            self.raw_wait(eng, targets)

    def flush(self):
        for eng in ENGS:
            if self.pend[eng]:
                thunks = self.pend[eng]
                self.pend[eng] = []
                getattr(self.block, eng)(lambda e, ts=tuple(thunks): [t(e) for t in ts])

    def sb(self, es, name, shape, dtype, strict=False):
        t = es.enter_context(self.nc.sbuf_tensor(name, list(shape), dtype))
        return t, R(name, strict=strict)

    def ps(self, es, name, shape, dtype=F32):
        t = es.enter_context(self.nc.psum_tensor(name, list(shape), dtype))
        return t, R(name)



def emit_ln_tile(P, x_ap, x_res, xn_ap, xn_res, bn12, bn12_r, mvi, mvi_r):
    eps_ap, eps_r = P.cpool_eps5
    P.ins("vector", lambda e: e.bn_stats(out=bn12[:, 0:6], in_=x_ap[:, 0:512]),
          reads=[x_res], writes=[bn12_r])
    P.ins("vector", lambda e: e.bn_stats(out=bn12[:, 6:12], in_=x_ap[:, 512:1024]),
          reads=[x_res], writes=[bn12_r])
    P.ins("vector", lambda e: e.bn_aggr(out=mvi[:, 0:2], in_=bn12[:, 0:12]),
          reads=[bn12_r], writes=[mvi_r])
    P.ins("scalar", lambda e: e.activation(out=mvi[:, 1:2], in_=mvi[:, 1:2], func=AF.Sqrt,
                                           bias=eps_ap, scale=1.0),
          reads=[mvi_r, eps_r], writes=[mvi_r])
    P.ins("vector", lambda e: e.reciprocal(out=mvi[:, 1:2], in_=mvi[:, 1:2]),
          reads=[mvi_r], writes=[mvi_r])
    P.ins("vector", lambda e: e.tensor_scalar(out=xn_ap, in0=x_ap, scalar1=mvi[:, 0:1],
                                              scalar2=mvi[:, 1:2], op0=OP.subtract, op1=OP.mult),
          reads=[x_res, mvi_r], writes=[xn_res])


def emit_transpose_128(P, dst_ap, dst_res, src_ap, src_res, ptr, ptr_res, ident_ap, ident_res):
    P.ins("tensor", lambda e: e.transpose(ptr, in_=src_ap, identity=ident_ap),
          reads=[src_res, ident_res], writes=[ptr_res])
    P.ins("scalar", lambda e: e.copy(out=dst_ap, in_=ptr),
          reads=[ptr_res], writes=[dst_res])


def emit_transpose_batch(P, srcs, src_res, dst_ap, dst_res, ps_ap, ps_res, ident_ap, ident_res,
                         eng_evac="scalar"):
    k = len(srcs)
    for t in range(k):
        src_ap = srcs[t]
        P.ins("tensor", lambda e, t=t, src_ap=src_ap: e.transpose(
            ps_ap[:, t * 128:(t + 1) * 128], in_=src_ap, identity=ident_ap),
            reads=[src_res, ident_res], writes=[ps_res])
    in3 = ps_ap[:, 0:k * 128].rearrange("p (k m) -> p k m", m=128)
    P.ins(eng_evac, lambda e: e.copy(out=dst_ap, in_=in3) if eng_evac == "scalar"
          else e.tensor_copy(out=dst_ap, in_=in3),
          reads=[ps_res], writes=[dst_res])



def emit_layer0(P, es, dram, NT):
    nc = P.nc
    x_in = dram["l0_in"]
    x_out = dram["l0_out"]
    NST = NT // 4

    wqT, wqT_r = P.sb(es, "l0_wqT", [128, NCT, D], BF16)
    wkT, wkT_r = P.sb(es, "l0_wkT", [128, NCT, D], BF16)
    wvT, wvT_r = P.sb(es, "l0_wvT", [128, NCT, D], BF16)
    woT, woT_r = P.sb(es, "l0_woT", [128, NCT, D], BF16)
    wf1, wf1_r = P.sb(es, "l0_wf1", [128, NCT, D], BF16)
    wf2, wf2_r = P.sb(es, "l0_wf2", [128, NCT, D], BF16)
    wf3, wf3_r = P.sb(es, "l0_wf3", [128, NCT, D], BF16)
    bias_b, bias_b_r = P.sb(es, "l0_biasb", [128, D], F32)
    for (buf, res, key) in [(wqT, wqT_r, "w0_qT"), (wkT, wkT_r, "w0_kT"), (wvT, wvT_r, "w0_vT"),
                            (woT, woT_r, "w0_oT"), (wf1, wf1_r, "w0_f1"), (wf2, wf2_r, "w0_f2"),
                            (wf3, wf3_r, "w0_f3")]:
        P.dma("gpsimd", buf[:], dram[key].rearrange("(b p) d -> p b d", p=128), writes=[res])
    P.dma("sync", bias_b[:], dram["w0_bias"].unsqueeze(0).broadcast_to([128, D]), writes=[bias_b_r])

    ident = dram["ident_sb"]
    ident_ap, ident_r = ident
    m_old_ap, m_old_r = dram["mask_old_sb"]
    m_diag_ap, m_diag_r = dram["mask_diag_sb"]

    x_st, x_st_r = P.sb(es, "l0_xst", [128, 4, D], F32)
    xn_st, xn_st_r = P.sb(es, "l0_xnst", [128, 4, D], BF16)
    xnT, xnT_r = P.sb(es, "l0_xnT", [128, NCT, 512], BF16, strict=True)
    qT, qT_r = P.sb(es, "l0_qT", [128, NCT, 512], BF16)
    kT, kT_r = P.sb(es, "l0_kT", [128, NCT, 768], BF16)
    vbuf, vbuf_r = P.sb(es, "l0_v", [128, 6, D], BF16)
    h1T, h1T_r = P.sb(es, "l0_h1T", [128, NCT, 512], BF16)
    h2T, h2T_r = P.sb(es, "l0_h2T", [128, NCT, 512], BF16)
    w_sb, w_sb_r = P.sb(es, "l0_wsb", [128, 384], BF16)
    wT_sb, wT_sb_r = P.sb(es, "l0_wTsb", [128, 384], BF16)
    o_attn, o_attn_r = P.sb(es, "l0_oattn", [128, D], BF16)
    oT, oT_r = P.sb(es, "l0_oT", [128, NCT, 128], BF16)
    tanh_sb, tanh_sb_r = P.sb(es, "l0_tanh", [128, 512], F32)
    bn12, bn12_r = P.sb(es, "l0_bn12", [128, 12], F32, strict=True)
    mvi, mvi_r = P.sb(es, "l0_mvi", [128, 2], F32, strict=True)
    negmax, negmax_r = P.sb(es, "l0_negmax", [128, 1], F32, strict=True)
    sumr, sumr_r = P.sb(es, "l0_sumr", [128, 1], F32, strict=True)

    pm = [P.ps(es, f"l0_pm{i}", [128, 512]) for i in range(2)]
    ps_s, ps_s_r = P.ps(es, "l0_ps_s", [128, 384])
    ps_tr, ps_tr_r = P.ps(es, "l0_ps_tr", [128, 1024], BF16)
    po = [P.ps(es, f"l0_po{i}", [128, 512]) for i in range(2)]
    pf = [P.ps(es, f"l0_pf{i}", [128, 512]) for i in range(2)]

    x_in_t = x_in["ap"].rearrange("(j p) c -> j p c", p=128)
    x_out_t = x_out["ap"].rearrange("(j p) c -> j p c", p=128)

    def proj_to(dst_ap_fn, dst_res, w_ap, w_res, n0, n1):
        for a in range(NCT):
            p_ap, p_res = pm[a % 2]
            for b in range(NCT):
                P.ins("tensor", lambda e, p_ap=p_ap, a=a, b=b: e.matmul(
                    p_ap[:, 0:(n1 - n0)], lhsT=w_ap[:, b, a * 128:(a + 1) * 128],
                    rhs=xnT[:, b, n0:n1], start=(b == 0), stop=(b == NCT - 1)),
                    reads=[w_res, xnT_r], writes=[p_res])
            P.ins("scalar", lambda e, p_ap=p_ap, a=a: e.copy(out=dst_ap_fn(a), in_=p_ap[:, 0:(n1 - n0)]),
                  reads=[p_res], writes=[dst_res])

    for u in range(NST):
        if u > 0:
            for b in range(NCT):
                P.ins("gpsimd", lambda e, b=b: e.tensor_copy(out=kT[:, b, 0:256], in_=kT[:, b, 512:768]),
                      reads=[kT_r], writes=[kT_r])
            for s in range(2):
                P.ins("gpsimd", lambda e, s=s: e.tensor_copy(out=vbuf[:, s, :], in_=vbuf[:, 4 + s, :]),
                      reads=[vbuf_r], writes=[vbuf_r])

        for j in range(4):
            i = u * 4 + j
            P.dma("sync", x_st[:, j, :], x_in_t[i], reads=[x_in["r"]], writes=[x_st_r])
        for j in range(4):
            emit_ln_tile(P, x_st[:, j, :], x_st_r, xn_st[:, j, :], xn_st_r, bn12, bn12_r, mvi, mvi_r)
            emit_transpose_batch(P, [xn_st[:, j, b * 128:(b + 1) * 128] for b in range(NCT)],
                                 xn_st_r, xnT[:, :, j * 128:(j + 1) * 128], xnT_r,
                                 ps_tr, ps_tr_r, ident_ap, ident_r)

        proj_to(lambda a: qT[:, a, :], qT_r, wqT, wqT_r, 0, 512)
        proj_to(lambda a: kT[:, a, 256:768], kT_r, wkT, wkT_r, 0, 512)
        for j in range(4):
            for half in range(2):
                p_ap, p_res = pm[half]
                for b in range(NCT):
                    P.ins("tensor", lambda e, p_ap=p_ap, j=j, b=b, half=half: e.matmul(
                        p_ap[:, 0:512], lhsT=xnT[:, b, j * 128:(j + 1) * 128],
                        rhs=wvT[:, b, half * 512:(half + 1) * 512],
                        start=(b == 0), stop=(b == NCT - 1)),
                        reads=[wvT_r, xnT_r], writes=[p_res])
                P.ins("scalar", lambda e, p_ap=p_ap, j=j, half=half: e.copy(
                    out=vbuf[:, 2 + j, half * 512:(half + 1) * 512], in_=p_ap[:, 0:512]),
                    reads=[p_res], writes=[vbuf_r])

        for j in range(4):
            i = u * 4 + j
            z0 = 2 - min(i, 2)
            ncols = (3 - z0) * 128
            c0 = z0 * 128
            for h in range(16):
                pb = (h % 2) * 64
                a = h // 2
                for z in range(z0, 3):
                    dk = 2 - z
                    pos = 256 + (j - dk) * 128
                    P.ins("tensor", lambda e, z=z, pb=pb, a=a, j=j, pos=pos: e.matmul(
                        ps_s[:, z * 128:(z + 1) * 128],
                        lhsT=qT[pb:pb + 64, a, j * 128:(j + 1) * 128],
                        rhs=kT[pb:pb + 64, a, pos:pos + 128], start=True, stop=True),
                        reads=[qT_r, kT_r], writes=[ps_s_r])
                if z0 == 0:
                    P.ins("vector", lambda e: e.tensor_tensor(out=ps_s[:, 0:128], in0=ps_s[:, 0:128],
                                                              in1=m_old_ap, op=OP.add),
                          reads=[ps_s_r, m_old_r], writes=[ps_s_r])
                P.ins("vector", lambda e: e.tensor_tensor(out=ps_s[:, 256:384], in0=ps_s[:, 256:384],
                                                          in1=m_diag_ap, op=OP.add),
                      reads=[ps_s_r, m_diag_r], writes=[ps_s_r])
                P.ins("vector", lambda e, c0=c0, ncols=ncols: e.reduce_max(
                    out=negmax[:, 0:1], in_=ps_s[:, c0:c0 + ncols], axis=AX, negate=True),
                    reads=[ps_s_r], writes=[negmax_r])
                P.ins("vector", lambda e: e.tensor_scalar(out=negmax[:, 0:1], in0=negmax[:, 0:1],
                                                          scalar1=0.125, scalar2=None, op0=OP.mult),
                      reads=[negmax_r], writes=[negmax_r])
                P.ins("scalar", lambda e, c0=c0, ncols=ncols: e.activation(
                    out=w_sb[:, c0:c0 + ncols], in_=ps_s[:, c0:c0 + ncols], func=AF.Exp,
                    bias=negmax[:, 0:1], scale=0.125, accum_out=sumr[:, 0:1]),
                    reads=[ps_s_r, negmax_r], writes=[w_sb_r, sumr_r])
                P.ins("vector", lambda e: e.reciprocal(out=sumr[:, 0:1], in_=sumr[:, 0:1]),
                      reads=[sumr_r], writes=[sumr_r])
                P.ins("vector", lambda e, c0=c0, ncols=ncols: e.tensor_scalar(
                    out=w_sb[:, c0:c0 + ncols], in0=w_sb[:, c0:c0 + ncols],
                    scalar1=sumr[:, 0:1], scalar2=None, op0=OP.mult),
                    reads=[w_sb_r, sumr_r], writes=[w_sb_r])
                emit_transpose_batch(P, [w_sb[:, z * 128:(z + 1) * 128] for z in range(z0, 3)],
                                     w_sb_r,
                                     wT_sb[:, z0 * 128:384].rearrange("p (k m) -> p k m", m=128),
                                     wT_sb_r, ps_tr, ps_tr_r, ident_ap, ident_r)
                bank, bank_r = po[h // 8]
                col = (h % 8) * 64
                for z in range(z0, 3):
                    dk = 2 - z
                    slot = 2 + j - dk
                    P.ins("tensor", lambda e, bank=bank, col=col, z=z, slot=slot, h=h, z0=z0: e.matmul(
                        bank[:, col:col + 64], lhsT=wT_sb[:, z * 128:(z + 1) * 128],
                        rhs=vbuf[:, slot, h * 64:h * 64 + 64],
                        start=(z == z0 and h % 8 == 0), stop=(z == 2 and h % 8 == 7)),
                        reads=[wT_sb_r, vbuf_r], writes=[bank_r])
            for half in range(2):
                bank, bank_r = po[half]
                P.ins("scalar", lambda e, bank=bank, half=half: e.copy(
                    out=o_attn[:, half * 512:(half + 1) * 512], in_=bank[:, 0:512]),
                    reads=[bank_r], writes=[o_attn_r])
            emit_transpose_batch(P, [o_attn[:, b * 128:(b + 1) * 128] for b in range(NCT)],
                                 o_attn_r, oT[:, :, :], oT_r, ps_tr, ps_tr_r, ident_ap, ident_r)
            for half in range(2):
                p_ap, p_res = pf[half]
                for b in range(NCT):
                    P.ins("tensor", lambda e, p_ap=p_ap, b=b, half=half: e.matmul(
                        p_ap[:, 0:512], lhsT=oT[:, b, :], rhs=woT[:, b, half * 512:(half + 1) * 512],
                        start=(b == 0), stop=(b == NCT - 1)),
                        reads=[oT_r, woT_r], writes=[p_res])
                P.ins("vector", lambda e, p_ap=p_ap, j=j, half=half: e.tensor_tensor(
                    out=x_st[:, j, half * 512:(half + 1) * 512],
                    in0=x_st[:, j, half * 512:(half + 1) * 512], in1=p_ap[:, 0:512], op=OP.add),
                    reads=[p_res, x_st_r], writes=[x_st_r])

        proj_to(lambda a: h1T[:, a, :], h1T_r, wf1, wf1_r, 0, 512)
        for a in range(NCT):
            p_ap, p_res = pm[a % 2]
            for b in range(NCT):
                P.ins("tensor", lambda e, p_ap=p_ap, a=a, b=b: e.matmul(
                    p_ap[:, 0:512], lhsT=wf2[:, b, a * 128:(a + 1) * 128], rhs=h1T[:, b, :],
                    start=(b == 0), stop=(b == NCT - 1)),
                    reads=[wf2_r, h1T_r], writes=[p_res])
            P.ins("scalar", lambda e, p_ap=p_ap, a=a: e.copy(out=h2T[:, a, :], in_=p_ap[:, 0:512]),
                  reads=[p_res], writes=[h2T_r])
        for j in range(4):
            for half in range(2):
                p_ap, p_res = pf[half]
                for b in range(NCT):
                    P.ins("tensor", lambda e, p_ap=p_ap, j=j, b=b, half=half: e.matmul(
                        p_ap[:, 0:512], lhsT=h2T[:, b, j * 128:(j + 1) * 128],
                        rhs=wf3[:, b, half * 512:(half + 1) * 512],
                        start=(b == 0), stop=(b == NCT - 1)),
                        reads=[h2T_r, wf3_r], writes=[p_res])
                P.ins("vector", lambda e, p_ap=p_ap, half=half: e.tensor_tensor(
                    out=p_ap[:, 0:512], in0=p_ap[:, 0:512],
                    in1=bias_b[:, half * 512:(half + 1) * 512], op=OP.add),
                    reads=[p_res, bias_b_r], writes=[p_res])
                P.ins("scalar", lambda e, p_ap=p_ap: e.activation(out=tanh_sb[:, 0:512],
                                                                  in_=p_ap[:, 0:512], func=AF.Tanh),
                      reads=[p_res], writes=[tanh_sb_r])
                P.ins("vector", lambda e, j=j, half=half: e.tensor_tensor(
                    out=x_st[:, j, half * 512:(half + 1) * 512],
                    in0=x_st[:, j, half * 512:(half + 1) * 512], in1=tanh_sb[:, 0:512], op=OP.add),
                    reads=[tanh_sb_r, x_st_r], writes=[x_st_r])

        for j in range(4):
            i = u * 4 + j
            P.dma("scalar", x_out_t[i], x_st[:, j, :], reads=[x_st_r], writes=[x_out["r"]])



def emit_layer3(P, es, dram, NT):
    nc = P.nc
    x_in = dram["l3_in"]
    x_out = dram["l3_out"]
    ident_ap, ident_r = dram["ident_sb"]
    ident32_ap, ident32_r = dram["ident32_sb"]
    ones_ap, ones_r = dram["ones_sb"]
    GT = 8
    NG = NT // GT
    NHT = 32

    kv_acc, kv_acc_r = P.sb(es, "l3_kvacc", [128, 512], F32, strict=True)
    kv_bf, kv_bf_r = P.sb(es, "l3_kvbf", [128, 512], BF16, strict=True)
    mvi_all, mvi_all_r = P.sb(es, "l3_mviall", [128, 2 * NT], F32, strict=True)
    w_all, w_all_r = P.sb(es, "l3_wall", [128, 8 * NT], F32, strict=True)
    wT8_all, wT8_all_r = P.sb(es, "l3_wT8", [8, 128 * NT], BF16)
    bn12, bn12_r = P.sb(es, "l3_bn12", [128, 12], F32, strict=True)
    P.ins("vector", lambda e: e.memset(kv_acc[:], 0.0), writes=[kv_acc_r])

    pool = [P.ps(es, f"l3_pp{i}", [128, 512]) for i in range(8)]

    x_in_t = x_in["ap"].rearrange("(j p) c -> j p c", p=128)
    x_out_t = x_out["ap"].rearrange("(j p) c -> j p c", p=128)

    def _phase1():
      with ExitStack() as es1:
        wk, wk_r = P.sb(es1, "l3_wk", [128, NCT, D], BF16)
        wv, wv_r = P.sb(es1, "l3_wv", [128, NCT, D], BF16)
        P.dma("gpsimd", wk[:], dram["w3_Wk"].rearrange("(b p) d -> p b d", p=128), writes=[wk_r])
        P.dma("gpsimd", wv[:], dram["w3_Wv"].rearrange("(b p) d -> p b d", p=128), writes=[wv_r])
        bk_row, bk_row_r = P.sb(es1, "l3_bk", [1, D], BF16)
        bv_row, bv_row_r = P.sb(es1, "l3_bv", [1, D], BF16)
        P.dma("gpsimd", bk_row[:], dram["w3_bk"].unsqueeze(0), writes=[bk_row_r])
        P.dma("gpsimd", bv_row[:], dram["w3_bv"].unsqueeze(0), writes=[bv_row_r])
        x_sb, x_sb_r = P.sb(es1, "l3_x", [128, D], F32)
        xn_sb, xn_sb_r = P.sb(es1, "l3_xn", [128, D], BF16)
        xnT, xnT_r = P.sb(es1, "l3_xnT", [128, NCT, 128], BF16, strict=True)
        k_sb, k_sb_r = P.sb(es1, "l3_k", [128, D], BF16)
        v_sb, v_sb_r = P.sb(es1, "l3_v", [128, D], BF16)
        t1, t1_r = P.sb(es1, "l3_t1", [128, 512], F32)
        t2, t2_r = P.sb(es1, "l3_t2", [128, 512], F32)
        mvi, mvi_r = P.sb(es1, "l3_mvi", [128, 2], F32, strict=True)

        for i in range(NT):
            P.dma("sync", x_sb[:], x_in_t[i], reads=[x_in["r"]], writes=[x_sb_r])
            emit_ln_tile(P, x_sb[:], x_sb_r, xn_sb[:], xn_sb_r, bn12, bn12_r, mvi, mvi_r)
            P.ins("vector", lambda e, i=i: e.tensor_copy(out=mvi_all[:, 2 * i:2 * i + 2],
                                                         in_=mvi[:, 0:2]),
                  reads=[mvi_r], writes=[mvi_all_r])
            emit_transpose_batch(P, [xn_sb[:, b * 128:(b + 1) * 128] for b in range(NCT)],
                                 xn_sb_r, xnT[:, :, :], xnT_r,
                                 pool[7][0][:, :].bitcast(BF16), pool[7][1], ident_ap, ident_r)
            for (wmat, wmat_r, brow, brow_r, dst, dst_r, use_phi) in [
                    (wk, wk_r, bk_row, bk_row_r, k_sb, k_sb_r, True),
                    (wv, wv_r, bv_row, bv_row_r, v_sb, v_sb_r, False)]:
                for half in range(2):
                    p_ap, p_res = pool[half]
                    for b in range(NCT):
                        P.ins("tensor", lambda e, p_ap=p_ap, b=b, half=half, wmat=wmat: e.matmul(
                            p_ap[:, 0:512], lhsT=xnT[:, b, :],
                            rhs=wmat[:, b, half * 512:(half + 1) * 512],
                            start=(b == 0), stop=False),
                            reads=[xnT_r, wmat_r], writes=[p_res])
                    P.ins("tensor", lambda e, p_ap=p_ap, half=half, brow=brow: e.matmul(
                        p_ap[:, 0:512], lhsT=ones_ap[0:1, 0:128],
                        rhs=brow[0:1, half * 512:(half + 1) * 512], start=False, stop=True),
                        reads=[ones_r, brow_r], writes=[p_res])
                    if use_phi:
                        emit_phi_evac(P, dst[:, half * 512:(half + 1) * 512], dst_r,
                                      p_ap[:, 0:512], p_res, t1, t1_r, t2, t2_r, 512)
                    else:
                        P.ins("scalar", lambda e, p_ap=p_ap, half=half, dst=dst: e.copy(
                            out=dst[:, half * 512:(half + 1) * 512], in_=p_ap[:, 0:512]),
                            reads=[p_res], writes=[dst_r])
            for h in range(16):
                pb = (h % 2) * 64
                p2 = h // 2
                p_ap, p_res = pool[2]
                P.ins("tensor", lambda e, p_ap=p_ap, h=h: e.matmul(
                    p_ap[0:64, 0:64], lhsT=k_sb[:, h * 64:(h + 1) * 64],
                    rhs=v_sb[:, h * 64:(h + 1) * 64], start=True, stop=True),
                    reads=[k_sb_r, v_sb_r], writes=[p_res])
                P.ins("vector", lambda e, p_ap=p_ap, pb=pb, p2=p2: e.tensor_tensor(
                    out=kv_acc[pb:pb + 64, p2 * 64:(p2 + 1) * 64],
                    in0=kv_acc[pb:pb + 64, p2 * 64:(p2 + 1) * 64],
                    in1=p_ap[0:64, 0:64], op=OP.add),
                    reads=[p_res, kv_acc_r], writes=[kv_acc_r])
        P.ins("scalar", lambda e: e.copy(out=kv_bf[:], in_=kv_acc[:]),
              reads=[kv_acc_r], writes=[kv_bf_r])
        P.barrier()

    _phase1()

    def _phase2():
      xo_rt = [R(f"l3xo{i}") for i in range(NT)]
      b2_all, b2_all_r = P.sb(es, "l3_b2", [8, D], BF16)
      P.dma("gpsimd", b2_all[:], dram["w3_b2"][:, :], writes=[b2_all_r])
      b1c, b1c_r = P.sb(es, "l3_b1c", [128, NHT * 8], F32, strict=True)
      for e_ in range(8):
          emit_load_cols(P, b1c[:, e_ * NHT:(e_ + 1) * NHT], b1c_r, dram["w3_b1"][e_], nb=NHT)

      def _phaseA():
        with ExitStack() as es2:
            wq, wq_r = P.sb(es2, "l3_wq", [128, NCT, D], BF16)
            wp, wp_r = P.sb(es2, "l3_wp", [128, NCT, D], BF16)
            P.dma("gpsimd", wq[:], dram["w3_Wq"].rearrange("(b p) d -> p b d", p=128), writes=[wq_r])
            P.dma("gpsimd", wp[:], dram["w3_Wp"].rearrange("(b p) d -> p b d", p=128), writes=[wp_r])
            bp_row, bp_row_r = P.sb(es2, "l3_bp", [1, D], BF16)
            P.dma("gpsimd", bp_row[:], dram["w3_bp"].unsqueeze(0), writes=[bp_row_r])
            bq_cols, bq_cols_r = P.sb(es2, "l3_bqc", [128, NCT], F32, strict=True)
            emit_load_cols(P, bq_cols, bq_cols_r, dram["w3_bq"])
            centT, centT_r = P.sb(es2, "l3_centT", [128, NCT, 8], BF16)
            P.dma("gpsimd", centT[:], dram["w3_centT"].rearrange("(b p) e -> p b e", p=128),
                  writes=[centT_r])
            xsb2 = [P.sb(es2, f"l3_x2{i}", [128, D], F32) for i in range(2)]
            xn_sb, xn_sb_r = P.sb(es2, "l3_xn2", [128, D], BF16)
            xnT_g, xnT_g_r = P.sb(es2, "l3_xnTg", [128, NCT, GT * 128], BF16, strict=True)
            qb_sb, qb_sb_r = P.sb(es2, "l3_qb", [128, 512], F32)
            qT_g, qT_g_r = P.sb(es2, "l3_qTg", [128, NCT, GT * 128], BF16)
            oT_t, oT_t_r = P.sb(es2, "l3_oTt", [128, NCT, 128], BF16)
            t1, t1_r = P.sb(es2, "l3_t1b", [128, 512], F32)
            g_sb, g_sb_r = P.sb(es2, "l3_g", [128, 8], F32)
            rot_sb, rot_sb_r = P.sb(es2, "l3_rot", [128, 8], F32)
            cmp_sb, cmp_sb_r = P.sb(es2, "l3_cmp", [128, 8], F32)
            ngt_sb, ngt_sb_r = P.sb(es2, "l3_ngt", [128, 8], F32, strict=True)
            wnorm, wnorm_r = P.sb(es2, "l3_wn", [128, 1], F32, strict=True)

            for g in range(NG):
                for jj in range(GT):
                    i = g * GT + jj
                    x_sb, x_sb_r = xsb2[jj % 2]
                    P.dma("sync", x_sb[:], x_in_t[i], reads=[x_in["r"]], writes=[x_sb_r],
                          slot=jj % 2)
                    P.ins("vector", lambda e, i=i, x_sb=x_sb: e.tensor_scalar(
                        out=xn_sb[:], in0=x_sb[:], scalar1=mvi_all[:, 2 * i:2 * i + 1],
                        scalar2=mvi_all[:, 2 * i + 1:2 * i + 2], op0=OP.subtract, op1=OP.mult),
                        reads=[x_sb_r, mvi_all_r], writes=[xn_sb_r])
                    emit_transpose_batch(P, [xn_sb[:, b * 128:(b + 1) * 128] for b in range(NCT)],
                                         xn_sb_r, xnT_g[:, :, jj * 128:(jj + 1) * 128], xnT_g_r,
                                         pool[7][0][:, :].bitcast(BF16), pool[7][1],
                                         ident_ap, ident_r)
                    p_ap, p_res = pool[6]
                    for b in range(NCT):
                        P.ins("tensor", lambda e, p_ap=p_ap, b=b, jj=jj: e.matmul(
                            p_ap[:, 0:8], lhsT=xnT_g[:, b, jj * 128:(jj + 1) * 128],
                            rhs=centT[:, b, :], start=(b == 0), stop=(b == NCT - 1)),
                            reads=[xnT_g_r, centT_r], writes=[p_res])
                    P.ins("scalar", lambda e, p_ap=p_ap: e.activation(out=g_sb[:], in_=p_ap[:, 0:8],
                                                                     func=AF.Sigmoid),
                          reads=[p_res], writes=[g_sb_r])
                    P.ins("vector", lambda e: e.memset(ngt_sb[:], 0.0), writes=[ngt_sb_r])
                    for s in range(1, 8):
                        P.ins("vector", lambda e, s=s: e.tensor_copy(out=rot_sb[:, 0:8 - s],
                                                                     in_=g_sb[:, s:8]),
                              reads=[g_sb_r], writes=[rot_sb_r])
                        P.ins("vector", lambda e, s=s: e.tensor_copy(out=rot_sb[:, 8 - s:8],
                                                                     in_=g_sb[:, 0:s]),
                              reads=[g_sb_r], writes=[rot_sb_r])
                        P.ins("vector", lambda e: e.tensor_tensor(out=cmp_sb[:], in0=rot_sb[:],
                                                                  in1=g_sb[:], op=OP.is_gt),
                              reads=[rot_sb_r, g_sb_r], writes=[cmp_sb_r])
                        P.ins("vector", lambda e: e.tensor_tensor(out=ngt_sb[:], in0=ngt_sb[:],
                                                                  in1=cmp_sb[:], op=OP.add),
                              reads=[cmp_sb_r, ngt_sb_r], writes=[ngt_sb_r])
                    P.ins("vector", lambda e: e.tensor_scalar(out=ngt_sb[:], in0=ngt_sb[:],
                                                              scalar1=1.5, scalar2=1.0,
                                                              op0=OP.is_gt, op1=OP.mult),
                          reads=[ngt_sb_r], writes=[ngt_sb_r])
                    P.ins("vector", lambda e: e.tensor_tensor(out=cmp_sb[:], in0=g_sb[:],
                                                              in1=ngt_sb[:], op=OP.mult),
                          reads=[g_sb_r, ngt_sb_r], writes=[cmp_sb_r])
                    P.ins("vector", lambda e: e.tensor_tensor(out=g_sb[:], in0=g_sb[:],
                                                              in1=cmp_sb[:], op=OP.subtract),
                          reads=[cmp_sb_r, g_sb_r], writes=[g_sb_r])
                    P.ins("vector", lambda e: e.reduce_sum(out=wnorm[:, 0:1], in_=g_sb[:], axis=AX),
                          reads=[g_sb_r], writes=[wnorm_r])
                    P.ins("vector", lambda e: e.reciprocal(out=wnorm[:, 0:1], in_=wnorm[:, 0:1]),
                          reads=[wnorm_r], writes=[wnorm_r])
                    P.ins("vector", lambda e, i=i: e.tensor_scalar(
                        out=w_all[:, i * 8:(i + 1) * 8], in0=g_sb[:], scalar1=wnorm[:, 0:1],
                        scalar2=None, op0=OP.mult),
                        reads=[g_sb_r, wnorm_r], writes=[w_all_r])
                    P.ins("tensor", lambda e, i=i: e.transpose(
                        pool[7][0][0:8, 0:128], in_=w_all[:, i * 8:(i + 1) * 8],
                        identity=ident32_ap),
                        reads=[w_all_r, ident32_r], writes=[pool[7][1]])
                    P.ins("scalar", lambda e, i=i: e.copy(out=wT8_all[0:8, i * 128:(i + 1) * 128],
                                                          in_=pool[7][0][0:8, 0:128]),
                          reads=[pool[7][1]], writes=[wT8_all_r])
                for a in range(NCT):
                    for chunk in range(2):
                        p_ap, p_res = pool[(a * 2 + chunk) % 2]
                        for b in range(NCT):
                            P.ins("tensor", lambda e, p_ap=p_ap, a=a, b=b, chunk=chunk: e.matmul(
                                p_ap[:, 0:512], lhsT=wq[:, b, a * 128:(a + 1) * 128],
                                rhs=xnT_g[:, b, chunk * 512:(chunk + 1) * 512],
                                start=(b == 0), stop=(b == NCT - 1)),
                                reads=[wq_r, xnT_g_r], writes=[p_res])
                        P.ins("scalar", lambda e, p_ap=p_ap, a=a: e.activation(
                            out=qb_sb[:, 0:512], in_=p_ap[:, 0:512], func=AF.Identity,
                            bias=bq_cols[:, a:a + 1], scale=1.0),
                            reads=[p_res, bq_cols_r], writes=[qb_sb_r])
                        emit_phi_evac(P, qT_g[:, a, chunk * 512:(chunk + 1) * 512], qT_g_r,
                                      qb_sb[:, 0:512], qb_sb_r, t1, t1_r, qb_sb, qb_sb_r, 512)
                for jj in range(GT):
                    i = g * GT + jj
                    x_sb, x_sb_r = xsb2[jj % 2]
                    P.dma("sync", x_sb[:], x_in_t[i], reads=[x_in["r"]], writes=[x_sb_r],
                          slot=jj % 2)
                    for a in range(NCT):
                        p_ap, p_res = pool[2 + (a % 2)]
                        for par in range(2):
                            pb = par * 64
                            h = a * 2 + par
                            P.ins("tensor", lambda e, p_ap=p_ap, pb=pb, h=h, a=a, jj=jj: e.matmul(
                                p_ap[pb:pb + 64, 0:128],
                                lhsT=kv_bf[pb:pb + 64, (h // 2) * 64:(h // 2 + 1) * 64],
                                rhs=qT_g[pb:pb + 64, a, jj * 128:(jj + 1) * 128],
                                start=True, stop=True),
                                reads=[kv_bf_r, qT_g_r], writes=[p_res])
                        P.ins("scalar", lambda e, p_ap=p_ap, a=a: e.copy(out=oT_t[:, a, :],
                                                                        in_=p_ap[:, 0:128]),
                              reads=[p_res], writes=[oT_t_r])
                    for half in range(2):
                        p_ap, p_res = pool[4 + half]
                        for b in range(NCT):
                            P.ins("tensor", lambda e, p_ap=p_ap, b=b, half=half: e.matmul(
                                p_ap[:, 0:512], lhsT=oT_t[:, b, :],
                                rhs=wp[:, b, half * 512:(half + 1) * 512],
                                start=(b == 0), stop=False),
                                reads=[oT_t_r, wp_r], writes=[p_res])
                        P.ins("tensor", lambda e, p_ap=p_ap, half=half: e.matmul(
                            p_ap[:, 0:512], lhsT=ones_ap[0:1, 0:128],
                            rhs=bp_row[0:1, half * 512:(half + 1) * 512], start=False, stop=False),
                            reads=[ones_r, bp_row_r], writes=[p_res])
                        P.ins("tensor", lambda e, p_ap=p_ap, i=i, half=half: e.matmul(
                            p_ap[:, 0:512], lhsT=wT8_all[0:8, i * 128:(i + 1) * 128],
                            rhs=b2_all[0:8, half * 512:(half + 1) * 512], start=False, stop=True),
                            reads=[wT8_all_r, b2_all_r], writes=[p_res])
                        P.ins("vector", lambda e, p_ap=p_ap, half=half, x_sb=x_sb: e.tensor_tensor(
                            out=x_sb[:, half * 512:(half + 1) * 512],
                            in0=x_sb[:, half * 512:(half + 1) * 512], in1=p_ap[:, 0:512],
                            op=OP.add),
                            reads=[p_res, x_sb_r], writes=[x_sb_r])
                    P.dma("scalar", x_out_t[i], x_sb[:], reads=[x_sb_r],
                          writes=[x_out["r"], xo_rt[i]], slot=jj % 2)
                xnt_d = dram["l3_xnt"]["ap"].rearrange("(b p) t -> p b t", p=128)
                P.dma("gpsimd", xnt_d[:, :, g * GT * 128:(g + 1) * GT * 128], xnT_g[:],
                      reads=[xnT_g_r], writes=[dram["l3_xnt"]["r"]], slot=g % 2)
        P.barrier()

      _phaseA()

      def _phaseB():
        QD = 2
        with ExitStack() as esB:
            W1sb, W1sb_r = P.sb(esB, "l3_W1sb", [128, NCT, 4096], BF16)
            W2sb, W2sb_r = P.sb(esB, "l3_W2sb", [128, NHT, D], BF16)
            hTq2 = [P.sb(esB, f"l3_hTq{i}", [128, NHT, QD * 128], BF16) for i in range(2)]
            xnTq2 = [P.sb(esB, f"l3_xnTq{i}", [128, NCT, QD * 128], BF16, strict=True) for i in range(2)]
            xo2 = [P.sb(esB, f"l3_xo{i}", [128, D], F32) for i in range(2)]
            w1_t = [dram["w3_W1"][e_].rearrange("(b p) m -> p b m", p=128) for e_ in range(8)]
            w2_t = [dram["w3_W2"][e_].rearrange("(hb p) m -> p hb m", p=128) for e_ in range(8)]
            xnt_d = dram["l3_xnt"]["ap"].rearrange("(b p) t -> p b t", p=128)
            for e_ in range(8):
                for b in range(NCT):
                    P.dma("gpsimd", W1sb[:, b, :], w1_t[e_][:, b, :],
                          writes=[W1sb_r], slot=b % 2)
                for c4 in range(8):
                    P.dma("gpsimd", W2sb[:, c4 * 4:(c4 + 1) * 4, :],
                          w2_t[e_][:, c4 * 4:(c4 + 1) * 4, :], writes=[W2sb_r], slot=c4 % 2)
                for quad in range(NT // QD):
                    xnT_q, xnT_q_r = xnTq2[quad % 2]
                    hT_q, hT_q_r = hTq2[quad % 2]
                    P.dma("sync", xnT_q[:], xnt_d[:, :, quad * QD * 128:(quad + 1) * QD * 128],
                          reads=[dram["l3_xnt"]["r"]], writes=[xnT_q_r], slot=quad % 2)
                    for ht in range(NHT):
                        p_ap, p_res = pool[ht % 4]
                        for b in range(NCT):
                            P.ins("tensor", lambda e, p_ap=p_ap, b=b, ht=ht, xnT_q=xnT_q: e.matmul(
                                p_ap[:, 0:QD * 128], lhsT=W1sb[:, b, ht * 128:(ht + 1) * 128],
                                rhs=xnT_q[:, b, 0:QD * 128],
                                start=(b == 0), stop=(b == NCT - 1)),
                                reads=[W1sb_r, xnT_q_r], writes=[p_res])
                        P.ins("scalar", lambda e, p_ap=p_ap, ht=ht, e_=e_, hT_q=hT_q: e.activation(
                            out=hT_q[:, ht, :], in_=p_ap[:, 0:QD * 128], func=AF.Gelu,
                            bias=b1c[:, e_ * NHT + ht:e_ * NHT + ht + 1], scale=1.0),
                            reads=[p_res, b1c_r], writes=[hT_q_r])
                    for tloc in range(QD):
                        i = quad * QD + tloc
                        xo_sb, xo_r = xo2[tloc % 2]
                        P.dma("sync", xo_sb[:], x_out_t[i], reads=[xo_rt[i]], writes=[xo_r],
                              slot=tloc % 2)
                        for half in range(2):
                            p_ap, p_res = pool[4 + tloc * 2 + half]
                            for ht in range(NHT):
                                P.ins("tensor", lambda e, p_ap=p_ap, ht=ht, tloc=tloc, half=half,
                                      hT_q=hT_q: e.matmul(p_ap[:, 0:512],
                                               lhsT=hT_q[:, ht, tloc * 128:(tloc + 1) * 128],
                                               rhs=W2sb[:, ht, half * 512:(half + 1) * 512],
                                               start=(ht == 0), stop=(ht == NHT - 1)),
                                      reads=[hT_q_r, W2sb_r], writes=[p_res])
                            P.ins("vector", lambda e, p_ap=p_ap, i=i, e_=e_, half=half,
                                  xo_sb=xo_sb: e.scalar_tensor_tensor(
                                      out=xo_sb[:, half * 512:(half + 1) * 512],
                                      in0=p_ap[:, 0:512],
                                      scalar=w_all[:, i * 8 + e_:i * 8 + e_ + 1],
                                      in1=xo_sb[:, half * 512:(half + 1) * 512],
                                      op0=OP.mult, op1=OP.add),
                                  reads=[p_res, w_all_r, xo_r], writes=[xo_r])
                        P.dma("scalar", x_out_t[i], xo_sb[:], reads=[xo_r],
                              writes=[x_out["r"], xo_rt[i]], slot=tloc % 2)
        P.barrier()

      _phaseB()

    _phase2()



def emit_load_cols(P, dst, dst_r, dram_ap, nb=NCT, eng="sync"):
    src = dram_ap.rearrange("(b p) -> b p", p=128)
    for b in range(nb):
        P.dma(eng, dst[:, b:b + 1], src[b].unsqueeze(1), writes=[dst_r])


def emit_fp4_quant(P, out_ap, out_res, in_ap, in_res, scr, K, eng_v="vector"):
    nb = K // 16
    z, z_r = scr["z"]
    y, y_r = scr["y"]
    aq, aq_r = scr["aq"]
    bq, bq_r = scr["bq"]
    sg, sg_r = scr["sg"]
    bm, bm_r = scr["bm"]
    in3 = in_ap.rearrange("p (g k) -> p g k", k=16)
    P.ins("vector", lambda e: e.tensor_reduce(out=bm[:, 0:nb], in_=in3, axis=AX, op=OP.max,
                                           apply_absolute_value=True),
          reads=[in_res], writes=[bm_r])
    P.ins("vector", lambda e: e.tensor_scalar(out=bm[:, nb:2 * nb], in0=bm[:, 0:nb],
                                           scalar1=1.0 / 6.0, scalar2=1e-12, op0=OP.mult, op1=OP.max),
          reads=[bm_r], writes=[bm_r])
    P.ins("vector", lambda e: e.reciprocal(out=bm[:, 2 * nb:3 * nb], in_=bm[:, nb:2 * nb]),
          reads=[bm_r], writes=[bm_r])
    sc3 = bm[:, nb:2 * nb].unsqueeze(2).broadcast_to([128, nb, 16])
    rs3 = bm[:, 2 * nb:3 * nb].unsqueeze(2).broadcast_to([128, nb, 16])
    z3 = z[:, 0:K].rearrange("p (g k) -> p g k", k=16)
    P.ins(eng_v, lambda e: e.tensor_tensor(out=z3, in0=in3, in1=rs3, op=OP.mult),
          reads=[in_res, bm_r], writes=[z_r])
    P.ins("scalar", lambda e: e.sign(out=sg[:, 0:K], in_=z[:, 0:K]),
          reads=[z_r], writes=[sg_r])
    P.ins("scalar", lambda e: e.activation(out=y[:, 0:K], in_=z[:, 0:K], func=AF.Abs),
          reads=[z_r], writes=[y_r])
    MAGIC = 12582912.0
    i1, i1_r = scr["i1"]
    i2, i2_r = scr["i2"]
    P.ins(eng_v, lambda e: e.tensor_scalar(out=i1[:, 0:K], in0=y[:, 0:K], scalar1=2.0, scalar2=2.0,
                                           op0=OP.min, op1=OP.mult),
          reads=[y_r], writes=[i1_r])
    P.ins(eng_v, lambda e: e.tensor_scalar(out=i1[:, 0:K], in0=i1[:, 0:K], scalar1=MAGIC,
                                           scalar2=-MAGIC, op0=OP.add, op1=OP.add),
          reads=[i1_r], writes=[i1_r])
    P.ins(eng_v, lambda e: e.tensor_scalar(out=aq[:, 0:K], in0=i1[:, 0:K], scalar1=0.5, scalar2=-2.0,
                                           op0=OP.mult, op1=OP.add),
          reads=[i1_r], writes=[aq_r])
    P.ins(eng_v, lambda e: e.tensor_scalar(out=i2[:, 0:K], in0=y[:, 0:K], scalar1=2.0, scalar2=4.0,
                                           op0=OP.max, op1=OP.min),
          reads=[y_r], writes=[i2_r])
    P.ins(eng_v, lambda e: e.tensor_scalar(out=i2[:, 0:K], in0=i2[:, 0:K], scalar1=MAGIC,
                                           scalar2=-MAGIC, op0=OP.add, op1=OP.add),
          reads=[i2_r], writes=[i2_r])
    P.ins(eng_v, lambda e: e.tensor_tensor(out=aq[:, 0:K], in0=i2[:, 0:K], in1=aq[:, 0:K],
                                           op=OP.add),
          reads=[i2_r, aq_r], writes=[aq_r])
    P.ins(eng_v, lambda e: e.tensor_scalar(out=bq[:, 0:K], in0=y[:, 0:K], scalar1=5.0, scalar2=2.0,
                                           op0=OP.is_gt, op1=OP.mult),
          reads=[y_r], writes=[bq_r])
    P.ins(eng_v, lambda e: e.tensor_tensor(out=aq[:, 0:K], in0=aq[:, 0:K], in1=bq[:, 0:K], op=OP.add),
          reads=[aq_r, bq_r], writes=[aq_r])
    P.ins(eng_v, lambda e: e.tensor_tensor(out=aq[:, 0:K], in0=aq[:, 0:K], in1=sg[:, 0:K], op=OP.mult),
          reads=[aq_r, sg_r], writes=[aq_r])
    aq3 = aq[:, 0:K].rearrange("p (g k) -> p g k", k=16)
    out3 = out_ap.rearrange("p (g k) -> p g k", k=16)
    P.ins(eng_v, lambda e: e.tensor_tensor(out=out3, in0=aq3, in1=sc3, op=OP.mult),
          reads=[aq_r, bm_r], writes=[out_res])


def emit_layer1_attn(P, es, dram, NT):
    nc = P.nc
    x_in = dram["l1_in"]
    delta = dram["delta"]
    NSEG = NT // 4
    ident_ap, ident_r = dram["ident_sb"]
    ident32_ap, ident32_r = dram["ident32_sb"]
    ones_ap, ones_r = dram["ones_sb"]
    eps5_ap, eps5_r = P.cpool_eps5

    wkT, wkT_r = P.sb(es, "l1_wkT", [128, NCT, D], BF16)
    wvT, wvT_r = P.sb(es, "l1_wvT", [128, NCT, D], BF16)
    wrT, wrT_r = P.sb(es, "l1_wrT", [128, NCT, D], BF16)
    woT, woT_r = P.sb(es, "l1_woT", [128, NCT, D], BF16)
    for (buf, res, key) in [(wkT, wkT_r, "w1_kT"), (wvT, wvT_r, "w1_vT"),
                            (wrT, wrT_r, "w1_rT"), (woT, woT_r, "w1_oT")]:
        P.dma("gpsimd", buf[:], dram[key].rearrange("(b p) d -> p b d", p=128), writes=[res])
    tmk, tmk_r = P.sb(es, "l1_tmk", [128, NCT], F32, strict=True)
    tmv, tmv_r = P.sb(es, "l1_tmv", [128, NCT], F32, strict=True)
    tmr, tmr_r = P.sb(es, "l1_tmr", [128, NCT], F32, strict=True)
    dec, dec_r = P.sb(es, "l1_dec", [128, NCT], F32, strict=True)
    tf, tf_r = P.sb(es, "l1_tf", [128, NCT], F32, strict=True)
    lng, lng_r = P.sb(es, "l1_lng", [128, NCT], F32, strict=True)
    lnb, lnb_r = P.sb(es, "l1_lnb", [128, NCT], BF16, strict=True)
    emit_load_cols(P, tmk, tmk_r, dram["w1_tmk"])
    emit_load_cols(P, tmv, tmv_r, dram["w1_tmv"])
    emit_load_cols(P, tmr, tmr_r, dram["w1_tmr"])
    emit_load_cols(P, dec, dec_r, dram["w1_decay"])
    emit_load_cols(P, tf, tf_r, dram["w1_tfirst"])
    emit_load_cols(P, lng, lng_r, dram["w1_lng"])
    emit_load_cols(P, lnb, lnb_r, dram["w1_lnb"], eng="gpsimd")
    P.ins("scalar", lambda e: e.activation(out=dec[:], in_=dec[:], func=AF.Exp),
          reads=[dec_r], writes=[dec_r])

    Gb, Gb_r = P.sb(es, "l1_Gb", [128, D], F32)
    Bb, Bb_r = P.sb(es, "l1_Bb", [128, D], F32)
    row_sb, row_sb_r = P.sb(es, "l1_row", [1, D], BF16)

    x_sb, x_sb_r = P.sb(es, "l1_x", [128, D], F32)
    xn_sb, xn_sb_r = P.sb(es, "l1_xn", [128, D], BF16)
    xnT, xnT_r = P.sb(es, "l1_xnT", [128, NCT, 513], BF16, strict=True)
    xkT, xkT_r = P.sb(es, "l1_xkT", [128, NCT, 512], BF16)
    xvT, xvT_r = P.sb(es, "l1_xvT", [128, NCT, 512], BF16)
    xrT, xrT_r = P.sb(es, "l1_xrT", [128, NCT, 512], BF16)
    kTs, kTs_r = P.sb(es, "l1_kTs", [128, NCT, 512], BF16)
    vTs, vTs_r = P.sb(es, "l1_vTs", [128, NCT, 512], BF16)
    rTs, rTs_r = P.sb(es, "l1_rTs", [128, NCT, 512], BF16)
    kvs, kvs_r = P.sb(es, "l1_kvs", [128, NCT, 512], F32)
    otm, otm_r = P.sb(es, "l1_otm", [128, NCT, 512], F32)
    sseg, sseg_r = P.sb(es, "l1_sseg", [128, NCT, 512], F32)
    oTs, oTs_r = P.sb(es, "l1_oTs", [128, NCT, 512], BF16)
    sq_sb, sq_sb_r = P.sb(es, "l1_sq", [128, 512], BF16)
    states, states_r = P.sb(es, "l1_states", [128, NCT], F32, strict=True)
    srow, srow_r = P.sb(es, "l1_srow", [1, 3 * 512], F32, strict=True)
    mvc, mvc_r = P.sb(es, "l1_mvc", [128, 2], F32, strict=True)
    att_sb, att_sb_r = P.sb(es, "l1_att", [128, 512], F32)
    c1n, c1n_r = P.sb(es, "l1_c1n", [128, 1], F32, strict=True)
    bn12, bn12_r = P.sb(es, "l1_bn12", [128, 12], F32, strict=True)
    mvi, mvi_r = P.sb(es, "l1_mvi", [128, 2], F32, strict=True)

    pm = [P.ps(es, f"l1_pm{i}", [128, 512]) for i in range(2)]
    pstat, pstat_r = P.ps(es, "l1_pstat", [128, 1024])
    ps_tr, ps_tr_r = P.ps(es, "l1_ps_tr", [128, 1024], BF16)
    pf = [P.ps(es, f"l1_pf{i}", [128, 512]) for i in range(2)]
    ps32, ps32_r = P.ps(es, "l1_ps32", [128, 2])

    x_in_t = x_in["ap"].rearrange("(j p) c -> j p c", p=128)
    delta_t = delta["ap"].rearrange("(j p) c -> j p c", p=128)

    for half in range(2):
        for b in range(NCT):
            P.ins("tensor", lambda e, half=half, b=b: e.matmul(
                pstat[0:1, half * 512:half * 512 + 512], lhsT=lnb[:, b:b + 1],
                rhs=woT[:, b, half * 512:(half + 1) * 512], start=(b == 0), stop=(b == NCT - 1)),
                reads=[lnb_r, woT_r], writes=[pstat_r])
    P.ins("scalar", lambda e: e.copy(out=row_sb[0:1, :], in_=pstat[0:1, 0:1024]),
          reads=[pstat_r], writes=[row_sb_r])
    for half in range(2):
        P.ins("tensor", lambda e, half=half: e.matmul(
            pf[half][0][:, 0:512], lhsT=ones_ap[0:1, 0:128],
            rhs=row_sb[0:1, half * 512:(half + 1) * 512], start=True, stop=True),
            reads=[row_sb_r, ones_r], writes=[pf[half][1]])
        P.ins("scalar", lambda e, half=half: e.copy(out=Bb[:, half * 512:(half + 1) * 512],
                                                    in_=pf[half][0][:, 0:512]),
              reads=[pf[half][1]], writes=[Bb_r])
    for b in range(NCT):
        P.ins("vector", lambda e, b=b: e.tensor_scalar(out=woT[:, b, :], in0=woT[:, b, :],
                                                       scalar1=lng[:, b:b + 1], scalar2=None,
                                                       op0=OP.mult),
              reads=[woT_r, lng_r], writes=[woT_r])
    for half in range(2):
        for b in range(NCT):
            P.ins("tensor", lambda e, half=half, b=b: e.matmul(
                pstat[0:1, half * 512:half * 512 + 512], lhsT=ones_ap[:, 0:1],
                rhs=woT[:, b, half * 512:(half + 1) * 512], start=(b == 0), stop=(b == NCT - 1)),
                reads=[ones_r, woT_r], writes=[pstat_r])
    P.ins("scalar", lambda e: e.copy(out=row_sb[0:1, :], in_=pstat[0:1, 0:1024]),
          reads=[pstat_r], writes=[row_sb_r])
    for half in range(2):
        P.ins("tensor", lambda e, half=half: e.matmul(
            pf[half][0][:, 0:512], lhsT=ones_ap[0:1, 0:128],
            rhs=row_sb[0:1, half * 512:(half + 1) * 512], start=True, stop=True),
            reads=[row_sb_r, ones_r], writes=[pf[half][1]])
        P.ins("scalar", lambda e, half=half: e.copy(out=Gb[:, half * 512:(half + 1) * 512],
                                                    in_=pf[half][0][:, 0:512]),
              reads=[pf[half][1]], writes=[Gb_r])

    P.ins("vector", lambda e: e.memset(states[:], 0.0), writes=[states_r])
    for b in range(NCT):
        P.ins("vector", lambda e, b=b: e.memset(xnT[:, b, 0:1], 0.0), writes=[xnT_r])

    for seg in range(NSEG):
        if seg > 0:
            for b in range(NCT):
                P.ins("gpsimd", lambda e, b=b: e.tensor_copy(out=xnT[:, b, 0:1], in_=xnT[:, b, 512:513]),
                      reads=[xnT_r], writes=[xnT_r])
        for j4 in range(4):
            i = seg * 4 + j4
            P.dma("sync", x_sb[:], x_in_t[i], reads=[x_in["r"]], writes=[x_sb_r])
            emit_ln_tile(P, x_sb[:], x_sb_r, xn_sb[:], xn_sb_r, bn12, bn12_r, mvi, mvi_r)
            emit_transpose_batch(P, [xn_sb[:, b * 128:(b + 1) * 128] for b in range(NCT)],
                                 xn_sb_r, xnT[:, :, 1 + j4 * 128:1 + (j4 + 1) * 128], xnT_r,
                                 ps_tr, ps_tr_r, ident_ap, ident_r)
        for b in range(NCT):
            P.ins("vector", lambda e, b=b: e.tensor_tensor(out=sq_sb[:, 0:512], in0=xnT[:, b, 1:513],
                                                           in1=xnT[:, b, 0:512], op=OP.subtract),
                  reads=[xnT_r], writes=[sq_sb_r])
            for mi, (dst, dst_r, tm, tm_r) in enumerate([(xkT, xkT_r, tmk, tmk_r),
                                                         (xvT, xvT_r, tmv, tmv_r),
                                                         (xrT, xrT_r, tmr, tmr_r)]):
                P.ins("vector", lambda e, b=b, dst=dst, tm=tm:
                      e.scalar_tensor_tensor(
                    out=dst[:, b, :], in0=sq_sb[:, 0:512], scalar=tm[:, b:b + 1],
                    in1=xnT[:, b, 0:512], op0=OP.mult, op1=OP.add),
                    reads=[sq_sb_r, tm_r, xnT_r], writes=[dst_r])
        for (src, src_r, wT, wT_r, dst, dst_r) in [
                (xkT, xkT_r, wkT, wkT_r, kTs, kTs_r),
                (xvT, xvT_r, wvT, wvT_r, vTs, vTs_r),
                (xrT, xrT_r, wrT, wrT_r, rTs, rTs_r)]:
            for a in range(NCT):
                p_ap, p_res = pm[a % 2]
                for b in range(NCT):
                    P.ins("tensor", lambda e, p_ap=p_ap, a=a, b=b, wT=wT, src=src: e.matmul(
                        p_ap[:, 0:512], lhsT=wT[:, b, a * 128:(a + 1) * 128], rhs=src[:, b, :],
                        start=(b == 0), stop=(b == NCT - 1)),
                        reads=[wT_r, src_r], writes=[p_res])
                P.ins("scalar", lambda e, p_ap=p_ap, a=a, dst=dst: e.copy(out=dst[:, a, :],
                                                                          in_=p_ap[:, 0:512]),
                      reads=[p_res], writes=[dst_r])
        for a in range(NCT):
            eng_kv = "vector"
            P.ins(eng_kv, lambda e, a=a: e.tensor_tensor(out=kvs[:, a, :], in0=kTs[:, a, :],
                                                           in1=vTs[:, a, :], op=OP.mult),
                  reads=[kTs_r, vTs_r], writes=[kvs_r])
            P.ins("vector", lambda e, a=a: e.tensor_scalar(out=otm[:, a, :], in0=kvs[:, a, :],
                                                           scalar1=tf[:, a:a + 1], scalar2=None,
                                                           op0=OP.mult),
                  reads=[kvs_r, tf_r], writes=[otm_r])
            P.ins("vector", lambda e, a=a: e.tensor_tensor_scan(
                out=sseg[:, a, :], data0=dec[:, a:a + 1].broadcast_to([128, 512]),
                data1=kvs[:, a, :], initial=states[:, a:a + 1], op0=OP.mult, op1=OP.add),
                reads=[dec_r, kvs_r, states_r], writes=[sseg_r])
            P.ins("vector", lambda e, a=a: e.tensor_copy(out=states[:, a:a + 1],
                                                         in_=sseg[:, a, 511:512]),
                  reads=[sseg_r], writes=[states_r])
            eng_o = "vector"
            P.ins(eng_o, lambda e, a=a: e.tensor_tensor(out=otm[:, a, :], in0=otm[:, a, :],
                                                           in1=sseg[:, a, :], op=OP.add),
                  reads=[otm_r, sseg_r], writes=[otm_r])
            P.ins(eng_o, lambda e, a=a: e.tensor_tensor(out=oTs[:, a, :], in0=otm[:, a, :],
                                                           in1=rTs[:, a, :], op=OP.mult),
                  reads=[otm_r, rTs_r], writes=[oTs_r])
        for a in range(NCT):
            P.ins("scalar", lambda e, a=a: e.square(out=sq_sb[:, 0:512], in_=oTs[:, a, :]),
                  reads=[oTs_r], writes=[sq_sb_r])
            P.ins("tensor", lambda e, a=a: e.matmul(
                pstat[0:1, 0:512], lhsT=ones_ap[:, 0:1], rhs=oTs[:, a, :],
                start=(a == 0), stop=(a == NCT - 1)),
                reads=[ones_r, oTs_r], writes=[pstat_r])
            P.ins("tensor", lambda e, a=a: e.matmul(
                pstat[0:1, 512:1024], lhsT=ones_ap[:, 0:1], rhs=sq_sb[:, 0:512],
                start=(a == 0), stop=(a == NCT - 1)),
                reads=[ones_r, sq_sb_r], writes=[pstat_r])
        P.ins("vector", lambda e: e.tensor_scalar(out=srow[0:1, 0:512], in0=pstat[0:1, 0:512],
                                                  scalar1=1.0 / D, scalar2=None, op0=OP.mult),
              reads=[pstat_r], writes=[srow_r])
        P.ins("vector", lambda e: e.tensor_scalar(out=srow[0:1, 512:1024], in0=pstat[0:1, 512:1024],
                                                  scalar1=1.0 / D, scalar2=None, op0=OP.mult),
              reads=[pstat_r], writes=[srow_r])
        P.ins("vector", lambda e: e.tensor_tensor(out=srow[0:1, 1024:1536], in0=srow[0:1, 0:512],
                                                  in1=srow[0:1, 0:512], op=OP.mult),
              reads=[srow_r], writes=[srow_r])
        P.ins("vector", lambda e: e.tensor_tensor(out=srow[0:1, 512:1024], in0=srow[0:1, 512:1024],
                                                  in1=srow[0:1, 1024:1536], op=OP.subtract),
              reads=[srow_r], writes=[srow_r])
        P.ins("scalar", lambda e: e.activation(out=srow[0:1, 512:1024], in_=srow[0:1, 512:1024],
                                               func=AF.Sqrt, bias=eps5_ap[0:1, 0:1], scale=1.0),
              reads=[srow_r, eps5_r], writes=[srow_r])
        P.ins("vector", lambda e: e.reciprocal(out=srow[0:1, 512:1024], in_=srow[0:1, 512:1024]),
              reads=[srow_r], writes=[srow_r])
        for j4 in range(4):
            i = seg * 4 + j4
            P.ins("tensor", lambda e, j4=j4: e.transpose(
                ps32[:, 0:1], in_=srow[0:1, j4 * 128:(j4 + 1) * 128], identity=ident32_ap[0:1, 0:1]),
                reads=[srow_r, ident32_r], writes=[ps32_r])
            P.ins("tensor", lambda e, j4=j4: e.transpose(
                ps32[:, 1:2], in_=srow[0:1, 512 + j4 * 128:512 + (j4 + 1) * 128],
                identity=ident32_ap[0:1, 0:1]),
                reads=[srow_r, ident32_r], writes=[ps32_r])
            P.ins("scalar", lambda e: e.copy(out=mvc[:, 0:2], in_=ps32[:, 0:2]),
                  reads=[ps32_r], writes=[mvc_r])
            P.ins("vector", lambda e: e.tensor_tensor(out=c1n[:, 0:1], in0=mvc[:, 0:1],
                                                      in1=mvc[:, 1:2], op=OP.mult),
                  reads=[mvc_r], writes=[c1n_r])
            P.ins("vector", lambda e: e.tensor_scalar(out=c1n[:, 0:1], in0=c1n[:, 0:1],
                                                      scalar1=-1.0, scalar2=None, op0=OP.mult),
                  reads=[c1n_r], writes=[c1n_r])
            for half in range(2):
                p_ap, p_res = pf[half]
                for b in range(NCT):
                    P.ins("tensor", lambda e, p_ap=p_ap, b=b, j4=j4, half=half: e.matmul(
                        p_ap[:, 0:512], lhsT=oTs[:, b, j4 * 128:(j4 + 1) * 128],
                        rhs=woT[:, b, half * 512:(half + 1) * 512],
                        start=(b == 0), stop=(b == NCT - 1)),
                        reads=[oTs_r, woT_r], writes=[p_res])
                P.ins("vector", lambda e, p_ap=p_ap: e.tensor_scalar(
                    out=att_sb[:, 0:512], in0=p_ap[:, 0:512], scalar1=mvc[:, 1:2], scalar2=None,
                    op0=OP.mult),
                    reads=[p_res, mvc_r], writes=[att_sb_r])
                P.ins("vector", lambda e, half=half: e.scalar_tensor_tensor(
                    out=att_sb[:, 0:512], in0=Gb[:, half * 512:(half + 1) * 512],
                    scalar=c1n[:, 0:1], in1=att_sb[:, 0:512], op0=OP.mult, op1=OP.add),
                    reads=[Gb_r, c1n_r, att_sb_r], writes=[att_sb_r])
                P.ins("vector", lambda e, half=half: e.tensor_tensor(
                    out=att_sb[:, 0:512], in0=att_sb[:, 0:512],
                    in1=Bb[:, half * 512:(half + 1) * 512], op=OP.add),
                    reads=[att_sb_r, Bb_r], writes=[att_sb_r])
                P.dma("scalar", delta_t[i][:, half * 512:(half + 1) * 512], att_sb[:, 0:512],
                      reads=[att_sb_r], writes=[delta["r"]])


def emit_layer1_ffn(P, es, dram, NT):
    nc = P.nc
    x_in = dram["l1_in"]
    x_out = dram["l1_out"]
    delta = dram["delta"]
    ident_ap, ident_r = dram["ident_sb"]

    NI = 2816
    NKT = NI // 128
    wgT, wgT_r = P.sb(es, "l1_wgT", [128, NCT, NI], BF16)
    wuT, wuT_r = P.sb(es, "l1_wuT", [128, NCT, NI], BF16)
    wdT, wdT_r = P.sb(es, "l1_wdT", [128, NKT, D], BF16)
    ps_tr, ps_tr_r = P.ps(es, "l1f_ps_tr", [128, 1024], BF16)

    with ExitStack() as es_q:
        stage = [P.sb(es_q, f"l1_stage{i}", [128, NI], F32) for i in range(2)]
        z = P.sb(es_q, "l1_qz", [128, 1024], F32)
        y = P.sb(es_q, "l1_qy", [128, 1024], F32)
        aq = P.sb(es_q, "l1_qa", [128, 1024], F32)
        bq = P.sb(es_q, "l1_qb", [128, 1024], F32)
        sg = P.sb(es_q, "l1_qs", [128, 1024], BF16)
        bm = P.sb(es_q, "l1_qm", [128, 3 * 64], F32, strict=True)
        qi1 = P.sb(es_q, "l1_qi1", [128, 1024], F32)
        qi2 = P.sb(es_q, "l1_qi2", [128, 1024], F32)
        wq_nat = [P.sb(es_q, f"l1_wqn{i}", [128, NI], BF16) for i in range(2)]
        scr = {"z": z, "y": y, "aq": aq, "bq": bq, "sg": sg, "bm": bm, "i1": qi1, "i2": qi2}

        def quant_transpose(dram_key, nrows_t, kdim, dst, dst_r):
            src_t = dram[dram_key].rearrange("(r p) k -> r p k", p=128)
            for r in range(nrows_t):
                st, st_r = stage[r % 2]
                wn, wn_r = wq_nat[r % 2]
                P.dma("sync", st[:, 0:kdim], src_t[r], writes=[st_r])
                for k0 in range(0, kdim, 1024):
                    kw = min(1024, kdim - k0)
                    emit_fp4_quant(P, wn[:, k0:k0 + kw], wn_r, st[:, k0:k0 + kw], st_r, scr, kw)
                nkb = kdim // 128
                for kb0 in range(0, nkb, 8):
                    kk = min(8, nkb - kb0)
                    emit_transpose_batch(
                        P, [wn[:, (kb0 + t) * 128:(kb0 + t + 1) * 128] for t in range(kk)],
                        wn_r, dst[:, kb0:kb0 + kk, r * 128:(r + 1) * 128], dst_r,
                        ps_tr, ps_tr_r, ident_ap, ident_r)

        quant_transpose("w1_gate", NKT, D, wgT, wgT_r)
        quant_transpose("w1_up", NKT, D, wuT, wuT_r)
        quant_transpose("w1_down", NCT, NI, wdT, wdT_r)
        P.barrier()

    x_sb, x_sb_r = P.sb(es, "l1f_x", [128, D], F32)
    xn_sb, xn_sb_r = P.sb(es, "l1f_xn", [128, D], BF16)
    xq_sb, xq_sb_r = P.sb(es, "l1f_xq", [128, D], BF16)
    xqT, xqT_r = P.sb(es, "l1f_xqT", [128, NCT, 128], BF16)
    hq, hq_r = P.sb(es, "l1f_hq", [128, NI], BF16)
    hqT, hqT_r = P.sb(es, "l1f_hqT", [128, NKT, 128], BF16)
    h_sb, h_sb_r = P.sb(es, "l1f_h", [128, 512], BF16)
    silu_sb, silu_sb_r = P.sb(es, "l1f_silu", [128, 512], F32)
    d_sb, d_sb_r = P.sb(es, "l1f_d", [128, D], F32)
    bn12, bn12_r = P.sb(es, "l1f_bn12", [128, 12], F32, strict=True)
    mvi, mvi_r = P.sb(es, "l1f_mvi", [128, 2], F32, strict=True)
    zb = P.sb(es, "l1f_qz", [128, 1024], BF16)
    yb = P.sb(es, "l1f_qy", [128, 1024], BF16)
    ab = P.sb(es, "l1f_qa", [128, 1024], BF16)
    bb = P.sb(es, "l1f_qb", [128, 1024], BF16)
    sgb = P.sb(es, "l1f_qs", [128, 1024], BF16)
    bmb = P.sb(es, "l1f_qm", [128, 3 * 64], F32, strict=True)
    qi1b = P.sb(es, "l1f_qi1", [128, 1024], BF16)
    qi2b = P.sb(es, "l1f_qi2", [128, 1024], BF16)
    scrb = {"z": zb, "y": yb, "aq": ab, "bq": bb, "sg": sgb, "bm": bmb, "i1": qi1b, "i2": qi2b}

    pg, pg_r = P.ps(es, "l1f_pg", [128, 512])
    pu, pu_r = P.ps(es, "l1f_pu", [128, 512])
    pd = [P.ps(es, f"l1f_pd{i}", [128, 512]) for i in range(2)]

    x_in_t = x_in["ap"].rearrange("(j p) c -> j p c", p=128)
    x_out_t = x_out["ap"].rearrange("(j p) c -> j p c", p=128)
    delta_t = delta["ap"].rearrange("(j p) c -> j p c", p=128)

    chunks = [(c0, min(512, NI - c0)) for c0 in range(0, NI, 512)]
    for i in range(NT):
        P.dma("sync", x_sb[:], x_in_t[i], reads=[x_in["r"]], writes=[x_sb_r])
        P.dma("sync", d_sb[:], delta_t[i], reads=[delta["r"]], writes=[d_sb_r])
        emit_ln_tile(P, x_sb[:], x_sb_r, xn_sb[:], xn_sb_r, bn12, bn12_r, mvi, mvi_r)
        emit_fp4_quant(P, xq_sb[:], xq_sb_r, xn_sb[:], xn_sb_r, scrb, D)
        emit_transpose_batch(P, [xq_sb[:, b * 128:(b + 1) * 128] for b in range(NCT)],
                             xq_sb_r, xqT[:, :, :], xqT_r, ps_tr, ps_tr_r, ident_ap, ident_r)
        for (c0, cw) in chunks:
            for b in range(NCT):
                P.ins("tensor", lambda e, b=b, c0=c0, cw=cw: e.matmul(
                    pg[:, 0:cw], lhsT=xqT[:, b, :], rhs=wgT[:, b, c0:c0 + cw],
                    start=(b == 0), stop=(b == NCT - 1)),
                    reads=[xqT_r, wgT_r], writes=[pg_r])
            for b in range(NCT):
                P.ins("tensor", lambda e, b=b, c0=c0, cw=cw: e.matmul(
                    pu[:, 0:cw], lhsT=xqT[:, b, :], rhs=wuT[:, b, c0:c0 + cw],
                    start=(b == 0), stop=(b == NCT - 1)),
                    reads=[xqT_r, wuT_r], writes=[pu_r])
            P.ins("scalar", lambda e, cw=cw: e.activation(out=silu_sb[:, 0:cw], in_=pg[:, 0:cw],
                                                          func=AF.Silu),
                  reads=[pg_r], writes=[silu_sb_r])
            P.ins("vector", lambda e, cw=cw: e.tensor_tensor(out=h_sb[:, 0:cw], in0=silu_sb[:, 0:cw],
                                                             in1=pu[:, 0:cw], op=OP.mult),
                  reads=[silu_sb_r, pu_r], writes=[h_sb_r])
            emit_fp4_quant(P, hq[:, c0:c0 + cw], hq_r, h_sb[:, 0:cw], h_sb_r, scrb, cw)
        for kb0 in range(0, NKT, 8):
            kk = min(8, NKT - kb0)
            emit_transpose_batch(P, [hq[:, (kb0 + t) * 128:(kb0 + t + 1) * 128] for t in range(kk)],
                                 hq_r, hqT[:, kb0:kb0 + kk, :], hqT_r,
                                 ps_tr, ps_tr_r, ident_ap, ident_r)
        for half in range(2):
            p_ap, p_res = pd[half]
            for kb in range(NKT):
                P.ins("tensor", lambda e, p_ap=p_ap, kb=kb, half=half: e.matmul(
                    p_ap[:, 0:512], lhsT=hqT[:, kb, :], rhs=wdT[:, kb, half * 512:(half + 1) * 512],
                    start=(kb == 0), stop=(kb == NKT - 1)),
                    reads=[hqT_r, wdT_r], writes=[p_res])
            P.ins("vector", lambda e, p_ap=p_ap, half=half: e.tensor_tensor(
                out=x_sb[:, half * 512:(half + 1) * 512], in0=x_sb[:, half * 512:(half + 1) * 512],
                in1=p_ap[:, 0:512], op=OP.add),
                reads=[p_res, x_sb_r], writes=[x_sb_r])
            P.ins("vector", lambda e, half=half: e.tensor_tensor(
                out=x_sb[:, half * 512:(half + 1) * 512], in0=x_sb[:, half * 512:(half + 1) * 512],
                in1=d_sb[:, half * 512:(half + 1) * 512], op=OP.add),
                reads=[d_sb_r, x_sb_r], writes=[x_sb_r])
        P.dma("scalar", x_out_t[i], x_sb[:], reads=[x_sb_r], writes=[x_out["r"]])



def emit_phi_evac(P, dst_ap, dst_res, psum_ap, psum_res, t1, t1_r, t2, t2_r, n):
    P.ins("scalar", lambda e: e.activation(out=t1[:, 0:n], in_=psum_ap, func=AF.Relu),
          reads=[psum_res], writes=[t1_r])
    P.ins("vector", lambda e: e.tensor_tensor(out=t2[:, 0:n], in0=psum_ap, in1=t1[:, 0:n],
                                              op=OP.subtract),
          reads=[psum_res, t1_r], writes=[t2_r])
    P.ins("scalar", lambda e: e.activation(out=t2[:, 0:n], in_=t2[:, 0:n], func=AF.Exp),
          reads=[t2_r], writes=[t2_r])
    P.ins("vector", lambda e: e.tensor_tensor(out=dst_ap, in0=t2[:, 0:n], in1=t1[:, 0:n], op=OP.add),
          reads=[t2_r, t1_r], writes=[dst_res])


def emit_layer2_attn(P, es, dram, NT):
    nc = P.nc
    x_in = dram["l2_in"]
    delta = dram["delta"]
    NST = NT // 4
    ident_ap, ident_r = dram["ident_sb"]
    binut_ap, binut_r = dram["binut_sb"]

    wqkvT, wqkvT_r = P.sb(es, "l2_wqkvT", [128, NCT, 3 * D], BF16)
    woT, woT_r = P.sb(es, "l2_woT", [128, NCT, D], BF16)
    P.dma("gpsimd", wqkvT[:], dram["w2_qkvT"].rearrange("(b p) d -> p b d", p=128), writes=[wqkvT_r])
    P.dma("gpsimd", woT[:], dram["w2_outT"].rearrange("(b p) d -> p b d", p=128), writes=[woT_r])

    kv_acc, kv_acc_r = P.sb(es, "l2_kvacc", [128, 8 * 65], F32, strict=True)
    kv_bf, kv_bf_r = P.sb(es, "l2_kvbf", [128, 8 * 65], BF16, strict=True)
    P.ins("vector", lambda e: e.memset(kv_acc[:], 0.0), writes=[kv_acc_r])
    P.ins("vector", lambda e: e.memset(kv_bf[:], 0.0), writes=[kv_bf_r])

    x_st, x_st_r = P.sb(es, "l2_xst", [128, 4, D], F32)
    xn_sb, xn_sb_r = P.sb(es, "l2_xn", [128, D], BF16)
    xnT, xnT_r = P.sb(es, "l2_xnT", [128, NCT, 512], BF16, strict=True)
    qT, qT_r = P.sb(es, "l2_qT", [128, NCT, 512], BF16)
    kT, kT_r = P.sb(es, "l2_kT", [128, NCT, 512], BF16)
    k_st, k_st_r = P.sb(es, "l2_kst", [128, 4, D], BF16)
    v_st, v_st_r = P.sb(es, "l2_vst", [128, 4, 16 * 65], BF16)
    sT_sb, sT_sb_r = P.sb(es, "l2_sT", [128, 128], BF16)
    o_attn, o_attn_r = P.sb(es, "l2_oattn", [128, D], BF16)
    oT, oT_r = P.sb(es, "l2_oT", [128, NCT, 128], BF16)
    t1, t1_r = P.sb(es, "l2_t1", [128, 512], F32)
    t2, t2_r = P.sb(es, "l2_t2", [128, 512], F32)
    rden, rden_r = P.sb(es, "l2_rden", [128, 1], F32, strict=True)
    att_sb, att_sb_r = P.sb(es, "l2_att", [128, 512], F32)
    bn12, bn12_r = P.sb(es, "l2_bn12", [128, 12], F32, strict=True)
    mvi, mvi_r = P.sb(es, "l2_mvi", [128, 2], F32, strict=True)

    pm = [P.ps(es, f"l2_pm{i}", [128, 512]) for i in range(2)]
    ps_sT, ps_sT_r = P.ps(es, "l2_ps_sT", [128, 128])
    patt, patt_r = P.ps(es, "l2_patt", [128, 1536])
    ps_kv, ps_kv_r = P.ps(es, "l2_pskv", [128, 65])
    ps_tr, ps_tr_r = P.ps(es, "l2_ps_tr", [128, 1024], BF16)

    x_in_t = x_in["ap"].rearrange("(j p) c -> j p c", p=128)
    delta_t = delta["ap"].rearrange("(j p) c -> j p c", p=128)

    def slot_col(h):
        return (h % 7) * 65 + (h // 7) * 512

    v3 = v_st[:, :, :].rearrange("p j (h x) -> p j h x", x=65)

    for u in range(NST):
        for j4 in range(4):
            i = u * 4 + j4
            P.dma("sync", x_st[:, j4, :], x_in_t[i], reads=[x_in["r"]], writes=[x_st_r])
            emit_ln_tile(P, x_st[:, j4, :], x_st_r, xn_sb[:], xn_sb_r, bn12, bn12_r, mvi, mvi_r)
            emit_transpose_batch(P, [xn_sb[:, b * 128:(b + 1) * 128] for b in range(NCT)],
                                 xn_sb_r, xnT[:, :, j4 * 128:(j4 + 1) * 128], xnT_r,
                                 ps_tr, ps_tr_r, ident_ap, ident_r)
        for (dst, dst_r, off) in [(qT, qT_r, 0), (kT, kT_r, D)]:
            for a in range(NCT):
                p_ap, p_res = pm[a % 2]
                for b in range(NCT):
                    P.ins("tensor", lambda e, p_ap=p_ap, a=a, b=b, off=off: e.matmul(
                        p_ap[:, 0:512], lhsT=wqkvT[:, b, off + a * 128:off + (a + 1) * 128],
                        rhs=xnT[:, b, :], start=(b == 0), stop=(b == NCT - 1)),
                        reads=[wqkvT_r, xnT_r], writes=[p_res])
                emit_phi_evac(P, dst[:, a, :], dst_r, p_ap[:, 0:512], p_res, t1, t1_r, t2, t2_r, 512)
        for j4 in range(4):
            emit_transpose_batch(P, [kT[:, b, j4 * 128:(j4 + 1) * 128] for b in range(NCT)],
                                 kT_r, k_st[:, j4, :].rearrange("p (k m) -> p k m", m=128), k_st_r,
                                 ps_tr, ps_tr_r, ident_ap, ident_r)
        for j4 in range(4):
            for half in range(2):
                p_ap, p_res = pm[half]
                for b in range(NCT):
                    P.ins("tensor", lambda e, p_ap=p_ap, j4=j4, b=b, half=half: e.matmul(
                        p_ap[:, 0:512], lhsT=xnT[:, b, j4 * 128:(j4 + 1) * 128],
                        rhs=wqkvT[:, b, 2 * D + half * 512:2 * D + (half + 1) * 512],
                        start=(b == 0), stop=(b == NCT - 1)),
                        reads=[wqkvT_r, xnT_r], writes=[p_res])
                P.ins("scalar", lambda e, p_ap=p_ap, j4=j4, half=half: e.copy(
                    out=v3[:, j4, half * 8:(half + 1) * 8, 0:64],
                    in_=p_ap[:, 0:512].rearrange("p (h x) -> p h x", x=64)),
                    reads=[p_res], writes=[v_st_r])
            P.ins("vector", lambda e, j4=j4: e.memset(v3[:, j4, :, 64:65], 1.0), writes=[v_st_r])
        for j4 in range(4):
            for h in range(16):
                pb = (h % 2) * 64
                a = h // 2
                p2 = h // 2
                sc = slot_col(h)
                first_in_bank = h in (0, 7, 14)
                last_in_bank = h in (6, 13, 15)
                P.ins("tensor", lambda e, pb=pb, a=a, j4=j4: e.matmul(
                    ps_sT[:, 0:128], lhsT=kT[pb:pb + 64, a, j4 * 128:(j4 + 1) * 128],
                    rhs=qT[pb:pb + 64, a, j4 * 128:(j4 + 1) * 128], start=True, stop=True),
                    reads=[kT_r, qT_r], writes=[ps_sT_r])
                P.ins("vector", lambda e: e.tensor_tensor(out=sT_sb[:], in0=ps_sT[:, 0:128],
                                                          in1=binut_ap, op=OP.mult),
                      reads=[ps_sT_r, binut_r], writes=[sT_sb_r])
                P.ins("tensor", lambda e, sc=sc, j4=j4, h=h, first=first_in_bank: e.matmul(
                    patt[:, sc:sc + 65], lhsT=sT_sb[:], rhs=v3[:, j4, h, 0:65],
                    start=first, stop=False),
                    reads=[sT_sb_r, v_st_r], writes=[patt_r])
                P.ins("tensor", lambda e, sc=sc, pb=pb, a=a, p2=p2, j4=j4, last=last_in_bank: e.matmul(
                    patt[:, sc:sc + 65], lhsT=qT[pb:pb + 64, a, j4 * 128:(j4 + 1) * 128],
                    rhs=kv_bf[pb:pb + 64, p2 * 65:(p2 + 1) * 65], start=False, stop=last),
                    reads=[qT_r, kv_bf_r], writes=[patt_r])
            for h in range(16):
                sc = slot_col(h)
                P.ins("vector", lambda e, sc=sc: e.tensor_scalar(
                    out=rden[:, 0:1], in0=patt[:, sc + 64:sc + 65], scalar1=1e-6, scalar2=None,
                    op0=OP.add),
                    reads=[patt_r], writes=[rden_r])
                P.ins("vector", lambda e: e.reciprocal(out=rden[:, 0:1], in_=rden[:, 0:1]),
                      reads=[rden_r], writes=[rden_r])
                P.ins("vector", lambda e, sc=sc, h=h: e.tensor_scalar(
                    out=o_attn[:, h * 64:(h + 1) * 64], in0=patt[:, sc:sc + 64],
                    scalar1=rden[:, 0:1], scalar2=None, op0=OP.mult),
                    reads=[patt_r, rden_r], writes=[o_attn_r])
            for h in range(16):
                pb = (h % 2) * 64
                p2 = h // 2
                P.ins("tensor", lambda e, h=h, j4=j4: e.matmul(
                    ps_kv[0:64, 0:65], lhsT=k_st[:, j4, h * 64:(h + 1) * 64],
                    rhs=v3[:, j4, h, 0:65], start=True, stop=True),
                    reads=[k_st_r, v_st_r], writes=[ps_kv_r])
                P.ins("vector", lambda e, pb=pb, p2=p2: e.tensor_tensor(
                    out=kv_acc[pb:pb + 64, p2 * 65:(p2 + 1) * 65],
                    in0=kv_acc[pb:pb + 64, p2 * 65:(p2 + 1) * 65], in1=ps_kv[0:64, 0:65],
                    op=OP.add),
                    reads=[ps_kv_r, kv_acc_r], writes=[kv_acc_r])
            P.ins("scalar", lambda e: e.copy(out=kv_bf[:], in_=kv_acc[:]),
                  reads=[kv_acc_r], writes=[kv_bf_r])
            emit_transpose_batch(P, [o_attn[:, b * 128:(b + 1) * 128] for b in range(NCT)],
                                 o_attn_r, oT[:, :, :], oT_r, ps_tr, ps_tr_r, ident_ap, ident_r)
            i = u * 4 + j4
            for half in range(2):
                p_ap, p_res = pm[half]
                for b in range(NCT):
                    P.ins("tensor", lambda e, p_ap=p_ap, b=b, half=half: e.matmul(
                        p_ap[:, 0:512], lhsT=oT[:, b, :], rhs=woT[:, b, half * 512:(half + 1) * 512],
                        start=(b == 0), stop=(b == NCT - 1)),
                        reads=[oT_r, woT_r], writes=[p_res])
                P.ins("scalar", lambda e, p_ap=p_ap: e.copy(out=att_sb[:, 0:512], in_=p_ap[:, 0:512]),
                      reads=[p_res], writes=[att_sb_r])
                P.dma("scalar", delta_t[i][:, half * 512:(half + 1) * 512], att_sb[:, 0:512],
                      reads=[att_sb_r], writes=[delta["r"]])


def emit_absmean_quant(P, es_q, dram, dram_key, nrt, kdim, dst, dst_r, stage, stage_r, cpool_one):
    nc = P.nc
    one_ap, one_r = cpool_one
    asum, asum_r = P.sb(es_q, "q_asum_" + dram_key, [128, nrt], F32)
    g_ps, g_ps_r = P.ps(es_q, "q_gps_" + dram_key, [128, 512])
    grow, grow_r = P.sb(es_q, "q_grow_" + dram_key, [1, max(nrt, 128)], F32)
    thr, thr_r = P.sb(es_q, "q_thr_" + dram_key, [128, 1], F32, strict=True)
    sgn, sgn_r = P.sb(es_q, "q_sgn_" + dram_key, [128, kdim], BF16)
    ones_row32, ones_row32_r = P.sb(es_q, "q_or32_" + dram_key, [1, 128], F32)
    P.ins("vector", lambda e: e.memset(ones_row32[:], 1.0), writes=[ones_row32_r])
    src_t = dram[dram_key].rearrange("(r p) k -> r p k", p=128)
    for r in range(nrt):
        P.dma("sync", stage[:, 0:kdim], src_t[r], writes=[stage_r])
        P.ins("scalar", lambda e, r=r: e.activation(out=stage[:, 0:kdim], in_=stage[:, 0:kdim],
                                                    func=AF.Abs, accum_out=asum[:, r:r + 1]),
              reads=[stage_r], writes=[stage_r, asum_r])
    P.ins("tensor", lambda e: e.matmul(g_ps[0:1, 0:nrt], lhsT=one_ap, rhs=asum[:, 0:nrt],
                                       start=True, stop=True),
          reads=[one_r, asum_r], writes=[g_ps_r])
    P.ins("vector", lambda e: e.reduce_sum(out=grow[0:1, 0:1], in_=g_ps[0:1, 0:nrt], axis=AX),
          reads=[g_ps_r], writes=[grow_r])
    N = nrt * 128 * kdim
    P.ins("vector", lambda e: e.tensor_scalar(out=grow[0:1, 0:1], in0=grow[0:1, 0:1],
                                              scalar1=0.5 / N, scalar2=0.5e-6, op0=OP.mult,
                                              op1=OP.add),
          reads=[grow_r], writes=[grow_r])
    P.ins("tensor", lambda e: e.matmul(g_ps[:, 0:1], lhsT=ones_row32[0:1, 0:128],
                                       rhs=grow[0:1, 0:1], start=True, stop=True),
          reads=[ones_row32_r, grow_r], writes=[g_ps_r])
    P.ins("scalar", lambda e: e.copy(out=thr[:, 0:1], in_=g_ps[:, 0:1]),
          reads=[g_ps_r], writes=[thr_r])
    for r in range(nrt):
        P.dma("sync", stage[:, 0:kdim], src_t[r], writes=[stage_r])
        P.ins("scalar", lambda e: e.sign(out=sgn[:, 0:kdim], in_=stage[:, 0:kdim]),
              reads=[stage_r], writes=[sgn_r])
        P.ins("scalar", lambda e: e.activation(out=stage[:, 0:kdim], in_=stage[:, 0:kdim], func=AF.Abs),
              reads=[stage_r], writes=[stage_r])
        P.ins("vector", lambda e: e.tensor_scalar(out=stage[:, 0:kdim], in0=stage[:, 0:kdim],
                                                  scalar1=thr[:, 0:1], scalar2=1.0,
                                                  op0=OP.is_gt, op1=OP.mult),
              reads=[stage_r, thr_r], writes=[stage_r])
        P.ins("vector", lambda e, r=r: e.tensor_tensor(out=dst[:, r, :], in0=stage[:, 0:kdim],
                                                       in1=sgn[:, 0:kdim], op=OP.mult),
              reads=[stage_r, sgn_r], writes=[dst_r])


def emit_layer2_ffn(P, es, dram, NT):
    nc = P.nc
    x_in = dram["l2_in"]
    x_out = dram["l2_out"]
    delta = dram["delta"]
    ident_ap, ident_r = dram["ident_sb"]
    NH = 4096
    NHT = NH // 128

    wfc, wfc_r = P.sb(es, "l2_wfc", [128, NCT, NH], BF16)
    wpj, wpj_r = P.sb(es, "l2_wpj", [128, NHT, D], BF16)
    lnw_b, lnw_b_r = P.sb(es, "l2_lnwb", [128, NH], BF16)
    lnb_b, lnb_b_r = P.sb(es, "l2_lnbb", [128, NH], BF16, strict=True)
    P.dma("gpsimd", lnw_b[:], dram["w2_lnw"].unsqueeze(0).broadcast_to([128, NH]), writes=[lnw_b_r])
    P.dma("gpsimd", lnb_b[:], dram["w2_lnb"].unsqueeze(0).broadcast_to([128, NH]), writes=[lnb_b_r])

    with ExitStack() as es_q:
        stage, stage_r = P.sb(es_q, "l2_stage", [128, NH], F32)
        emit_absmean_quant(P, es_q, dram, "w2_fcT", NCT, NH, wfc, wfc_r, stage, stage_r, P.cpool_one)
        emit_absmean_quant(P, es_q, dram, "w2_projT", NHT, D, wpj, wpj_r, stage, stage_r, P.cpool_one)
        P.barrier()

    x_sb, x_sb_r = P.sb(es, "l2f_x", [128, D], F32)
    d_sb, d_sb_r = P.sb(es, "l2f_d", [128, D], F32)
    xn_sb, xn_sb_r = P.sb(es, "l2f_xn", [128, D], BF16)
    xnT, xnT_r = P.sb(es, "l2f_xnT", [128, NCT, 128], BF16, strict=True)
    h_sb, h_sb_r = P.sb(es, "l2f_h", [128, NH], F32)
    hn_sb, hn_sb_r = P.sb(es, "l2f_hn", [128, NH], BF16)
    hnT, hnT_r = P.sb(es, "l2f_hnT", [128, NHT, 128], BF16)
    bn12, bn12_r = P.sb(es, "l2f_bn12", [128, 12], F32, strict=True)
    bnh, bnh_r = P.sb(es, "l2f_bnh", [128, 6 * 8], F32, strict=True)
    mvi, mvi_r = P.sb(es, "l2f_mvi", [128, 2], F32, strict=True)
    mvh, mvh_r = P.sb(es, "l2f_mvh", [128, 2], F32, strict=True)
    eps5_ap, eps5_r = P.cpool_eps5

    pm = [P.ps(es, f"l2f_pm{i}", [128, 512]) for i in range(2)]
    pf = [P.ps(es, f"l2f_pf{i}", [128, 512]) for i in range(2)]
    ps_tr, ps_tr_r = P.ps(es, "l2f_ps_tr", [128, 1024], BF16)

    x_in_t = x_in["ap"].rearrange("(j p) c -> j p c", p=128)
    x_out_t = x_out["ap"].rearrange("(j p) c -> j p c", p=128)
    delta_t = delta["ap"].rearrange("(j p) c -> j p c", p=128)

    for i in range(NT):
        P.dma("sync", x_sb[:], x_in_t[i], reads=[x_in["r"]], writes=[x_sb_r])
        P.dma("sync", d_sb[:], delta_t[i], reads=[delta["r"]], writes=[d_sb_r])
        emit_ln_tile(P, x_sb[:], x_sb_r, xn_sb[:], xn_sb_r, bn12, bn12_r, mvi, mvi_r)
        emit_transpose_batch(P, [xn_sb[:, b * 128:(b + 1) * 128] for b in range(NCT)],
                             xn_sb_r, xnT[:, :, :], xnT_r, ps_tr, ps_tr_r, ident_ap, ident_r)
        for hc in range(NCT):
            p_ap, p_res = pm[hc % 2]
            for b in range(NCT):
                P.ins("tensor", lambda e, p_ap=p_ap, hc=hc, b=b: e.matmul(
                    p_ap[:, 0:512], lhsT=xnT[:, b, :], rhs=wfc[:, b, hc * 512:(hc + 1) * 512],
                    start=(b == 0), stop=(b == NCT - 1)),
                    reads=[xnT_r, wfc_r], writes=[p_res])
            P.ins("scalar", lambda e, p_ap=p_ap, hc=hc: e.activation(
                out=h_sb[:, hc * 512:(hc + 1) * 512], in_=p_ap[:, 0:512], func=AF.Gelu),
                reads=[p_res], writes=[h_sb_r])
        for hc in range(NCT):
            P.ins("vector", lambda e, hc=hc: e.bn_stats(out=bnh[:, hc * 6:(hc + 1) * 6],
                                                        in_=h_sb[:, hc * 512:(hc + 1) * 512]),
                  reads=[h_sb_r], writes=[bnh_r])
        P.ins("vector", lambda e: e.bn_aggr(out=mvh[:, 0:2], in_=bnh[:, 0:48]),
              reads=[bnh_r], writes=[mvh_r])
        P.ins("scalar", lambda e: e.activation(out=mvh[:, 1:2], in_=mvh[:, 1:2], func=AF.Sqrt,
                                               bias=eps5_ap, scale=1.0),
              reads=[mvh_r, eps5_r], writes=[mvh_r])
        P.ins("vector", lambda e: e.reciprocal(out=mvh[:, 1:2], in_=mvh[:, 1:2]),
              reads=[mvh_r], writes=[mvh_r])
        P.ins("vector", lambda e: e.tensor_scalar(out=h_sb[:], in0=h_sb[:], scalar1=mvh[:, 0:1],
                                                  scalar2=mvh[:, 1:2], op0=OP.subtract, op1=OP.mult),
              reads=[h_sb_r, mvh_r], writes=[h_sb_r])
        P.ins("vector", lambda e: e.tensor_tensor(out=h_sb[:], in0=h_sb[:], in1=lnw_b[:], op=OP.mult),
              reads=[h_sb_r, lnw_b_r], writes=[h_sb_r])
        P.ins("vector", lambda e: e.tensor_tensor(out=hn_sb[:], in0=h_sb[:], in1=lnb_b[:], op=OP.add),
              reads=[h_sb_r, lnb_b_r], writes=[hn_sb_r])
        for hb0 in range(0, NHT, 8):
            emit_transpose_batch(P, [hn_sb[:, (hb0 + t) * 128:(hb0 + t + 1) * 128] for t in range(8)],
                                 hn_sb_r, hnT[:, hb0:hb0 + 8, :], hnT_r,
                                 ps_tr, ps_tr_r, ident_ap, ident_r)
        for half in range(2):
            p_ap, p_res = pf[half]
            for hb in range(NHT):
                P.ins("tensor", lambda e, p_ap=p_ap, hb=hb, half=half: e.matmul(
                    p_ap[:, 0:512], lhsT=hnT[:, hb, :], rhs=wpj[:, hb, half * 512:(half + 1) * 512],
                    start=(hb == 0), stop=(hb == NHT - 1)),
                    reads=[hnT_r, wpj_r], writes=[p_res])
            P.ins("vector", lambda e, p_ap=p_ap, half=half: e.tensor_tensor(
                out=x_sb[:, half * 512:(half + 1) * 512], in0=x_sb[:, half * 512:(half + 1) * 512],
                in1=p_ap[:, 0:512], op=OP.add),
                reads=[p_res, x_sb_r], writes=[x_sb_r])
            P.ins("vector", lambda e, half=half: e.tensor_tensor(
                out=x_sb[:, half * 512:(half + 1) * 512], in0=x_sb[:, half * 512:(half + 1) * 512],
                in1=d_sb[:, half * 512:(half + 1) * 512], op=OP.add),
                reads=[d_sb_r, x_sb_r], writes=[x_sb_r])
        P.dma("scalar", x_out_t[i], x_sb[:], reads=[x_sb_r], writes=[x_out["r"]])


L2_PARTS = ("attn", "ffn")


def emit_layer2(P, es, dram, NT):
    if "attn" in L2_PARTS:
        with ExitStack() as es_a:
            emit_layer2_attn(P, es_a, dram, NT)
            P.barrier()
    if "ffn" in L2_PARTS:
        with ExitStack() as es_b:
            emit_layer2_ffn(P, es_b, dram, NT)
            P.barrier()


def emit_layer1(P, es, dram, NT):
    with ExitStack() as es_a:
        emit_layer1_attn(P, es_a, dram, NT)
        P.barrier()
    with ExitStack() as es_b:
        emit_layer1_ffn(P, es_b, dram, NT)
        P.barrier()



def build_program(T=4096, layers=(0, 1, 2, 3)):
    nc = bass.Bass("TRN2", target_bir_lowering=False, detect_race_conditions=False)
    NT = T // 128

    dram = {}
    dram["x_in"] = {"ap": nc.dram_tensor("x_in", [T, D], F32, kind="ExternalInput").ap(), "r": R("x_in")}
    dram["x_out"] = {"ap": nc.dram_tensor("x_out", [T, D], F32, kind="ExternalOutput").ap(), "r": R("x_out")}
    dram["x_mid_a"] = {"ap": nc.dram_tensor("x_mid_a", [T, D], F32, kind="Internal").ap(), "r": R("x_mid_a")}
    dram["x_mid_b"] = {"ap": nc.dram_tensor("x_mid_b", [T, D], F32, kind="Internal").ap(), "r": R("x_mid_b")}
    dram["delta"] = {"ap": nc.dram_tensor("delta", [T, D], F32, kind="Internal").ap(), "r": R("delta")}
    if 3 in layers:
        dram["l3_xnt"] = {"ap": nc.dram_tensor("l3_xnt", [D, T], BF16, kind="Internal").ap(),
                          "r": R("l3_xnt")}


    wspecs = []
    if 0 in layers:
        wspecs += [("w0_qT", [D, D]), ("w0_kT", [D, D]), ("w0_vT", [D, D]), ("w0_oT", [D, D]),
                   ("w0_f1", [D, D]), ("w0_f2", [D, D]), ("w0_f3", [D, D]), ("w0_bias", [D])]
    if 1 in layers:
        wspecs += [("w1_kT", [D, D]), ("w1_vT", [D, D]), ("w1_rT", [D, D]), ("w1_oT", [D, D]),
                   ("w1_tmk", [D]), ("w1_tmv", [D]), ("w1_tmr", [D]),
                   ("w1_decay", [D]), ("w1_tfirst", [D]), ("w1_lng", [D]), ("w1_lnb", [D]),
                   ("w1_gate", [2816, D]), ("w1_up", [2816, D]), ("w1_down", [D, 2816])]
    if 2 in layers:
        wspecs += [("w2_qkvT", [D, 3 * D]), ("w2_outT", [D, D]), ("w2_fcT", [D, 4096]),
                   ("w2_projT", [4096, D]), ("w2_lnw", [4096]), ("w2_lnb", [4096])]
    if 3 in layers:
        wspecs += [("w3_Wq", [D, D]), ("w3_Wk", [D, D]), ("w3_Wv", [D, D]), ("w3_Wp", [D, D]),
                   ("w3_bq", [D]), ("w3_bk", [D]), ("w3_bv", [D]), ("w3_bp", [D]),
                   ("w3_centT", [D, 8]), ("w3_W1", [8, D, 4096]), ("w3_b1", [8, 4096]),
                   ("w3_W2", [8, 4096, D]), ("w3_b2", [8, D])]
    for key, shape in wspecs:
        dram[key] = nc.dram_tensor(key, shape, F32, kind="ExternalInput").ap()
    dram["c_ident"] = nc.dram_tensor("c_ident", [128, 128], F32, kind="ExternalInput").ap()
    dram["c_mask_old"] = nc.dram_tensor("c_mask_old", [128, 128], F32, kind="ExternalInput").ap()
    dram["c_mask_diag"] = nc.dram_tensor("c_mask_diag", [128, 128], F32, kind="ExternalInput").ap()
    dram["c_bin_ut"] = nc.dram_tensor("c_bin_ut", [128, 128], F32, kind="ExternalInput").ap()

    es_global = ExitStack()
    with es_global:
        sems = {}
        sem_names = ["e_" + e for e in ENGS] + ["dq_" + e for e in ("sync", "scalar", "gpsimd")]
        sem_names += ["dq_" + e + str(k) for e in ("sync", "scalar", "gpsimd") for k in (0, 1)]
        for s in sem_names:
            sems[s] = es_global.enter_context(nc.semaphore(s))
        with nc.Block() as block:
            P = Prog(nc, block)
            P.sems = sems

            cpool, cpool_r = P.sb(es_global, "g_cpool", [128, 4], F32, strict=True)
            for col, val in [(0, 1e-5), (1, 1e-6), (2, 1.0), (3, 0.0)]:
                P.ins("vector", lambda e, col=col, val=val: e.memset(cpool[:, col:col + 1], val),
                      writes=[cpool_r])
            P.cpool_eps5 = (cpool[:, 0:1], cpool_r)
            P.cpool_eps6 = (cpool[:, 1:2], cpool_r)
            P.cpool_one = (cpool[:, 2:3], cpool_r)
            P.cpool_zero = (cpool[:, 3:4], cpool_r)

            ident_sb, ident_r = P.sb(es_global, "g_ident", [128, 128], BF16)
            ident32_sb, ident32_r = P.sb(es_global, "g_ident32", [128, 128], F32)
            ones_sb, ones_r = P.sb(es_global, "g_ones", [128, 512], BF16)
            mold_sb, mold_r = P.sb(es_global, "g_mold", [128, 128], F32)
            mdiag_sb, mdiag_r = P.sb(es_global, "g_mdiag", [128, 128], F32)
            binut_sb, binut_r = P.sb(es_global, "g_binut", [128, 128], BF16)
            P.dma("gpsimd", ident_sb[:], dram["c_ident"][:, :], writes=[ident_r])
            P.dma("gpsimd", binut_sb[:], dram["c_bin_ut"][:, :], writes=[binut_r])
            P.dma("sync", ident32_sb[:], dram["c_ident"][:, :], writes=[ident32_r])
            P.ins("vector", lambda e: e.memset(ones_sb[:], 1.0), writes=[ones_r])
            P.dma("sync", mold_sb[:], dram["c_mask_old"][:, :], writes=[mold_r])
            P.dma("sync", mdiag_sb[:], dram["c_mask_diag"][:, :], writes=[mdiag_r])
            dram["ident_sb"] = (ident_sb[:], ident_r)
            dram["ident32_sb"] = (ident32_sb[:], ident32_r)
            dram["ones_sb"] = (ones_sb[:], ones_r)
            dram["mask_old_sb"] = (mold_sb[:], mold_r)
            dram["mask_diag_sb"] = (mdiag_sb[:], mdiag_r)
            dram["binut_sb"] = (binut_sb[:], binut_r)

            bufs = ["x_in"] + (["x_mid_a", "x_mid_b", "x_mid_a"][:len(layers) - 1]) + ["x_out"]
            for li, layer in enumerate(layers):
                dram["l%d_in" % layer] = dram[bufs[li]]
                dram["l%d_out" % layer] = dram[bufs[li + 1]]
                with ExitStack() as es_layer:
                    if layer == 0:
                        emit_layer0(P, es_layer, dram, NT)
                    elif layer == 1:
                        emit_layer1(P, es_layer, dram, NT)
                    elif layer == 2:
                        emit_layer2(P, es_layer, dram, NT)
                    elif layer == 3:
                        emit_layer3(P, es_layer, dram, NT)
                    else:
                        raise NotImplementedError(layer)
                    P.barrier()
                    P.flush()
            P.barrier()
            P.flush()
    return nc


def make_weight_arrays(inputs):
    c = np.ascontiguousarray
    w = {}
    w["w0_qT"] = c(inputs["L0_attn_Wq"].T)
    w["w0_kT"] = c(inputs["L0_attn_Wk"].T)
    w["w0_vT"] = c(inputs["L0_attn_Wv"].T)
    w["w0_oT"] = c(inputs["L0_attn_Wo"].T)
    w["w0_f1"] = c(inputs["L0_ffn_weight"])
    w["w0_f2"] = c(inputs["L0_ffn_aux_weight1"])
    w["w0_f3"] = c(inputs["L0_ffn_aux_weight2"])
    w["w0_bias"] = c(inputs["L0_ffn_bias"])
    w["w1_kT"] = c(inputs["L1_attn_Wk"].T)
    w["w1_vT"] = c(inputs["L1_attn_Wv"].T)
    w["w1_rT"] = c(inputs["L1_attn_Wr"].T)
    w["w1_oT"] = c(inputs["L1_attn_Wo"].T)
    w["w1_tmk"] = c(inputs["L1_attn_time_mix_k"].reshape(-1))
    w["w1_tmv"] = c(inputs["L1_attn_time_mix_v"].reshape(-1))
    w["w1_tmr"] = c(inputs["L1_attn_time_mix_r"].reshape(-1))
    w["w1_decay"] = c(inputs["L1_attn_time_decay"].reshape(-1))
    w["w1_tfirst"] = c(inputs["L1_attn_time_first"].reshape(-1))
    w["w1_lng"] = c(inputs["L1_attn_ln_g"])
    w["w1_lnb"] = c(inputs["L1_attn_ln_b"])
    w["w1_gate"] = c(inputs["L1_ffn_w_gate"])
    w["w1_up"] = c(inputs["L1_ffn_w_up"])
    w["w1_down"] = c(inputs["L1_ffn_w_down"])
    w["w2_qkvT"] = c(inputs["L2_attn_w_qkv"].T)
    w["w2_outT"] = c(inputs["L2_attn_w_out"].T)
    w["w2_fcT"] = c(inputs["L2_ffn_W_fc"].T)
    w["w2_projT"] = c(inputs["L2_ffn_W_proj"].T)
    w["w2_lnw"] = c(inputs["L2_ffn_ln_w"])
    w["w2_lnb"] = c(inputs["L2_ffn_ln_b"])
    w["w3_Wq"] = c(inputs["L3_attn_Wq"])
    w["w3_Wk"] = c(inputs["L3_attn_Wk"])
    w["w3_Wv"] = c(inputs["L3_attn_Wv"])
    w["w3_Wp"] = c(inputs["L3_attn_Wp"])
    w["w3_bq"] = c(inputs["L3_attn_bq"])
    w["w3_bk"] = c(inputs["L3_attn_bk"])
    w["w3_bv"] = c(inputs["L3_attn_bv"])
    w["w3_bp"] = c(inputs["L3_attn_bp"])
    w["w3_centT"] = c(inputs["L3_ffn_centroids"].T)
    w["w3_W1"] = c(inputs["L3_ffn_W1"])
    w["w3_b1"] = c(inputs["L3_ffn_b1"])
    w["w3_W2"] = c(inputs["L3_ffn_W2"])
    w["w3_b2"] = c(inputs["L3_ffn_b2"])
    w["c_ident"] = np.eye(128, dtype=np.float32)
    r = np.arange(128)
    w["c_mask_old"] = np.where(r[None, :] >= r[:, None], 0.0, NEG).astype(np.float32)
    w["c_mask_diag"] = np.where(r[None, :] <= r[:, None], 0.0, NEG).astype(np.float32)
    w["c_bin_ut"] = (r[None, :] >= r[:, None]).astype(np.float32)
    return w


def kernel(**inputs):
    x = np.ascontiguousarray(inputs["L0_attn_x"], dtype=np.float32)
    w = make_weight_arrays(inputs)
    nc = build_program(T=4096, layers=(0, 1, 2, 3))
    in_maps = []
    for i in range(8):
        m = dict(w)
        m["x_in"] = np.ascontiguousarray(x[i])
        in_maps.append(m)
    res = run_bass_kernel_spmd(nc, in_maps, core_ids=list(range(8)))
    out = np.stack([np.asarray(r["x_out"]) for r in res.results], axis=0)
    return out.astype(np.float32)
